# version 03
# baseline (speedup 1.0000x reference)
import numpy as np
import ml_dtypes
import concourse.bass as bass
import concourse.mybir as mybir
from concourse.bass_utils import run_bass_kernel_spmd
from contextlib import ExitStack

F32 = mybir.dt.float32
BF16 = mybir.dt.bfloat16
F8 = mybir.dt.float8e4
I8 = mybir.dt.int8
AF = mybir.ActivationFunctionType
ALU = mybir.AluOpType
AX = mybir.AxisListType

B, N, DM = 4, 2048, 2048
H, HD, HKV = 32, 64, 8
SEG = 512
HS = 256
NHS = N // HS
NC = DM // 128
NI = 6144
NIC = NI // 128
OWN = {0: (0, 3), 1: (1, 2)}

QA = [8 * (j // 4) + j % 4 for j in range(16)]
QB = [8 * (j // 4) + 4 + j % 4 for j in range(16)]
QCOL_PERM = np.concatenate(
    [np.r_[QA[j] * 64:QA[j] * 64 + 64, QB[j] * 64:QB[j] * 64 + 64]
     for j in range(16)])

ENGS = ("sync", "tensor", "vector", "scalar", "gpsimd")


class Sem:
    __slots__ = ("name", "count", "handle")

    def __init__(self, name):
        self.name = name
        self.count = 0
        self.handle = None


class Prog:
    def __init__(self):
        self.ops = {e: [] for e in ENGS}
        self.sems = {}

    def sem(self, name):
        if name not in self.sems:
            self.sems[name] = Sem(name)
        return self.sems[name]

    def emit(self, eng, fn, waits=(), incs=()):
        incs = tuple(incs)
        for s, n in incs:
            s.count += n
        self.ops[eng].append((fn, tuple(waits), incs))
        return incs[0][0].count if incs else None

    def barrier_counts(self):
        return [(s, s.count) for s in self.sems.values() if s.count > 0]

    def build_into(self, nc, ctx):
        for s in self.sems.values():
            s.handle = ctx.enter_context(nc.semaphore(s.name))
        block = ctx.enter_context(nc.Block())
        for eng in ENGS:
            ops = self.ops[eng]

            def emit_all(engine, ops=ops):
                seen = {}
                for fn, waits, incs in ops:
                    for s, v in waits:
                        if seen.get(s.name, -1) < v:
                            engine.wait_ge(s.handle, v)
                            seen[s.name] = v
                    inst = fn(engine)
                    for s, n in incs:
                        if inst is None:
                            inst = engine.nop()
                        inst.then_inc(s.handle, n)
                        inst = None
            getattr(block, eng)(emit_all)


C_IDENT = slice(0, 128)
C_ONES = slice(128, 256)
C_STRICTL2 = slice(256, 320)
C_STRICTT2 = slice(320, 384)
C_CMASK2 = slice(384, 448)
C_CM256A = slice(448, 704)
C_CM256B = slice(704, 960)
C_HONES = slice(960, 962)
C_E0 = slice(962, 1026)
CONST_W = 1088


def make_consts():
    c = np.zeros((128, CONST_W), dtype=np.float32)
    p = np.arange(128)[:, None]
    f64 = np.arange(64)[None, :]
    f256 = np.arange(256)[None, :]
    c[:, C_IDENT] = np.eye(128)
    c[:, C_ONES] = 1.0
    c[:, C_STRICTL2] = (f64 < (p % 64)).astype(np.float32)
    c[:, C_STRICTT2] = ((p % 64) < f64).astype(np.float32)
    c[:, C_CMASK2] = (f64 >= (p % 64)).astype(np.float32)
    c[:, C_CM256A] = (f256 >= p).astype(np.float32)
    c[:, C_CM256B] = (f256 >= p + 128).astype(np.float32)
    c[0:64, 960:961] = 1.0
    c[64:128, 961:962] = 1.0
    c[0:1, 962:1026] = 1.0
    return c


class CoreBuilder:
    def __init__(self, half):
        self.half = half
        self.own = OWN[half]
        self.nc = bass.Bass("TRN2", target_bir_lowering=False,
                            detect_race_conditions=False)
        self.P = Prog()
        self.ctx = ExitStack()
        self.taps = []

    def sb(self, name, shape, dt):
        return self.ctx.enter_context(self.nc.sbuf_tensor(name, list(shape), dt))

    def ps(self, name, shape, dt=F32):
        return self.ctx.enter_context(self.nc.psum_tensor(name, list(shape), dt))

    def tap(self, name, ap, shape, dt=F32):
        t = self.nc.dram_tensor(name, list(shape), dt, kind="ExternalOutput")
        self.taps.append((name, t, ap))

    def split(self, eng, src, hi, lo, waits=(), incs=()):
        self.P.emit(eng, lambda e, s=src, h=hi: e.tensor_copy(out=h, in_=s),
                    waits=waits)
        return self.P.emit(eng, lambda e, s=src, h=hi, l=lo: e.tensor_tensor(
            out=l, in0=s, in1=h, op=ALU.subtract), incs=incs)

    def mm3(self, psum, lh, ll, rh, rl, start, stop, tile_position=None,
            waits=(), incs=(), passes=3):
        P = self.P
        tp = tile_position
        P.emit("tensor", lambda e: e.matmul(psum, lh, rh, start=start, stop=False,
                                            tile_position=tp), waits=waits)
        if passes == 2:
            return P.emit("tensor", lambda e: e.matmul(
                psum, lh, rl, start=False, stop=stop, tile_position=tp), incs=incs)
        P.emit("tensor", lambda e: e.matmul(psum, lh, rl, start=False, stop=False,
                                            tile_position=tp))
        return P.emit("tensor", lambda e: e.matmul(
            psum, ll, rh, start=False, stop=stop, tile_position=tp), incs=incs)

    def reset_pe(self, waits=()):
        return self.P.emit("tensor", lambda e: e.ldweights(self.identb[:]),
                           waits=waits)

    def alloc(self):
        nc = self.nc
        din = lambda n, s, d=F32: nc.dram_tensor(n, list(s), d, kind="ExternalInput")
        self.x_in = din("x", [N, DM])
        self.wq_in = din("wq", [DM, DM])
        self.wk_in = din("wk", [DM, 512])
        self.wv_in = din("wv", [DM, 512])
        self.wo_in = din("wo", [DM, DM])
        self.gw_in = din("gw", [DM, NI])
        self.uw_in = din("uw", [DM, NI])
        self.dw_in = din("dw", [NI, DM])
        self.gs_in = din("gs", [NI])
        self.us_in = din("us", [NI])
        self.ds_in = din("ds", [DM])
        self.const_in = din("consts", [128, CONST_W])
        self.xg_in = din("xg", [2 * SEG, DM])
        self.masks_in = nc.dram_tensor("masks", [128, 44 * 256], BF16,
                                       kind="ExternalInput")
        self.gofs_in = nc.dram_tensor("gofs", [1, 64], mybir.dt.int32,
                                      kind="ExternalInput")
        self.out = nc.dram_tensor("out", [2 * SEG, DM], F32, kind="ExternalOutput")
        self.attn_scr = nc.dram_tensor("attn_scr", [128, 4 * NC * HS], F32)
        self.ssnap_scr = nc.dram_tensor("ssnap_scr", [128, 32 * 1024], BF16)
        self.ups_scr = nc.dram_tensor("ups_scr", [128, 32 * 1024], BF16)
        self.klf_scr = nc.dram_tensor("klf_scr", [128, 8 * 4096], BF16)

        self.cst = self.sb("cst", [128, CONST_W], F32)
        self.onesb = self.sb("onesb", [128, 2], BF16)
        self.identb = self.sb("identb", [128, 128], BF16)
        self.fconst = self.sb("fconst", [128, 4], F32)
        self.lnstat = self.sb("lnstat", [128, 4], F32)
        self.scl = self.sb("scl", [128, 112], F32)
        self.bp1 = self.sb("bp1", [128, 49152], BF16)
        t = self.bp1
        self.k_hi = t[:, 0:8192]; self.k_lo = t[:, 8192:16384]
        self.v_hi = t[:, 16384:24576]; self.v_lo = t[:, 24576:32768]
        self.xs_hi = t[:, 32768:36864]; self.xs_lo = t[:, 36864:40960]
        self.klfz = [(t[:, 32768:33792], t[:, 33792:34816]),
                     (t[:, 34816:35840], t[:, 35840:36864])]
        self.wTs_hi = t[:, 36864:37376]; self.wTs_lo = t[:, 37376:37888]
        self.ups_hi = t[:, 37888:38400]; self.ups_lo = t[:, 38400:38912]
        self.Sd_hi = t[:, 38912:39424]; self.Sd_lo = t[:, 39424:39936]
        self.slvP = t[:, 39936:40064]; self.slvPT = t[:, 40064:40192]
        self.slvP2 = t[:, 40192:40320]; self.slvPT2 = t[:, 40320:40448]
        self.ybf = t[:, 40448:40576]
        self.Pt_hi = t[:, 32768:33280]; self.Pt_lo = t[:, 33280:33792]
        self.q_hi = t[:, 40960:45056]; self.q_lo = t[:, 45056:49152]
        self.eq_hi = t[:, 45056 - 4096:45056 - 2048] if False else None
        self.fuse_hi = t[:, 0:24576]; self.fuse_lo = t[:, 24576:49152]
        self.bp2 = self.sb("bp2", [128, 22912], BF16)
        u = self.bp2
        self.kbf_hi = u[:, 0:1024]; self.kbf_lo = u[:, 1024:2048]
        self.kltm_hi = u[:, 2048:3072]; self.kltm_lo = u[:, 3072:4096]
        self.kbtm_hi = u[:, 4096:5120]; self.kbtm_lo = u[:, 5120:6144]
        self.vbtm_hi = u[:, 6144:7168]; self.vbtm_lo = u[:, 7168:8192]
        self.oc_hi = u[:, 0:4096]; self.oc_lo = u[:, 4096:8192]
        self.Ssn_hi = u[:, 8192:8704]; self.Ssn_lo = u[:, 8704:9216]
        self.wsp_hi = u[:, 9216:11264]; self.wsp_lo_x = u[:, 11264:13312]
        self.mkb = 13312
        self.eqb = 0
        self.eq_hi = u[:, 0:4096]; self.eq_lo = u[:, 4096:8192]
        self.xnr_hi = u[:, 0:8192]; self.xnr_lo = u[:, 8192:16384]
        self.w8 = self.sb("w8", [128, 4096], I8)
        self.fp = self.sb("fp", [128, 13568], F32)
        f = self.fp
        self.xt = [f[:, 0:2048], f[:, 2048:4096]]
        self.o_comb = f[:, 0:4096]
        self.sq_scr = f[:, 4096:6144]
        self.k_raw = f[:, 4096:5120]; self.v_raw = f[:, 5120:6144]
        self.ektmp = f[:, 6144:7168]
        self.gltm = f[:, 6144:7168]
        self.klf32 = f[:, 7168:8192]
        self.gl32 = f[:, 8192:9216]
        self.scr1 = f[:, 9216:10240]
        self.uw_f = f[:, 10240:11264]
        self.pexp = f[:, 11264:11776]
        self.wstrip = [f[:, 6144:7168], f[:, 7168:8192]]
        self.eqt = f[:, 4096:6144]
        self.stage = f[:, 11264:11776]
        self.misc = f[:, 11776:13568]
        self.rcpA = self.misc[0:1, 0:256]
        self.rcpB = self.misc[0:1, 256:512]
        self.bcast_sb = self.misc[:, 512:768]
        self.t1sb = self.misc[:, 768:1024]
        self.att_st = self.misc[:, 1024:1536]
        self.S32 = self.misc[:, 1536:1792]
        self.gofs_sb = self.sb("gofs_sb", [1, 64], mybir.dt.int32)
        self.Ssn_hi = self.bp2[:, 21504:21760]
        self.Ssn_lo = self.bp2[:, 21760:22016]
        self.pb = [self.ps(f"pb{i}", [128, 512]) for i in range(8)]

    def st_const(self):
        P = self.P
        ld = P.sem("ld")
        ve = P.sem("ve")
        P.emit("sync", lambda e: e.dma_start(out=self.cst[:], in_=self.const_in.ap()),
               incs=[(ld, 16)])
        P.emit("sync", lambda e: e.dma_start(
            out=self.scl[:, 0:48],
            allow_slow_non_contiguous=True,
            in_=bass.AP(self.gs_in, 0, [[1, 128], [128, 48]])), incs=[(ld, 16)])
        P.emit("sync", lambda e: e.dma_start(
            out=self.scl[:, 48:96],
            allow_slow_non_contiguous=True,
            in_=bass.AP(self.us_in, 0, [[1, 128], [128, 48]])), incs=[(ld, 16)])
        P.emit("sync", lambda e: e.dma_start(
            out=self.scl[:, 96:112],
            allow_slow_non_contiguous=True,
            in_=bass.AP(self.ds_in, 0, [[1, 128], [128, 16]])), incs=[(ld, 16)])
        tok = P.emit("sync", lambda e: e.dma_start(
            out=self.gofs_sb[:], in_=self.gofs_in.ap()), incs=[(ld, 16)])
        P.emit("vector", lambda e: e.memset(self.onesb[:], 1.0), waits=[(ld, tok)])
        P.emit("vector", lambda e: e.memset(self.fconst[:, 0:1], 1e-5))
        P.emit("vector", lambda e: e.tensor_copy(out=self.identb[:],
                                                 in_=self.cst[:, C_IDENT]))
        P.emit("vector", lambda e: e.memset(self.misc[:], 0.0))
        P.emit("vector", lambda e: e.memset(self.Sd_hi[:], 0.0))
        P.emit("vector", lambda e: e.memset(self.Sd_lo[:], 0.0))
        P.emit("vector", lambda e: e.memset(self.slvP[:], 0.0))
        P.emit("vector", lambda e: e.memset(self.slvPT[:], 0.0))
        P.emit("vector", lambda e: e.memset(self.slvP2[:], 0.0))
        P.emit("vector", lambda e: e.memset(self.slvPT2[:], 0.0))
        P.emit("vector", lambda e: e.memset(self.ups_hi[:], 0.0))
        P.emit("vector", lambda e: e.memset(self.ups_lo[:], 0.0))
        P.emit("vector", lambda e: e.memset(self.fconst[0:64, 1:2], 1.0))
        P.emit("vector", lambda e: e.memset(self.fconst[64:128, 1:2], 0.0))
        P.emit("vector", lambda e: e.memset(self.fconst[0:64, 2:3], 0.0))
        vtok = P.emit("vector", lambda e: e.memset(self.fconst[64:128, 2:3], 1.0),
                      incs=[(ve, 1)])
        for e in ENGS:
            P.emit(e, lambda en: None, waits=[(ld, tok), (ve, vtok)])

    def ln_tile(self, src_dram, row0, t, dst_hi=None, dst_lo=None, cstride=HS):
        P = self.P
        ld, ve, pe, sc = P.sem("ld"), P.sem("ve"), P.sem("pe"), P.sem("sc")
        hzv, hzs = P.sem("hzv"), P.sem("hzs")
        if dst_hi is None:
            dst_hi, dst_lo = self.xs_hi, self.xs_lo
        buf = self.xt[t % 2]
        tok_ld = P.emit("sync", lambda e, b=buf, o=row0, s=src_dram: e.dma_start(
            out=b, in_=s.ap()[o:o + 128, :]),
            waits=[(pe, pe.count), (P.sem("st"), P.sem("st").count)],
            incs=[(ld, 16)])
        st = self.lnstat
        h1 = P.emit("vector", lambda e, b=buf: e.reduce_sum(
            out=st[:, 0:1], in_=b, axis=AX.X, negate=True),
            waits=[(ld, tok_ld)], incs=[(hzv, 1)])
        h2 = P.emit("vector", lambda e: e.tensor_scalar_mul(
            out=st[:, 1:2], in0=st[:, 0:1], scalar1=1.0 / DM),
            waits=[(hzv, h1)], incs=[(hzv, 1)])
        vtok = P.emit("vector", lambda e, b=buf: e.tensor_scalar(
            out=b, in0=b, scalar1=st[:, 1:2], scalar2=None, op0=ALU.add),
            waits=[(hzv, h2)], incs=[(ve, 1)])
        h3 = P.emit("scalar", lambda e, b=buf: e.activation(
            out=self.sq_scr, in_=b, func=AF.Square, accum_out=st[:, 2:3]),
            waits=[(ve, vtok)], incs=[(hzs, 1)])
        h4 = P.emit("scalar", lambda e: e.activation(
            out=st[:, 3:4], in_=st[:, 2:3], func=AF.Ln, scale=1.0 / DM,
            bias=self.fconst[:, 0:1]), waits=[(hzs, h3)], incs=[(hzs, 1)])
        stok = P.emit("scalar", lambda e: e.activation(
            out=st[:, 3:4], in_=st[:, 3:4], func=AF.Exp, scale=-0.5),
            waits=[(hzs, h4)], incs=[(sc, 1)])
        vtok = P.emit("vector", lambda e, b=buf: e.tensor_scalar_mul(
            out=b, in0=b, scalar1=st[:, 3:4]), waits=[(sc, stok)],
            incs=[(ve, 1)])
        for c in range(NC):
            bank = self.pb[c % 2]
            ptok = P.emit("tensor", lambda e, b=buf, c=c, bank=bank: e.transpose(
                bank[:, 0:128], b[:, c * 128:(c + 1) * 128], self.cst[:, C_IDENT]),
                waits=[(ve, ve.count)], incs=[(pe, 1)])
            lo_c = c * cstride + t * 128
            self.split("vector", bank[:, 0:128],
                       dst_hi[:, lo_c:lo_c + 128], dst_lo[:, lo_c:lo_c + 128],
                       waits=[(pe, ptok)], incs=[(ve, 1)])

    def st_xn(self, hs):
        for t in range(2):
            self.ln_tile(self.x_in, hs * HS + t * 128, t)

    def st_proj(self, hs):
        P = self.P
        ld, ve, pe, sc = P.sem("ld"), P.sem("ve"), P.sem("pe"), P.sem("sc")
        kv_tok = {}

        def src(dc):
            return bass.AP(self.wk_in, dc * 128 * 512, [[512, 128], [1, 512]]), \
                   bass.AP(self.wv_in, dc * 128 * 512, [[512, 128], [1, 512]])

        last_mm = [pe.count, pe.count]
        last_split = [0, 0]
        for dc in range(NC):
            bi = dc % 2
            sbuf = self.wstrip[bi]
            sph = self.wsp_hi[:, bi * 1024:(bi + 1) * 1024]
            spl = self.wsp_lo_x[:, bi * 1024:(bi + 1) * 1024]
            ksrc, vsrc = src(dc)
            P.emit("sync", lambda e, b=sbuf, s=ksrc: e.dma_start(
                out=b[:, 0:512], in_=s),
                waits=[(ve, last_split[bi]), (pe, last_mm[bi])], incs=[(ld, 16)])
            tok_ld = P.emit("sync", lambda e, b=sbuf, s=vsrc: e.dma_start(
                out=b[:, 512:1024], in_=s), incs=[(ld, 16)])
            vtok = self.split("vector", sbuf, sph, spl,
                              waits=[(ld, tok_ld), (pe, last_mm[bi])], incs=[(ve, 1)])
            last_split[bi] = vtok
            for pc in range(4):
                lh = sph[:, pc * 128:(pc + 1) * 128]
                ll = spl[:, pc * 128:(pc + 1) * 128]
                rh = self.xs_hi[:, dc * HS:(dc + 1) * HS]
                rl = self.xs_lo[:, dc * HS:(dc + 1) * HS]
                tok = self.mm3(self.pb[pc][:, 0:256], lh, ll, rh, rl,
                               start=(dc == 0), stop=(dc == NC - 1),
                               waits=[(ve, vtok)],
                               incs=[(pe, 1)])
                if dc == NC - 1:
                    kv_tok[("k", pc)] = tok
            for tt in range(2):
                lh = self.xs_hi[:, dc * HS + tt * 128: dc * HS + (tt + 1) * 128]
                ll = self.xs_lo[:, dc * HS + tt * 128: dc * HS + (tt + 1) * 128]
                rh = sph[:, 512:1024]
                rl = spl[:, 512:1024]
                tok = self.mm3(self.pb[4 + tt][:], lh, ll, rh, rl,
                               start=(dc == 0), stop=(dc == NC - 1),
                               incs=[(pe, 1)])
                if dc == NC - 1:
                    kv_tok[("v", tt)] = tok
            last_mm[bi] = tok
        for pc in range(4):
            bank = self.pb[pc][:, 0:256]
            w = [(pe, kv_tok[("k", pc)])]
            stok = P.emit("scalar", lambda e, b=bank, pc=pc: e.activation(
                out=self.k_raw[:, pc * 256:(pc + 1) * 256], in_=b, func=AF.Identity),
                waits=w, incs=[(sc, 1)])
            co = pc * 2048 + hs * HS
            self.split("vector", bank, self.k_hi[:, co:co + 256],
                       self.k_lo[:, co:co + 256], waits=w + [(sc, stok)],
                       incs=[(ve, 1)])
        for tt in range(2):
            bank = self.pb[4 + tt][:]
            w = [(pe, kv_tok[("v", tt)])]
            stok = P.emit("scalar", lambda e, b=bank, tt=tt: e.activation(
                out=self.v_raw[:, tt * 512:(tt + 1) * 512], in_=b, func=AF.Identity),
                waits=w, incs=[(sc, 1)])
            co = (hs * 2 + tt) * 512
            self.split("vector", bank, self.v_hi[:, co:co + 512],
                       self.v_lo[:, co:co + 512], waits=w + [(sc, stok)],
                       incs=[(ve, 1)])

    def bcast_recip(self, src, waits=(), w=256):
        P = self.P
        pe, hzv, ve = P.sem("pe"), P.sem("hzv"), P.sem("ve")
        bsA, bsB, bbc = self.pb[5], self.pb[6], self.pb[7]
        t1 = P.emit("tensor", lambda e, s=src, w=w: e.matmul(
            bsA[0:1, 0:w], self.cst[:, 960:961], s, start=True, stop=True),
            waits=waits, incs=[(pe, 1)])
        t2 = P.emit("tensor", lambda e, s=src, w=w: e.matmul(
            bsB[0:1, 0:w], self.cst[:, 961:962], s, start=True, stop=True),
            incs=[(pe, 1)])
        v1 = P.emit("vector", lambda e, w=w: e.reciprocal(
            out=self.misc[0:1, 0:w], in_=bsA[0:1, 0:w]),
            waits=[(pe, t2)], incs=[(hzv, 1)])
        v2 = P.emit("vector", lambda e, w=w: e.reciprocal(
            out=self.misc[0:1, w:2 * w], in_=bsB[0:1, 0:w]),
            waits=[(hzv, v1)], incs=[(hzv, 1)])
        P.emit("tensor", lambda e, w=w: e.matmul(
            bbc[0:64, 0:w], self.cst[:, C_E0], self.misc[:, 0:w],
            start=True, stop=True), waits=[(hzv, v2), (ve, ve.count)])
        btok = P.emit("tensor", lambda e, w=w: e.matmul(
            bbc[64:128, 0:w], self.cst[:, C_E0], self.misc[:, w:2 * w],
            start=True, stop=True), incs=[(pe, 1)])
        return bbc[:, 0:w], btok

    def st_prep(self, hs):
        P = self.P
        ld, ve, pe, sc = P.sem("ld"), P.sem("ve"), P.sem("pe"), P.sem("sc")
        hzs = P.sem("hzs")
        etok = P.emit("scalar", lambda e: e.activation(
            out=self.ektmp, in_=self.k_raw, func=AF.Exp),
            waits=[(sc, sc.count), (ve, ve.count)], incs=[(sc, 1)])
        for pc in (0, 2):
            bb, btok = self.bcast_recip(self.ektmp[:, pc * 256:(pc + 2) * 256],
                                        waits=[(sc, etok), (ve, ve.count)], w=512)
            P.emit("vector", lambda e, pc=pc, bb=bb: e.tensor_tensor(
                out=self.klf32[:, pc * 256:(pc + 2) * 256],
                in0=self.ektmp[:, pc * 256:(pc + 2) * 256], in1=bb, op=ALU.mult),
                waits=[(pe, btok)])
            for z in range(2):
                hi, lo = self.klfz[z]
                msk = self.fconst[:, 1 + z:2 + z]
                sl = slice(pc * 256, (pc + 2) * 256)
                P.emit("vector", lambda e, sl=sl, hi=hi, msk=msk: e.tensor_scalar_mul(
                    out=hi[:, sl], in0=self.klf32[:, sl], scalar1=msk))
                P.emit("vector", lambda e, sl=sl, hi=hi, lo=lo, msk=msk:
                       e.scalar_tensor_tensor(
                    out=lo[:, sl], in0=self.klf32[:, sl], scalar=msk, in1=hi[:, sl],
                    op0=ALU.mult, op1=ALU.subtract), incs=[(ve, 1)])
        g1 = P.emit("scalar", lambda e: e.activation(
            out=self.scr1, in_=self.k_raw, func=AF.Sigmoid), incs=[(hzs, 1)])
        gtok = P.emit("scalar", lambda e: e.activation(
            out=self.gl32, in_=self.scr1, func=AF.Ln), waits=[(hzs, g1)],
            incs=[(sc, 1)])
        P.emit("vector", lambda e: e.scalar_tensor_tensor(
            out=self.scr1, in0=self.klf32, scalar=1.0 / 16.0, in1=self.gl32,
            op0=ALU.mult, op1=ALU.mult), waits=[(sc, gtok), (ve, ve.count)])
        kbtok = self.split("vector", self.scr1, self.kbf_hi, self.kbf_lo,
                           incs=[(ve, 1)])
        for pc in range(4):
            for tt in range(2):
                bank = self.pb[(pc * 2 + tt) % 2]
                srcT = self.klf32[:, pc * 256 + tt * 128: pc * 256 + (tt + 1) * 128]
                ptok = P.emit("tensor", lambda e, s=srcT, bank=bank: e.transpose(
                    bank[:, 0:128], s, self.cst[:, C_IDENT]),
                    waits=[(ve, ve.count), (sc, sc.count)], incs=[(pe, 1)])
                dst = 512 * tt + pc * 128
                stok = P.emit("scalar", lambda e, bank=bank, dst=dst: e.activation(
                    out=self.scr1[:, dst:dst + 128], in_=bank[:, 0:128],
                    func=AF.Identity), waits=[(pe, ptok), (ve, kbtok)], incs=[(sc, 1)])
                self.split("vector", bank[:, 0:128],
                           self.kltm_hi[:, dst:dst + 128],
                           self.kltm_lo[:, dst:dst + 128],
                           waits=[(pe, ptok), (sc, stok)], incs=[(ve, 1)])
        for pc in range(4):
            for tt in range(2):
                bank = self.pb[(pc * 2 + tt) % 2 + 2]
                srcT = self.gl32[:, pc * 256 + tt * 128: pc * 256 + (tt + 1) * 128]
                ptok = P.emit("tensor", lambda e, s=srcT, bank=bank: e.transpose(
                    bank[:, 0:128], s, self.cst[:, C_IDENT]),
                    waits=[(ve, ve.count), (sc, sc.count)], incs=[(pe, 1)])
                dst = 512 * tt + pc * 128
                P.emit("scalar", lambda e, bank=bank, dst=dst: e.activation(
                    out=self.gltm[:, dst:dst + 128], in_=bank[:, 0:128],
                    func=AF.Identity), waits=[(pe, ptok)], incs=[(sc, 1)])
        P.emit("vector", lambda e: e.scalar_tensor_tensor(
            out=self.klf32, in0=self.scr1, scalar=1.0 / 16.0, in1=self.gltm,
            op0=ALU.mult, op1=ALU.mult),
            waits=[(sc, sc.count), (ve, ve.count), (pe, pe.count)])
        self.split("vector", self.klf32, self.kbtm_hi, self.kbtm_lo, incs=[(ve, 1)])
        P.emit("vector", lambda e: e.scalar_tensor_tensor(
            out=self.k_raw, in0=self.v_raw, scalar=1.0 / 16.0, in1=self.gltm,
            op0=ALU.mult, op1=ALU.mult))
        self.split("vector", self.k_raw, self.vbtm_hi, self.vbtm_lo, incs=[(ve, 1)])
        stt = P.sem("st")
        for z in range(2):
            hi, lo = self.klfz[z]
            P.emit("gpsimd", lambda e, hs=hs, z=z, hi=hi: e.dma_start(
                out=self.klf_scr.ap()[:, hs * 4096 + z * 2048: hs * 4096 + z * 2048 + 1024],
                in_=hi), waits=[(ve, ve.count)], incs=[(stt, 16)])
            P.emit("gpsimd", lambda e, hs=hs, z=z, lo=lo: e.dma_start(
                out=self.klf_scr.ap()[:, hs * 4096 + z * 2048 + 1024: hs * 4096 + (z + 1) * 2048],
                in_=lo), incs=[(stt, 16)])

    def st_delta(self, hs):
        for tt in range(2):
            self.delta_solve(hs, tt)
            for c in range(2):
                self.delta_scan(hs, tt, c)

    def delta_solve(self, hs, tt):
        P = self.P
        ve, pe = P.sem("ve"), P.sem("pe")
        bL, bY, bSq, bWT = self.pb[0], self.pb[1], self.pb[2], self.pb[3]
        P.emit("vector", lambda e: e.memset(self.slvP[:], 0.0),
               waits=[(pe, pe.count)])
        P.emit("vector", lambda e: e.memset(self.slvPT[:], 0.0))
        P.emit("vector", lambda e: e.memset(self.slvP2[:], 0.0))
        P.emit("vector", lambda e: e.memset(self.slvPT2[:], 0.0), incs=[(ve, 1)])
        for kv in range(8):
            z = kv % 2
            base = z * 64
            pc = kv // 2
            colc = [pc * 256 + tt * 128 + c * 64 for c in range(2)]
            klz_hi, klz_lo = self.klfz[z]
            for c in range(2):
                kb = self.kbf_hi[:, colc[c]:colc[c] + 64]
                kl = klz_hi[:, colc[c]:colc[c] + 64]
                P.emit("tensor", lambda e, kb=kb, kl=kl, c=c: e.matmul(
                    bL[c * 64:(c + 1) * 64, 0:64], kb, kl, start=True, stop=True),
                    waits=[(ve, ve.count)])
                P.emit("tensor", lambda e, kb=kb, kl=kl, c=c: e.matmul(
                    bL[c * 64:(c + 1) * 64, 64:128], kl, kb, start=True, stop=True),
                    incs=[(pe, 1)])
            ltok = pe.count
            P.emit("vector", lambda e: e.scalar_tensor_tensor(
                out=self.slvP[0:64, 0:64], in0=bL[0:64, 0:64], scalar=-1.0,
                in1=self.cst[0:64, C_STRICTL2], op0=ALU.mult, op1=ALU.mult),
                waits=[(pe, ltok)])
            P.emit("vector", lambda e: e.scalar_tensor_tensor(
                out=self.slvP[64:128, 64:128], in0=bL[64:128, 0:64], scalar=-1.0,
                in1=self.cst[64:128, C_STRICTL2], op0=ALU.mult, op1=ALU.mult))
            P.emit("vector", lambda e: e.scalar_tensor_tensor(
                out=self.slvPT[0:64, 0:64], in0=bL[0:64, 64:128], scalar=-1.0,
                in1=self.cst[0:64, C_STRICTT2], op0=ALU.mult, op1=ALU.mult))
            vtok = P.emit("vector", lambda e: e.scalar_tensor_tensor(
                out=self.slvPT[64:128, 64:128], in0=bL[64:128, 64:128], scalar=-1.0,
                in1=self.cst[64:128, C_STRICTT2], op0=ALU.mult, op1=ALU.mult),
                incs=[(ve, 1)])
            ucol = 64 if kv % 2 == 0 else 0
            wcol = 64 - ucol
            y32 = self.uw_f[:, kv * 128: kv * 128 + 128]
            vb_h = self.vbtm_hi[:, tt * 512 + kv * 64: tt * 512 + (kv + 1) * 64]
            vb_l = self.vbtm_lo[:, tt * 512 + kv * 64: tt * 512 + (kv + 1) * 64]
            kb_h = self.kbtm_hi[:, tt * 512 + kv * 64: tt * 512 + (kv + 1) * 64]
            kb_l = self.kbtm_lo[:, tt * 512 + kv * 64: tt * 512 + (kv + 1) * 64]
            P.emit("vector", lambda e, y=y32, u=ucol, h=vb_h, l=vb_l: e.tensor_add(
                out=y[:, u:u + 64], in0=h, in1=l), waits=[(ve, ve.count)])
            P.emit("vector", lambda e, y=y32, w=wcol, h=kb_h, l=kb_l: e.tensor_add(
                out=y[:, w:w + 64], in0=h, in1=l), incs=[(ve, 1)])
            cur = (self.slvP, self.slvPT)
            nxt = (self.slvP2, self.slvPT2)
            for k in range(6):
                ytok = P.emit("vector", lambda e, y=y32: e.tensor_copy(
                    out=self.ybf, in_=y), waits=[(pe, pe.count), (ve, ve.count)],
                    incs=[(ve, 1)])
                atok = P.emit("tensor", lambda e, cur=cur: e.matmul(
                    bY[:, 0:128], cur[1], self.ybf, start=True, stop=True),
                    waits=[(ve, ytok)], incs=[(pe, 1)])
                P.emit("vector", lambda e, y=y32: e.tensor_add(
                    out=y, in0=y, in1=bY[:, 0:128]), waits=[(pe, atok)],
                    incs=[(ve, 1)])
                if k < 5:
                    P.emit("tensor", lambda e, cur=cur: e.matmul(
                        bSq[:, 0:128], cur[1], cur[0], start=True, stop=True))
                    sqt = P.emit("tensor", lambda e, cur=cur: e.matmul(
                        bSq[:, 128:256], cur[0], cur[1], start=True, stop=True),
                        incs=[(pe, 1)])
                    P.emit("vector", lambda e, nxt=nxt: e.tensor_copy(
                        out=nxt[0], in_=bSq[:, 0:128]), waits=[(pe, sqt)])
                    P.emit("vector", lambda e, nxt=nxt: e.tensor_copy(
                        out=nxt[1], in_=bSq[:, 128:256]), incs=[(ve, 1)])
                    cur, nxt = nxt, cur
            uwtok = P.emit("vector", lambda e: None, incs=[(ve, 1)])
            srcT = self.uw_f[:, kv * 128: kv * 128 + 128]
            wtok = P.emit("tensor", lambda e, s=srcT: e.transpose(
                bWT[:, 0:128], s, self.cst[:, C_IDENT]),
                waits=[(ve, uwtok)], incs=[(pe, 1)])
            rows = slice(0, 64) if kv % 2 == 0 else slice(64, 128)
            wsl = slice((kv // 2) * 128, (kv // 2) * 128 + 128)
            P.emit("vector", lambda e, rows=rows, wsl=wsl: e.tensor_scalar_mul(
                out=self.wTs_hi[rows, wsl], in0=bWT[rows, 0:128], scalar1=-1.0),
                waits=[(pe, wtok)])
            P.emit("vector", lambda e, rows=rows, wsl=wsl: e.scalar_tensor_tensor(
                out=self.wTs_lo[rows, wsl], in0=bWT[rows, 0:128],
                scalar=-1.0, in1=self.wTs_hi[rows, wsl],
                op0=ALU.mult, op1=ALU.subtract), incs=[(ve, 1)])

    def delta_scan(self, hs, tt, c):
        P = self.P
        ve, pe, sc = P.sem("ve"), P.sem("pe"), P.sem("sc")
        stt = P.sem("st")
        bUP, bS = self.pb[4], self.pb[5]
        cp = slice(c * 64, (c + 1) * 64)
        op = slice(64 - c * 64, 128 - c * 64)
        gc = hs * 4 + tt * 2 + c
        def s32ap(rows):
            return bass.AP(self.fp, rows * 13568 + 11776 + 1536,
                           [[13568, 64], [64, 4], [1, 64]])

        def sdap(t, rows, coff):
            return bass.AP(self.bp1, rows * 49152 + t + coff,
                           [[49152, 64], [128, 4], [1, 64]])
        P.emit("vector", lambda e: e.memset(self.Sd_hi[:], 0.0),
               waits=[(pe, pe.count)])
        P.emit("vector", lambda e: e.memset(self.Sd_lo[:], 0.0))
        for rows, coff in ((0, 0), (64, 64)):
            P.emit("vector", lambda e, rows=rows, coff=coff: e.tensor_copy(
                out=sdap(38912, rows, coff), in_=s32ap(rows)))
            P.emit("vector", lambda e, rows=rows, coff=coff: e.tensor_tensor(
                out=sdap(39424, rows, coff), in0=s32ap(rows),
                in1=sdap(38912, rows, coff), op=ALU.subtract), incs=[(ve, 1)])
        P.emit("gpsimd", lambda e, gc=gc: e.dma_start(
            out=self.ssnap_scr.ap()[:, gc * 1024: gc * 1024 + 512], in_=self.Sd_hi),
            waits=[(ve, ve.count)], incs=[(stt, 16)])
        P.emit("gpsimd", lambda e, gc=gc: e.dma_start(
            out=self.ssnap_scr.ap()[:, gc * 1024 + 512: (gc + 1) * 1024],
            in_=self.Sd_lo), incs=[(stt, 16)])
        P.emit("vector", lambda e: e.memset(self.ups_hi[op, :], 0.0),
               waits=[(pe, pe.count)])
        P.emit("vector", lambda e: e.memset(self.ups_lo[op, :], 0.0), incs=[(ve, 1)])
        for m in range(4):
            wh = self.wTs_hi[:, m * 128 + c * 64: m * 128 + (c + 1) * 64]
            wl = self.wTs_lo[:, m * 128 + c * 64: m * 128 + (c + 1) * 64]
            sh = self.Sd_hi[:, m * 128:(m + 1) * 128]
            sl = self.Sd_lo[:, m * 128:(m + 1) * 128]
            mtok = self.mm3(bUP[cp, m * 128:(m + 1) * 128], wh, wl, sh, sl,
                            start=True, stop=True,
                            waits=[(ve, ve.count)], incs=[(pe, 1)])
            for z in range(2):
                kv = 2 * m + z
                ucol = 64 if z == 0 else 0
                up32 = self.scr1[cp, kv * 64:(kv + 1) * 64]
                u_sl = self.uw_f[cp, kv * 128 + ucol: kv * 128 + ucol + 64]
                P.emit("vector", lambda e, up32=up32, u=u_sl, m=m, z=z: e.tensor_add(
                    out=up32, in0=u,
                    in1=bUP[cp, m * 128 + z * 64: m * 128 + (z + 1) * 64]),
                    waits=[(pe, mtok)])
                self.split("vector", up32,
                           self.ups_hi[cp, kv * 64:(kv + 1) * 64],
                           self.ups_lo[cp, kv * 64:(kv + 1) * 64], incs=[(ve, 1)])
        P.emit("gpsimd", lambda e, gc=gc: e.dma_start(
            out=self.ups_scr.ap()[:, gc * 1024: gc * 1024 + 512], in_=self.ups_hi),
            waits=[(ve, ve.count)], incs=[(stt, 16)])
        P.emit("gpsimd", lambda e, gc=gc: e.dma_start(
            out=self.ups_scr.ap()[:, gc * 1024 + 512: (gc + 1) * 1024],
            in_=self.ups_lo), incs=[(stt, 16)])
        for kv in range(8):
            base = (kv % 2) * 64
            pc = kv // 2
            kcol = slice(tt * 512 + pc * 128 + (kv % 2) * 64,
                         tt * 512 + pc * 128 + (kv % 2) * 64 + 64)
            kh = self.kltm_hi[:, kcol]
            kl = self.kltm_lo[:, kcol]
            uh = self.ups_hi[:, kv * 64:(kv + 1) * 64]
            ul = self.ups_lo[:, kv * 64:(kv + 1) * 64]
            self.mm3(bS[slice(base, base + 64), (kv // 2) * 64:(kv // 2 + 1) * 64],
                     kh, kl, uh, ul, start=True, stop=True,
                     waits=[(ve, ve.count), (stt, stt.count)],
                     incs=[(pe, 1)] if kv == 7 else ())
        stok = P.sem("pe").count
        P.emit("vector", lambda e: e.tensor_add(
            out=self.S32[:], in0=self.S32[:], in1=bS[:, 0:256]),
            waits=[(pe, stok)], incs=[(ve, 1)])

    def st_q(self, hs):
        P = self.P
        ld, ve, pe, sc = P.sem("ld"), P.sem("ve"), P.sem("pe"), P.sem("sc")
        for qh in range(2):
            last_mm = [pe.count, pe.count]
            last_split = [0, 0]
            qtok = {}
            for dc in range(NC):
                bi = dc % 2
                sbuf = self.wstrip[bi]
                sph = self.wsp_hi[:, bi * 1024:(bi + 1) * 1024]
                spl = self.wsp_lo_x[:, bi * 1024:(bi + 1) * 1024]
                tok_ld = P.emit("sync", lambda e, b=sbuf, dc=dc, qh=qh: e.dma_start(
                    out=b, in_=bass.AP(self.wq_in, dc * 128 * DM + qh * 1024,
                                       [[DM, 128], [1, 1024]])),
                    waits=[(ve, last_split[bi]), (pe, last_mm[bi])],
                    incs=[(ld, 16)])
                vtok = self.split("vector", sbuf, sph, spl,
                                  waits=[(ld, tok_ld), (pe, last_mm[bi])],
                                  incs=[(ve, 1)])
                last_split[bi] = vtok
                for qi in range(8):
                    qc = qh * 8 + qi
                    lh = sph[:, qi * 128:(qi + 1) * 128]
                    ll = spl[:, qi * 128:(qi + 1) * 128]
                    rh = self.xs_hi[:, dc * HS:(dc + 1) * HS]
                    rl = self.xs_lo[:, dc * HS:(dc + 1) * HS]
                    tok = self.mm3(self.pb[qi][:, 0:256], lh, ll, rh, rl,
                                   start=(dc == 0), stop=(dc == NC - 1),
                                   waits=[(ve, vtok)], incs=[(pe, 1)])
                    if dc == NC - 1:
                        qtok[qc] = tok
                last_mm[bi] = tok
            for qi in range(8):
                qc = qh * 8 + qi
                bank = self.pb[qi][:, 0:256]
                w = [(pe, qtok[qc])]
                co = qc * HS
                stok = P.emit("scalar", lambda e, b=bank, qi=qi: e.activation(
                    out=self.eqt[:, qi * 256:(qi + 1) * 256], in_=b, func=AF.Exp),
                    waits=w, incs=[(sc, 1)])
                self.split("vector", bank, self.q_hi[:, co:co + 256],
                           self.q_lo[:, co:co + 256], waits=w + [(sc, stok)],
                           incs=[(ve, 1)])
            for qi in (0, 2, 4, 6):
                src = self.eqt[:, qi * 256:(qi + 2) * 256]
                bb, btok = self.bcast_recip(src, waits=[(sc, sc.count),
                                                        (ve, ve.count)], w=512)
                P.emit("vector", lambda e, s=src, bb=bb: e.tensor_tensor(
                    out=s, in0=s, in1=bb, op=ALU.mult), waits=[(pe, btok)])
                for q2 in range(2):
                    qc = qh * 8 + qi + q2
                    sap = bass.AP(self.fp, 4096 + (qi + q2) * 256,
                                  [[13568, 128], [64, 4], [1, 64]])
                    hap = bass.AP(self.bp2, self.eqb + qc * 64,
                                  [[22912, 128], [1024, 4], [1, 64]])
                    lap = bass.AP(self.bp2, self.eqb + 4096 + qc * 64,
                                  [[22912, 128], [1024, 4], [1, 64]])
                    P.emit("vector", lambda e, s=sap, h=hap: e.tensor_copy(
                        out=h, in_=s))
                    P.emit("vector", lambda e, s=sap, h=hap, l=lap: e.tensor_tensor(
                        out=l, in0=s, in1=h, op=ALU.subtract), incs=[(ve, 1)])

    MASK_BASE = (0, 6, 14, 28)
    NJT = ((6, 8), (14, 16))

    def st_std(self, oi, slot, h2):
        P = self.P
        ld, ve, pe, sc = P.sem("ld"), P.sem("ve"), P.sem("pe"), P.sem("sc")
        hzv = P.sem("hzv")
        njt = self.NJT[slot][h2]
        mbase = self.MASK_BASE[oi]
        bSc = [self.pb[0], self.pb[1]]
        bO = self.pb[2]
        bRs = [self.pb[3], self.pb[4]]
        bBc = self.pb[5]
        mld = 0
        for jt in range(njt):
            mld = P.emit("sync", lambda e, jt=jt: e.dma_start(
                out=self.bp2[:, self.mkb + jt * 256:self.mkb + (jt + 1) * 256],
                in_=self.masks_in.ap()[:, (mbase + jt) * 256:(mbase + jt + 1) * 256]),
                waits=[(pe, pe.count), (ve, ve.count)] if jt == 0 else (),
                incs=[(ld, 16)])
        PtA = (self.bp1[:, 32768:33280], self.bp1[:, 33280:33792])
        PtB = (self.bp1[:, 33792:34304], self.bp1[:, 34304:34816])
        Pt = (PtA, PtB)
        pexp_h = (self.pexp, self.scr1[:, 0:512])
        t1 = self.uw_f[:, 0:512]
        for qp in range(8):
            kvA = 2 * (qp // 2)
            otok = [None, None]
            rstok = [None, None]
            sp_prev = 0
            pv_prev = 0
            qsl = slice(2 * qp * HS, (2 * qp + 2) * HS)
            for jt in range(njt):
                stok = [0, 0]
                for hi_ in range(2):
                    kv = kvA + hi_
                    base = (kv % 2) * 64
                    rows = slice(base, base + 64)
                    pc = kv // 2
                    kc = slice(pc * 2048 + jt * 128, pc * 2048 + (jt + 1) * 128)
                    tok = self.mm3(bSc[hi_][:, 0:512], self.k_hi[rows, kc],
                                   self.k_lo[rows, kc],
                                   self.q_hi[rows, qsl], self.q_lo[rows, qsl],
                                   start=True, stop=True,
                                   tile_position=(base, 0),
                                   waits=[(ve, ve.count), (sc, sc.count)],
                                   incs=[(pe, 1)])
                    stok[hi_] = P.emit("scalar", lambda e, hi_=hi_: e.activation(
                        out=pexp_h[hi_], in_=bSc[hi_][:, 0:512],
                        func=AF.Exp, scale=0.125),
                        waits=[(pe, tok), (ve, sp_prev)], incs=[(sc, 1)])
                mask_ap = bass.AP(self.bp2, self.mkb + jt * 256,
                                  [[22912, 128], [0, 2], [1, 256]])
                for hi_ in range(2):
                    P.emit("vector", lambda e, hi_=hi_, m=mask_ap: e.tensor_tensor(
                        out=pexp_h[hi_], in0=pexp_h[hi_], in1=m, op=ALU.mult),
                        waits=[(sc, stok[hi_]), (pe, pv_prev), (ld, mld)],
                        incs=[(ve, 1)])
                    sp_prev = self.split("vector", pexp_h[hi_],
                                         Pt[hi_][0], Pt[hi_][1], incs=[(ve, 1)])
                for hi_ in range(2):
                    P.emit("tensor", lambda e, hi_=hi_, jt=jt: e.matmul(
                        bRs[hi_][0:1, 0:512], self.onesb[:, 0:1],
                        Pt[hi_][0], start=(jt == 0), stop=False),
                        waits=[(ve, sp_prev)])
                    tok = P.emit("tensor", lambda e, hi_=hi_, jt=jt: e.matmul(
                        bRs[hi_][0:1, 0:512], self.onesb[:, 0:1],
                        Pt[hi_][1], start=False, stop=(jt == njt - 1)),
                        incs=[(pe, 1)] if jt == njt - 1 else ())
                    if jt == njt - 1:
                        rstok[hi_] = tok
                    kv = kvA + hi_
                    base = (kv % 2) * 64
                    vc = slice(jt * 512 + kv * 64, jt * 512 + (kv + 1) * 64)
                    pv_prev = self.mm3(bO[slice(base, base + 64), 0:512],
                                       self.v_hi[:, vc], self.v_lo[:, vc],
                                       Pt[hi_][0], Pt[hi_][1],
                                       start=(jt == 0), stop=(jt == njt - 1),
                                       incs=[(pe, 1)])
                    if jt == njt - 1:
                        otok[hi_] = pv_prev
            v1 = P.emit("vector", lambda e: e.reciprocal(
                out=self.misc[0:1, 0:512], in_=bRs[0][0:1, 0:512]),
                waits=[(pe, rstok[0])], incs=[(hzv, 1)])
            v2 = P.emit("vector", lambda e: e.reciprocal(
                out=self.misc[0:1, 512:1024], in_=bRs[1][0:1, 0:512]),
                waits=[(pe, rstok[1]), (hzv, v1)], incs=[(hzv, 1)])
            P.emit("tensor", lambda e: e.matmul(
                bBc[0:64, 0:512], self.cst[:, C_E0], self.misc[:, 0:512],
                start=True, stop=True), waits=[(hzv, v2)])
            btok = P.emit("tensor", lambda e: e.matmul(
                bBc[64:128, 0:512], self.cst[:, C_E0], self.misc[:, 512:1024],
                start=True, stop=True), incs=[(pe, 1)])
            P.emit("vector", lambda e, t1=t1: e.tensor_copy(
                out=t1, in_=bBc[:, 0:512]), waits=[(pe, btok)])
            P.emit("vector", lambda e, t1=t1: e.tensor_tensor(
                out=t1, in0=bO[:, 0:512], in1=t1, op=ALU.mult),
                waits=[(pe, otok[0]), (pe, otok[1])])
            osl = slice(2 * qp * HS, (2 * qp + 2) * HS)
            P.emit("vector", lambda e, osl=osl, t1=t1: e.scalar_tensor_tensor(
                out=self.o_comb[:, osl], in0=t1, scalar=0.5,
                in1=self.o_comb[:, osl], op0=ALU.mult, op1=ALU.add),
                incs=[(ve, 1)])

    def st_wo(self, oi):
        P = self.P
        ld, ve, pe, sc = P.sem("ld"), P.sem("ve"), P.sem("pe"), P.sem("sc")
        stt = P.sem("st")
        for i in range(4):
            self.split("vector", self.o_comb[:, i * 1024:(i + 1) * 1024],
                       self.oc_hi[:, i * 1024:(i + 1) * 1024],
                       self.oc_lo[:, i * 1024:(i + 1) * 1024],
                       waits=[(ve, ve.count)], incs=[(ve, 1)])
        for wv in range(2):
            wtok = {}
            last_mm = [pe.count, pe.count]
            last_split = [0, 0]
            for ic in range(NC):
                bi = ic % 2
                sbuf = self.wstrip[bi]
                sph = self.wsp_hi[:, bi * 1024:(bi + 1) * 1024]
                spl = self.wsp_lo_x[:, bi * 1024:(bi + 1) * 1024]
                tok_ld = P.emit("sync", lambda e, b=sbuf, ic=ic, wv=wv: e.dma_start(
                    out=b, in_=bass.AP(self.wo_in, ic * 128 * DM + wv * 1024,
                                       [[DM, 128], [1, 1024]])),
                    waits=[(ve, last_split[bi]), (pe, last_mm[bi])],
                    incs=[(ld, 16)])
                vtok = self.split("vector", sbuf, sph, spl,
                                  waits=[(ld, tok_ld), (pe, last_mm[bi])],
                                  incs=[(ve, 1)])
                last_split[bi] = vtok
                for oc in range(8):
                    dchunk = wv * 8 + oc
                    lh = sph[:, oc * 128:(oc + 1) * 128]
                    ll = spl[:, oc * 128:(oc + 1) * 128]
                    rh = self.oc_hi[:, ic * 256:(ic + 1) * 256]
                    rl = self.oc_lo[:, ic * 256:(ic + 1) * 256]
                    tok = self.mm3(self.pb[oc][:, 0:256], lh, ll, rh, rl,
                                   start=(ic == 0), stop=(ic == NC - 1),
                                   waits=[(ve, vtok)], incs=[(pe, 1)])
                    if ic == NC - 1:
                        wtok[dchunk] = tok
                last_mm[bi] = tok
            for oc in range(8):
                dchunk = wv * 8 + oc
                bank = self.pb[oc][:, 0:256]
                sl = oc % 2
                vtok = P.emit("vector", lambda e, b=bank, sl=sl: e.tensor_copy(
                    out=self.att_st[:, sl * 256:(sl + 1) * 256], in_=b),
                    waits=[(pe, wtok[dchunk]), (stt, max(0, stt.count - 16))],
                    incs=[(ve, 1)])
                off = oi * NC * HS + dchunk * HS
                P.emit("gpsimd", lambda e, sl=sl, off=off: e.dma_start(
                    out=self.attn_scr.ap()[:, off:off + HS],
                    in_=self.att_st[:, sl * 256:(sl + 1) * 256]),
                    waits=[(ve, vtok)], incs=[(stt, 16)])

    def st_slot(self, slot):
        P = self.P
        ve, pe = P.sem("ve"), P.sem("pe")
        for h2 in range(2):
            oi = slot * 2 + h2
            for t in range(2):
                self.ln_tile(self.xg_in, oi * HS + t * 128, t)
            P.emit("vector", lambda e: e.memset(self.o_comb[:], 0.0),
                   waits=[(pe, pe.count)], incs=[(ve, 1)])
            self.st_q(oi)
            self.slot_delta_o(oi)
            self.st_std(oi, slot, h2)
            self.st_wo(oi)

    def slot_delta_o(self, oi):
        P = self.P
        ld, ve, pe, sc = P.sem("ld"), P.sem("ve"), P.sem("pe"), P.sem("sc")
        stt = P.sem("st")
        bA, bO = self.pb[6], self.pb[7]

        def dyn_dma(dst, scr, item, width):
            def fn(e, dst=dst, scr=scr, item=item, width=width):
                v = e.value_load(self.gofs_sb[0:1, item:item + 1])
                inst = e.dma_start(out=dst, in_=bass.AP(scr, v,
                                                        [[scr.shape[1], 128],
                                                         [1, width]]))
                self.nc.free_register(v.val)
                return inst
            return fn
        ktok = P.emit("sync", dyn_dma(self.bp1[:, 32768:36864], self.klf_scr,
                                      oi * 16 + 8, 4096),
                      waits=[(stt, stt.count), (ve, ve.count), (pe, pe.count)],
                      incs=[(ld, 16)])
        for tt in range(2):
            for c in range(2):
                cp = slice(c * 64, (c + 1) * 64)
                ch = tt * 2 + c
                it = oi * 16 + tt * 2 + c
                stok_ = P.emit("sync", dyn_dma(self.bp1[:, 38912:39936],
                                               self.ssnap_scr, it, 1024),
                               waits=[(ve, ve.count)], incs=[(ld, 16)])
                utok = P.emit("sync", dyn_dma(self.bp1[:, 37888:38912],
                                              self.ups_scr, it + 4, 1024),
                              incs=[(ld, 16)])
                for m in range(4):
                    eqo = ch * 1024 + 4 * m * 64
                    eh = self.bp2[:, self.eqb + eqo:self.eqb + eqo + 256]
                    el = self.bp2[:, self.eqb + 4096 + eqo:
                                  self.eqb + 4096 + eqo + 256]
                    for z in range(2):
                        kv = 2 * m + z
                        base = z * 64
                        kcol = slice(m * 256 + tt * 128 + c * 64,
                                     m * 256 + tt * 128 + (c + 1) * 64)
                        klh, kll = self.klfz[z][0][:, kcol], self.klfz[z][1][:, kcol]
                        atok = self.mm3(bA[cp, 0:256], klh, kll, eh, el,
                                        start=True, stop=True,
                                        waits=[(ve, ve.count),
                                               (ld, max(ktok, utok))],
                                        incs=[(pe, 1)])
                        cmask = bass.AP(self.cst, cp.start * CONST_W + 384,
                                        [[CONST_W, 64], [0, 4], [1, 64]])
                        P.emit("vector", lambda e, cp=cp, cm=cmask: e.tensor_tensor(
                            out=self.scr1[cp, 0:256], in0=bA[cp, 0:256],
                            in1=cm, op=ALU.mult),
                            waits=[(pe, atok)])
                        attok = self.split("vector", self.scr1[cp, 0:256],
                                           self.bp1[cp, 39936:40192],
                                           self.bp1[cp, 40192:40448],
                                           incs=[(ve, 1)])
                        sh = self.bp1[:, 38912 + m * 128 + z * 64:
                                      38912 + m * 128 + z * 64 + 64]
                        sl = self.bp1[:, 39424 + m * 128 + z * 64:
                                      39424 + m * 128 + z * 64 + 64]
                        self.mm3(bO[slice(base, base + 64), 0:256], sh, sl, eh, el,
                                 start=True, stop=False)
                        uh = self.ups_hi[:, kv * 64:(kv + 1) * 64]
                        ul = self.ups_lo[:, kv * 64:(kv + 1) * 64]
                        otok = self.mm3(bO[slice(base, base + 64), 0:256], uh, ul,
                                        self.bp1[:, 39936:40192],
                                        self.bp1[:, 40192:40448],
                                        start=False, stop=True,
                                        waits=[(ve, attok)], incs=[(pe, 1)])
                    oap = bass.AP(self.fp, 4 * m * HS + tt * 128 + c * 64,
                                  [[13568, 128], [HS, 4], [1, 64]])
                    bOap = bass.AP(self.pb[7], 0, [[2048 // 4, 128], [64, 4], [1, 64]])
                    P.emit("vector", lambda e, oap=oap, bOap=bOap:
                           e.scalar_tensor_tensor(
                        out=oap, in0=bOap, scalar=0.5,
                        in1=oap, op0=ALU.mult, op1=ALU.add),
                        waits=[(pe, pe.count)], incs=[(ve, 1)])

    def st_ffn(self, slot):
        P = self.P
        ld, ve, pe, sc = P.sem("ld"), P.sem("ve"), P.sem("pe"), P.sem("sc")
        gld, stt = P.sem("gld"), P.sem("st")
        f = self.fp
        outstage = [f[:, t * 2048:(t + 1) * 2048] for t in range(4)]
        gw_f32 = f[:, 6144:8192]
        dw_f32 = f[:, 10240:11264]
        uw_f32 = f[:, 8192:10240]
        gate32 = f[:, 4096:4608]
        fuse32 = self.misc[:, 0:512]
        att_re = self.misc[:, 512:1024]
        xnr_hi = self.bp2[:, 0:8192]
        xnr_lo = self.bp2[:, 8192:16384]
        ffw_g = self.bp2[:, 16384:18432]
        ffw_u = self.bp2[:, 18432:20480]
        ffw_d = self.bp2[:, 20480:21504]
        oi0 = slot * 2
        for t in range(4):
            self.ln_tile(self.xg_in, slot * SEG + t * 128, t,
                         dst_hi=xnr_hi, dst_lo=xnr_lo, cstride=SEG)
        ic_prev = [0, 0, 0]
        tok = 0
        for ic in range(NIC):
            g1 = P.emit("sync", lambda e, ic=ic: e.dma_start(
                out=gw_f32, in_=bass.AP(self.gw_in, ic * 128,
                                        [[NI, 128], [128 * NI, 16], [1, 128]])),
                waits=[(gld, ic_prev[0])], incs=[(ld, 16)])
            g2 = P.emit("sync", lambda e, ic=ic: e.dma_start(
                out=uw_f32, in_=bass.AP(self.uw_in, ic * 128,
                                        [[NI, 128], [128 * NI, 16], [1, 128]])),
                incs=[(ld, 16)])
            c1 = P.emit("gpsimd", lambda e: e.dma_start(
                out=self.w8[:, 0:2048].bitcast(F8), in_=gw_f32),
                waits=[(ld, g1), (gld, ic_prev[1])], incs=[(gld, 16)])
            c2 = P.emit("gpsimd", lambda e: e.dma_start(
                out=self.w8[:, 2048:4096].bitcast(F8), in_=uw_f32),
                waits=[(ld, g2)], incs=[(gld, 16)])
            w1 = P.emit("gpsimd", lambda e: e.dma_start(
                out=ffw_g, in_=self.w8[:, 0:2048].bitcast(F8)),
                waits=[(gld, c1), (pe, ic_prev[2])], incs=[(gld, 16)])
            w2 = P.emit("gpsimd", lambda e: e.dma_start(
                out=ffw_u, in_=self.w8[:, 2048:4096].bitcast(F8)),
                waits=[(gld, c2)], incs=[(gld, 16)])
            ic_prev[0] = c2
            ic_prev[1] = w2
            for dc in range(NC):
                self.mm3(self.pb[0][:], ffw_g[:, dc * 128:(dc + 1) * 128], None,
                         xnr_hi[:, dc * 512:(dc + 1) * 512],
                         xnr_lo[:, dc * 512:(dc + 1) * 512],
                         start=(dc == 0), stop=(dc == NC - 1),
                         waits=[(gld, w2), (sc, sc.count), (ve, ve.count)]
                         if dc == 0 else (),
                         incs=[(pe, 1)] if dc == NC - 1 else (), passes=2)
                tok = self.mm3(self.pb[1][:], ffw_u[:, dc * 128:(dc + 1) * 128],
                               None, xnr_hi[:, dc * 512:(dc + 1) * 512],
                               xnr_lo[:, dc * 512:(dc + 1) * 512],
                               start=(dc == 0), stop=(dc == NC - 1),
                               waits=[(gld, w2), (ve, ve.count)] if dc == 0
                               else (),
                               incs=[(pe, 1)] if dc == NC - 1 else (), passes=2)
            ic_prev[2] = tok
            gtok = P.emit("scalar", lambda e, ic=ic: e.activation(
                out=gate32, in_=self.pb[0][:], func=AF.Silu,
                scale=self.scl[:, ic:ic + 1]), waits=[(pe, tok)], incs=[(sc, 1)])
            P.emit("vector", lambda e, ic=ic: e.scalar_tensor_tensor(
                out=fuse32, in0=self.pb[1][:], scalar=self.scl[:, 48 + ic:49 + ic],
                in1=gate32, op0=ALU.mult, op1=ALU.mult),
                waits=[(sc, gtok), (ve, ve.count)])
            self.split("vector", fuse32,
                       self.fuse_hi[:, ic * 512:(ic + 1) * 512],
                       self.fuse_lo[:, ic * 512:(ic + 1) * 512], incs=[(ve, 1)])
        ostok = {}
        for t in range(4):
            row0 = slot * SEG + t * 128
            ostok[t] = P.emit("sync", lambda e, t=t, o=row0: e.dma_start(
                out=outstage[t], in_=self.xg_in.ap()[o:o + 128, :]),
                waits=[(pe, pe.count), (ve, ve.count), (sc, sc.count),
                       (stt, stt.count)],
                incs=[(ld, 16)])
        for dh in range(2):
            dtok = {}
            dprev = [P.sem("gld").count, P.sem("pe").count]
            for ic in range(NIC):
                d1 = P.emit("sync", lambda e, ic=ic, dh=dh: e.dma_start(
                    out=dw_f32, in_=bass.AP(self.dw_in, ic * 128 * DM + dh * 1024,
                                            [[DM, 128], [1, 1024]])),
                    waits=[(gld, dprev[0])], incs=[(ld, 16)])
                c1 = P.emit("gpsimd", lambda e: e.dma_start(
                    out=self.w8[:, 0:1024].bitcast(F8), in_=dw_f32),
                    waits=[(ld, d1)], incs=[(gld, 16)])
                w1 = P.emit("gpsimd", lambda e: e.dma_start(
                    out=ffw_d, in_=self.w8[:, 0:1024].bitcast(F8)),
                    waits=[(gld, c1), (pe, dprev[1])], incs=[(gld, 16)])
                dprev[0] = w1
                for od in range(8):
                    tok = self.mm3(self.pb[od][:],
                                   ffw_d[:, od * 128:(od + 1) * 128], None,
                                   self.fuse_hi[:, ic * 512:(ic + 1) * 512],
                                   self.fuse_lo[:, ic * 512:(ic + 1) * 512],
                                   start=(ic == 0), stop=(ic == NIC - 1),
                                   waits=[(gld, w1), (ve, ve.count)]
                                   if ic == 0 else ([(gld, w1)] if od == 0 else ()),
                                   incs=[(pe, 1)] if (ic == NIC - 1 or od == 7)
                                   else (), passes=2)
                    if ic == NIC - 1:
                        dtok[od] = tok
                dprev[1] = tok
            for od in range(8):
                dchunk = dh * 8 + od
                atok = P.emit("sync", lambda e, dchunk=dchunk: e.dma_start(
                    out=att_re, in_=bass.AP(self.attn_scr,
                                            oi0 * 4096 + dchunk * 256,
                                            [[16384, 128], [4096, 2], [1, 256]])),
                    waits=[(ve, ve.count)], incs=[(ld, 16)])
                ttok = P.emit("vector", lambda e, od=od, dchunk=dchunk:
                              e.scalar_tensor_tensor(
                    out=self.stage[:, 0:512], in0=self.pb[od][:],
                    scalar=self.scl[:, 96 + dchunk:97 + dchunk], in1=att_re,
                    op0=ALU.mult, op1=ALU.add),
                    waits=[(pe, dtok[od]), (ld, atok)], incs=[(ve, 1)])
                for t in range(4):
                    ptok = P.emit("tensor", lambda e, od=od, t=t: e.transpose(
                        self.pb[od][:, 0:128],
                        self.stage[:, t * 128:(t + 1) * 128],
                        self.cst[:, C_IDENT]),
                        waits=[(ve, ve.count), (pe, pe.count)], incs=[(pe, 1)])
                    P.emit("vector", lambda e, od=od, t=t, dchunk=dchunk:
                           e.tensor_add(
                        out=outstage[t][:, dchunk * 128:(dchunk + 1) * 128],
                        in0=self.pb[od][:, 0:128],
                        in1=outstage[t][:, dchunk * 128:(dchunk + 1) * 128]),
                        waits=[(pe, ptok), (ld, ostok[t])], incs=[(ve, 1)])
        for t in range(4):
            row0 = slot * SEG + t * 128
            P.emit("gpsimd", lambda e, t=t, row0=row0: e.dma_start(
                out=self.out.ap()[row0:row0 + 128, :], in_=outstage[t]),
                waits=[(ve, ve.count)], incs=[(stt, 16)])

    def finish(self):
        P = self.P
        stt = P.sem("st")
        for name, t, ap in self.taps:
            P.emit("gpsimd", lambda e, t=t, ap=ap: e.dma_start(out=t.ap(), in_=ap),
                   waits=P.barrier_counts(), incs=[(stt, 16)])
        P.emit("sync", lambda e: None, waits=[(stt, stt.count)])

    def build(self, upto=None):
        self.alloc()
        self.st_const()
        for hs in range(NHS):
            self.st_xn(hs)
            self.st_proj(hs)
            self.st_prep(hs)
            self.st_delta(hs)
            if upto is not None and hs >= upto:
                self.finish()
                self.P.build_into(self.nc, self.ctx)
                return self.nc
            if hs == 3:
                self.st_slot(0)
            if hs == 7:
                self.st_slot(1)
        self.st_ffn(0)
        self.st_ffn(1)
        self.finish()
        self.P.build_into(self.nc, self.ctx)
        return self.nc


def make_masks(own):
    m = np.zeros((128, 44 * 256), dtype=ml_dtypes.bfloat16)
    base = CoreBuilder.MASK_BASE
    njts = CoreBuilder.NJT
    j = np.arange(128)[:, None]
    r = np.arange(256)[None, :]
    for slot in range(2):
        for h2 in range(2):
            oi = slot * 2 + h2
            g = own[slot]
            ghs = 2 * g + h2
            for jt in range(njts[slot][h2]):
                tile = ((ghs * 256 + r) >= (jt * 128 + j))
                off = (base[oi] + jt) * 256
                m[:, off:off + 256] = tile.astype(ml_dtypes.bfloat16)
    return m


def make_gofs(own):
    g = np.zeros((1, 64), dtype=np.int32)
    for slot in range(2):
        for h2 in range(2):
            oi = slot * 2 + h2
            ghs = 2 * own[slot] + h2
            for tt in range(2):
                for c in range(2):
                    gc = 4 * ghs + 2 * tt + c
                    g[0, oi * 16 + tt * 2 + c] = gc * 1024
                    g[0, oi * 16 + 4 + tt * 2 + c] = gc * 1024
            g[0, oi * 16 + 8] = ghs * 4096
    return g


_CACHE = {}


def _get_nc():
    if "nc" not in _CACHE:
        _CACHE["nc"] = CoreBuilder(0).build()
    return _CACHE["nc"]


def _shard_inputs(inputs):
    perm = QCOL_PERM
    consts = make_consts()
    base = {
        "wq": np.ascontiguousarray(inputs["L0_attn_Wq"][:, perm]),
        "wk": np.ascontiguousarray(inputs["L0_attn_Wk"]),
        "wv": np.ascontiguousarray(inputs["L0_attn_Wv"]),
        "wo": np.ascontiguousarray(inputs["L0_attn_Wo"][perm, :]),
        "gw": np.ascontiguousarray(inputs["L0_ffn_gate_w"]),
        "uw": np.ascontiguousarray(inputs["L0_ffn_up_w"]),
        "dw": np.ascontiguousarray(inputs["L0_ffn_down_w"]),
        "gs": np.ascontiguousarray(inputs["L0_ffn_gate_scale"]),
        "us": np.ascontiguousarray(inputs["L0_ffn_up_scale"]),
        "ds": np.ascontiguousarray(inputs["L0_ffn_down_scale"]),
        "consts": consts,
    }
    masks = {h: make_masks(OWN[h]) for h in (0, 1)}
    gofs = {h: make_gofs(OWN[h]) for h in (0, 1)}
    x = inputs["L0_attn_hidden_states"]
    in_maps = []
    for core in range(8):
        b, h = core // 2, core % 2
        m = dict(base)
        m["x"] = np.ascontiguousarray(x[b])
        rows = np.r_[OWN[h][0] * SEG:(OWN[h][0] + 1) * SEG,
                     OWN[h][1] * SEG:(OWN[h][1] + 1) * SEG]
        m["xg"] = np.ascontiguousarray(x[b][rows])
        m["masks"] = masks[h]
        m["gofs"] = gofs[h]
        in_maps.append(m)
    return in_maps


def kernel(L0_attn_hidden_states, L0_attn_Wq, L0_attn_Wk, L0_attn_Wv, L0_attn_Wo,
           L0_ffn_gate_w, L0_ffn_up_w, L0_ffn_down_w, L0_ffn_gate_scale,
           L0_ffn_up_scale, L0_ffn_down_scale):
    inputs = dict(
        L0_attn_hidden_states=np.asarray(L0_attn_hidden_states),
        L0_attn_Wq=np.asarray(L0_attn_Wq), L0_attn_Wk=np.asarray(L0_attn_Wk),
        L0_attn_Wv=np.asarray(L0_attn_Wv), L0_attn_Wo=np.asarray(L0_attn_Wo),
        L0_ffn_gate_w=np.asarray(L0_ffn_gate_w),
        L0_ffn_up_w=np.asarray(L0_ffn_up_w),
        L0_ffn_down_w=np.asarray(L0_ffn_down_w),
        L0_ffn_gate_scale=np.asarray(L0_ffn_gate_scale),
        L0_ffn_up_scale=np.asarray(L0_ffn_up_scale),
        L0_ffn_down_scale=np.asarray(L0_ffn_down_scale))
    in_maps = _shard_inputs(inputs)
    nc = _get_nc()
    res = run_bass_kernel_spmd(nc, in_maps, core_ids=list(range(8)))
    out = np.empty((B, N, DM), dtype=np.float32)
    for core in range(8):
        b, h = core // 2, core % 2
        o = res.results[core]["out"]
        for slot in range(2):
            g = OWN[h][slot]
            out[b, g * SEG:(g + 1) * SEG] = o[slot * SEG:(slot + 1) * SEG]
    return out


# version 04
# speedup vs baseline: 1.0352x; 1.0352x over previous
import numpy as np
import ml_dtypes
import concourse.bass as bass
import concourse.mybir as mybir
from concourse.bass_utils import run_bass_kernel_spmd
from contextlib import ExitStack

F32 = mybir.dt.float32
BF16 = mybir.dt.bfloat16
F8 = mybir.dt.float8e4
I8 = mybir.dt.int8
AF = mybir.ActivationFunctionType
ALU = mybir.AluOpType
AX = mybir.AxisListType

B, N, DM = 4, 2048, 2048
H, HD, HKV = 32, 64, 8
SEG = 512
HS = 256
NHS = N // HS
NC = DM // 128
NI = 6144
NIC = NI // 128
OWN = {0: (0, 3), 1: (1, 2)}

QA = [8 * (j // 4) + j % 4 for j in range(16)]
QB = [8 * (j // 4) + 4 + j % 4 for j in range(16)]
QCOL_PERM = np.concatenate(
    [np.r_[QA[j] * 64:QA[j] * 64 + 64, QB[j] * 64:QB[j] * 64 + 64]
     for j in range(16)])

ENGS = ("sync", "tensor", "vector", "scalar", "gpsimd")


class Sem:
    __slots__ = ("name", "count", "handle")

    def __init__(self, name):
        self.name = name
        self.count = 0
        self.handle = None


class Prog:
    def __init__(self):
        self.ops = {e: [] for e in ENGS}
        self.sems = {}

    def sem(self, name):
        if name not in self.sems:
            self.sems[name] = Sem(name)
        return self.sems[name]

    def emit(self, eng, fn, waits=(), incs=()):
        incs = tuple(incs)
        for s, n in incs:
            s.count += n
        self.ops[eng].append((fn, tuple(waits), incs))
        return incs[0][0].count if incs else None

    def barrier_counts(self):
        return [(s, s.count) for s in self.sems.values() if s.count > 0]

    def build_into(self, nc, ctx):
        for s in self.sems.values():
            s.handle = ctx.enter_context(nc.semaphore(s.name))
        block = ctx.enter_context(nc.Block())
        for eng in ENGS:
            ops = self.ops[eng]

            def emit_all(engine, ops=ops):
                seen = {}
                for fn, waits, incs in ops:
                    for s, v in waits:
                        if seen.get(s.name, -1) < v:
                            engine.wait_ge(s.handle, v)
                            seen[s.name] = v
                    inst = fn(engine)
                    for s, n in incs:
                        if inst is None:
                            inst = engine.nop()
                        inst.then_inc(s.handle, n)
                        inst = None
            getattr(block, eng)(emit_all)


C_IDENT = slice(0, 128)
C_ONES = slice(128, 256)
C_STRICTL2 = slice(256, 320)
C_STRICTT2 = slice(320, 384)
C_CMASK2 = slice(384, 448)
C_CM256A = slice(448, 704)
C_CM256B = slice(704, 960)
C_HONES = slice(960, 962)
C_E0 = slice(962, 1026)
CONST_W = 1088


def make_consts():
    c = np.zeros((128, CONST_W), dtype=np.float32)
    p = np.arange(128)[:, None]
    f64 = np.arange(64)[None, :]
    f256 = np.arange(256)[None, :]
    c[:, C_IDENT] = np.eye(128)
    c[:, C_ONES] = 1.0
    c[:, C_STRICTL2] = (f64 < (p % 64)).astype(np.float32)
    c[:, C_STRICTT2] = ((p % 64) < f64).astype(np.float32)
    c[:, C_CMASK2] = (f64 >= (p % 64)).astype(np.float32)
    c[:, C_CM256A] = (f256 >= p).astype(np.float32)
    c[:, C_CM256B] = (f256 >= p + 128).astype(np.float32)
    c[0:64, 960:961] = 1.0
    c[64:128, 961:962] = 1.0
    c[0:1, 962:1026] = 1.0
    return c


class CoreBuilder:
    def __init__(self, half):
        self.half = half
        self.own = OWN[half]
        self.nc = bass.Bass("TRN2", target_bir_lowering=False,
                            detect_race_conditions=False)
        self.P = Prog()
        self.ctx = ExitStack()
        self.taps = []

    def sb(self, name, shape, dt):
        return self.ctx.enter_context(self.nc.sbuf_tensor(name, list(shape), dt))

    def ps(self, name, shape, dt=F32):
        return self.ctx.enter_context(self.nc.psum_tensor(name, list(shape), dt))

    def tap(self, name, ap, shape, dt=F32):
        t = self.nc.dram_tensor(name, list(shape), dt, kind="ExternalOutput")
        self.taps.append((name, t, ap))

    def split(self, eng, src, hi, lo, waits=(), incs=()):
        self.P.emit(eng, lambda e, s=src, h=hi: e.tensor_copy(out=h, in_=s),
                    waits=waits)
        return self.P.emit(eng, lambda e, s=src, h=hi, l=lo: e.tensor_tensor(
            out=l, in0=s, in1=h, op=ALU.subtract), incs=incs)

    def mm3(self, psum, lh, ll, rh, rl, start, stop, tile_position=None,
            waits=(), incs=(), passes=3):
        P = self.P
        tp = tile_position
        P.emit("tensor", lambda e: e.matmul(psum, lh, rh, start=start, stop=False,
                                            tile_position=tp), waits=waits)
        if passes == 2:
            return P.emit("tensor", lambda e: e.matmul(
                psum, lh, rl, start=False, stop=stop, tile_position=tp), incs=incs)
        P.emit("tensor", lambda e: e.matmul(psum, lh, rl, start=False, stop=False,
                                            tile_position=tp))
        return P.emit("tensor", lambda e: e.matmul(
            psum, ll, rh, start=False, stop=stop, tile_position=tp), incs=incs)

    def reset_pe(self, waits=()):
        return self.P.emit("tensor", lambda e: e.ldweights(self.identb[:]),
                           waits=waits)

    def alloc(self):
        nc = self.nc
        din = lambda n, s, d=F32: nc.dram_tensor(n, list(s), d, kind="ExternalInput")
        self.x_in = din("x", [N, DM])
        self.wq_in = din("wq", [DM, DM])
        self.wk_in = din("wk", [DM, 512])
        self.wv_in = din("wv", [DM, 512])
        self.wo_in = din("wo", [DM, DM])
        self.gw_in = din("gw", [NIC * 128, 2048])
        self.uw_in = din("uw", [NIC * 128, 2048])
        self.dw_in = din("dw", [2 * NIC * 128, 1024])
        self.gs_in = din("gs", [NI])
        self.us_in = din("us", [NI])
        self.ds_in = din("ds", [DM])
        self.const_in = din("consts", [128, CONST_W])
        self.xg_in = din("xg", [2 * SEG, DM])
        self.masks_in = nc.dram_tensor("masks", [128, 44 * 256], BF16,
                                       kind="ExternalInput")
        self.gofs_in = nc.dram_tensor("gofs", [1, 64], mybir.dt.int32,
                                      kind="ExternalInput")
        self.out = nc.dram_tensor("out", [2 * SEG, DM], F32, kind="ExternalOutput")
        self.attn_scr = nc.dram_tensor("attn_scr", [128, 4 * NC * HS], F32)
        self.ssnap_scr = nc.dram_tensor("ssnap_scr", [128, 32 * 1024], BF16)
        self.ups_scr = nc.dram_tensor("ups_scr", [128, 32 * 1024], BF16)
        self.klf_scr = nc.dram_tensor("klf_scr", [128, 8 * 4096], BF16)

        self.cst = self.sb("cst", [128, CONST_W], F32)
        self.onesb = self.sb("onesb", [128, 2], BF16)
        self.identb = self.sb("identb", [128, 128], BF16)
        self.fconst = self.sb("fconst", [128, 4], F32)
        self.lnstat = self.sb("lnstat", [128, 4], F32)
        self.scl = self.sb("scl", [128, 112], F32)
        self.bp1 = self.sb("bp1", [128, 49152], BF16)
        t = self.bp1
        self.k_hi = t[:, 0:8192]; self.k_lo = t[:, 8192:16384]
        self.v_hi = t[:, 16384:24576]; self.v_lo = t[:, 24576:32768]
        self.xs_hi = t[:, 32768:36864]; self.xs_lo = t[:, 36864:40960]
        self.klfz = [(t[:, 32768:33792], t[:, 33792:34816]),
                     (t[:, 34816:35840], t[:, 35840:36864])]
        self.wTs_hi = t[:, 36864:37376]; self.wTs_lo = t[:, 37376:37888]
        self.ups_hi = t[:, 37888:38400]; self.ups_lo = t[:, 38400:38912]
        self.Sd_hi = t[:, 38912:39424]; self.Sd_lo = t[:, 39424:39936]
        self.slvP = t[:, 39936:40064]; self.slvPT = t[:, 40064:40192]
        self.slvP2 = t[:, 40192:40320]; self.slvPT2 = t[:, 40320:40448]
        self.ybf = t[:, 40448:40576]
        self.Pt_hi = t[:, 32768:33280]; self.Pt_lo = t[:, 33280:33792]
        self.q_hi = t[:, 40960:45056]; self.q_lo = t[:, 45056:49152]
        self.eq_hi = t[:, 45056 - 4096:45056 - 2048] if False else None
        self.fuse_hi = t[:, 0:24576]; self.fuse_lo = t[:, 24576:49152]
        self.bp2 = self.sb("bp2", [128, 22912], BF16)
        u = self.bp2
        self.kbf_hi = u[:, 0:1024]; self.kbf_lo = u[:, 1024:2048]
        self.kltm_hi = u[:, 2048:3072]; self.kltm_lo = u[:, 3072:4096]
        self.kbtm_hi = u[:, 4096:5120]; self.kbtm_lo = u[:, 5120:6144]
        self.vbtm_hi = u[:, 6144:7168]; self.vbtm_lo = u[:, 7168:8192]
        self.oc_hi = u[:, 0:4096]; self.oc_lo = u[:, 4096:8192]
        self.Ssn_hi = u[:, 8192:8704]; self.Ssn_lo = u[:, 8704:9216]
        self.wsp_hi = u[:, 9216:11264]; self.wsp_lo_x = u[:, 11264:13312]
        self.mkb = 13312
        self.eqb = 0
        self.eq_hi = u[:, 0:4096]; self.eq_lo = u[:, 4096:8192]
        self.xnr_hi = u[:, 0:8192]; self.xnr_lo = u[:, 8192:16384]
        self.w8 = self.sb("w8", [128, 4096], I8)
        self.fp = self.sb("fp", [128, 13568], F32)
        f = self.fp
        self.xt = [f[:, 0:2048], f[:, 2048:4096]]
        self.o_comb = f[:, 0:4096]
        self.sq_scr = f[:, 4096:6144]
        self.k_raw = f[:, 4096:5120]; self.v_raw = f[:, 5120:6144]
        self.ektmp = f[:, 6144:7168]
        self.gltm = f[:, 6144:7168]
        self.klf32 = f[:, 7168:8192]
        self.gl32 = f[:, 8192:9216]
        self.scr1 = f[:, 9216:10240]
        self.uw_f = f[:, 10240:11264]
        self.pexp = f[:, 11264:11776]
        self.wstrip = [f[:, 6144:7168], f[:, 7168:8192]]
        self.eqt = f[:, 4096:6144]
        self.stage = f[:, 11264:11776]
        self.misc = f[:, 11776:13568]
        self.rcpA = self.misc[0:1, 0:256]
        self.rcpB = self.misc[0:1, 256:512]
        self.bcast_sb = self.misc[:, 512:768]
        self.t1sb = self.misc[:, 768:1024]
        self.att_st = self.misc[:, 1024:1536]
        self.S32 = self.misc[:, 1536:1792]
        self.gofs_sb = self.sb("gofs_sb", [1, 64], mybir.dt.int32)
        self.Ssn_hi = self.bp2[:, 21504:21760]
        self.Ssn_lo = self.bp2[:, 21760:22016]
        self.pb = [self.ps(f"pb{i}", [128, 512]) for i in range(8)]

    def st_const(self):
        P = self.P
        ld = P.sem("ld")
        ve = P.sem("ve")
        P.emit("sync", lambda e: e.dma_start(out=self.cst[:], in_=self.const_in.ap()),
               incs=[(ld, 16)])
        P.emit("sync", lambda e: e.dma_start(
            out=self.scl[:, 0:48],
            allow_slow_non_contiguous=True,
            in_=bass.AP(self.gs_in, 0, [[1, 128], [128, 48]])), incs=[(ld, 16)])
        P.emit("sync", lambda e: e.dma_start(
            out=self.scl[:, 48:96],
            allow_slow_non_contiguous=True,
            in_=bass.AP(self.us_in, 0, [[1, 128], [128, 48]])), incs=[(ld, 16)])
        P.emit("sync", lambda e: e.dma_start(
            out=self.scl[:, 96:112],
            allow_slow_non_contiguous=True,
            in_=bass.AP(self.ds_in, 0, [[1, 128], [128, 16]])), incs=[(ld, 16)])
        tok = P.emit("sync", lambda e: e.dma_start(
            out=self.gofs_sb[:], in_=self.gofs_in.ap()), incs=[(ld, 16)])
        P.emit("vector", lambda e: e.memset(self.onesb[:], 1.0), waits=[(ld, tok)])
        P.emit("vector", lambda e: e.memset(self.fconst[:, 0:1], 1e-5))
        P.emit("vector", lambda e: e.tensor_copy(out=self.identb[:],
                                                 in_=self.cst[:, C_IDENT]))
        P.emit("vector", lambda e: e.memset(self.misc[:], 0.0))
        P.emit("vector", lambda e: e.memset(self.Sd_hi[:], 0.0))
        P.emit("vector", lambda e: e.memset(self.Sd_lo[:], 0.0))
        P.emit("vector", lambda e: e.memset(self.slvP[:], 0.0))
        P.emit("vector", lambda e: e.memset(self.slvPT[:], 0.0))
        P.emit("vector", lambda e: e.memset(self.slvP2[:], 0.0))
        P.emit("vector", lambda e: e.memset(self.slvPT2[:], 0.0))
        P.emit("vector", lambda e: e.memset(self.ups_hi[:], 0.0))
        P.emit("vector", lambda e: e.memset(self.ups_lo[:], 0.0))
        P.emit("vector", lambda e: e.memset(self.fconst[0:64, 1:2], 1.0))
        P.emit("vector", lambda e: e.memset(self.fconst[64:128, 1:2], 0.0))
        P.emit("vector", lambda e: e.memset(self.fconst[0:64, 2:3], 0.0))
        vtok = P.emit("vector", lambda e: e.memset(self.fconst[64:128, 2:3], 1.0),
                      incs=[(ve, 1)])
        for e in ENGS:
            P.emit(e, lambda en: None, waits=[(ld, tok), (ve, vtok)])

    def ln_tile(self, src_dram, row0, t, dst_hi=None, dst_lo=None, cstride=HS):
        P = self.P
        ld, ve, pe, sc = P.sem("ld"), P.sem("ve"), P.sem("pe"), P.sem("sc")
        hzv, hzs = P.sem("hzv"), P.sem("hzs")
        if dst_hi is None:
            dst_hi, dst_lo = self.xs_hi, self.xs_lo
        buf = self.xt[t % 2]
        tok_ld = P.emit("sync", lambda e, b=buf, o=row0, s=src_dram: e.dma_start(
            out=b, in_=s.ap()[o:o + 128, :]),
            waits=[(pe, pe.count), (P.sem("st"), P.sem("st").count)],
            incs=[(ld, 16)])
        st = self.lnstat
        h1 = P.emit("vector", lambda e, b=buf: e.reduce_sum(
            out=st[:, 0:1], in_=b, axis=AX.X, negate=True),
            waits=[(ld, tok_ld)], incs=[(hzv, 1)])
        h2 = P.emit("vector", lambda e: e.tensor_scalar_mul(
            out=st[:, 1:2], in0=st[:, 0:1], scalar1=1.0 / DM),
            waits=[(hzv, h1)], incs=[(hzv, 1)])
        vtok = P.emit("vector", lambda e, b=buf: e.tensor_scalar(
            out=b, in0=b, scalar1=st[:, 1:2], scalar2=None, op0=ALU.add),
            waits=[(hzv, h2)], incs=[(ve, 1)])
        h3 = P.emit("scalar", lambda e, b=buf: e.activation(
            out=self.sq_scr, in_=b, func=AF.Square, accum_out=st[:, 2:3]),
            waits=[(ve, vtok)], incs=[(hzs, 1)])
        h4 = P.emit("scalar", lambda e: e.activation(
            out=st[:, 3:4], in_=st[:, 2:3], func=AF.Ln, scale=1.0 / DM,
            bias=self.fconst[:, 0:1]), waits=[(hzs, h3)], incs=[(hzs, 1)])
        stok = P.emit("scalar", lambda e: e.activation(
            out=st[:, 3:4], in_=st[:, 3:4], func=AF.Exp, scale=-0.5),
            waits=[(hzs, h4)], incs=[(sc, 1)])
        vtok = P.emit("vector", lambda e, b=buf: e.tensor_scalar_mul(
            out=b, in0=b, scalar1=st[:, 3:4]), waits=[(sc, stok)],
            incs=[(ve, 1)])
        for c in range(NC):
            bank = self.pb[c % 2]
            ptok = P.emit("tensor", lambda e, b=buf, c=c, bank=bank: e.transpose(
                bank[:, 0:128], b[:, c * 128:(c + 1) * 128], self.cst[:, C_IDENT]),
                waits=[(ve, ve.count)], incs=[(pe, 1)])
            lo_c = c * cstride + t * 128
            self.split("vector", bank[:, 0:128],
                       dst_hi[:, lo_c:lo_c + 128], dst_lo[:, lo_c:lo_c + 128],
                       waits=[(pe, ptok)], incs=[(ve, 1)])

    def st_xn(self, hs):
        for t in range(2):
            self.ln_tile(self.x_in, hs * HS + t * 128, t)

    def st_proj(self, hs):
        P = self.P
        ld, ve, pe, sc = P.sem("ld"), P.sem("ve"), P.sem("pe"), P.sem("sc")
        kv_tok = {}

        def src(dc):
            return bass.AP(self.wk_in, dc * 128 * 512, [[512, 128], [1, 512]]), \
                   bass.AP(self.wv_in, dc * 128 * 512, [[512, 128], [1, 512]])

        last_mm = [pe.count, pe.count]
        last_split = [0, 0]
        for dc in range(NC):
            bi = dc % 2
            sbuf = self.wstrip[bi]
            sph = self.wsp_hi[:, bi * 1024:(bi + 1) * 1024]
            spl = self.wsp_lo_x[:, bi * 1024:(bi + 1) * 1024]
            ksrc, vsrc = src(dc)
            P.emit("sync", lambda e, b=sbuf, s=ksrc: e.dma_start(
                out=b[:, 0:512], in_=s),
                waits=[(ve, last_split[bi]), (pe, last_mm[bi])], incs=[(ld, 16)])
            tok_ld = P.emit("sync", lambda e, b=sbuf, s=vsrc: e.dma_start(
                out=b[:, 512:1024], in_=s), incs=[(ld, 16)])
            vtok = self.split("vector", sbuf, sph, spl,
                              waits=[(ld, tok_ld), (pe, last_mm[bi])], incs=[(ve, 1)])
            last_split[bi] = vtok
            for pc in range(4):
                lh = sph[:, pc * 128:(pc + 1) * 128]
                ll = spl[:, pc * 128:(pc + 1) * 128]
                rh = self.xs_hi[:, dc * HS:(dc + 1) * HS]
                rl = self.xs_lo[:, dc * HS:(dc + 1) * HS]
                tok = self.mm3(self.pb[pc][:, 0:256], lh, ll, rh, rl,
                               start=(dc == 0), stop=(dc == NC - 1),
                               waits=[(ve, vtok)],
                               incs=[(pe, 1)])
                if dc == NC - 1:
                    kv_tok[("k", pc)] = tok
            for tt in range(2):
                lh = self.xs_hi[:, dc * HS + tt * 128: dc * HS + (tt + 1) * 128]
                ll = self.xs_lo[:, dc * HS + tt * 128: dc * HS + (tt + 1) * 128]
                rh = sph[:, 512:1024]
                rl = spl[:, 512:1024]
                tok = self.mm3(self.pb[4 + tt][:], lh, ll, rh, rl,
                               start=(dc == 0), stop=(dc == NC - 1),
                               incs=[(pe, 1)])
                if dc == NC - 1:
                    kv_tok[("v", tt)] = tok
            last_mm[bi] = tok
        for pc in range(4):
            bank = self.pb[pc][:, 0:256]
            w = [(pe, kv_tok[("k", pc)])]
            stok = P.emit("scalar", lambda e, b=bank, pc=pc: e.activation(
                out=self.k_raw[:, pc * 256:(pc + 1) * 256], in_=b, func=AF.Identity),
                waits=w, incs=[(sc, 1)])
            co = pc * 2048 + hs * HS
            self.split("vector", bank, self.k_hi[:, co:co + 256],
                       self.k_lo[:, co:co + 256], waits=w + [(sc, stok)],
                       incs=[(ve, 1)])
        for tt in range(2):
            bank = self.pb[4 + tt][:]
            w = [(pe, kv_tok[("v", tt)])]
            stok = P.emit("scalar", lambda e, b=bank, tt=tt: e.activation(
                out=self.v_raw[:, tt * 512:(tt + 1) * 512], in_=b, func=AF.Identity),
                waits=w, incs=[(sc, 1)])
            co = (hs * 2 + tt) * 512
            self.split("vector", bank, self.v_hi[:, co:co + 512],
                       self.v_lo[:, co:co + 512], waits=w + [(sc, stok)],
                       incs=[(ve, 1)])

    def bcast_recip(self, src, waits=(), w=256):
        P = self.P
        pe, hzv, ve = P.sem("pe"), P.sem("hzv"), P.sem("ve")
        bsA, bsB, bbc = self.pb[5], self.pb[6], self.pb[7]
        t1 = P.emit("tensor", lambda e, s=src, w=w: e.matmul(
            bsA[0:1, 0:w], self.cst[:, 960:961], s, start=True, stop=True),
            waits=waits, incs=[(pe, 1)])
        t2 = P.emit("tensor", lambda e, s=src, w=w: e.matmul(
            bsB[0:1, 0:w], self.cst[:, 961:962], s, start=True, stop=True),
            incs=[(pe, 1)])
        v1 = P.emit("vector", lambda e, w=w: e.reciprocal(
            out=self.misc[0:1, 0:w], in_=bsA[0:1, 0:w]),
            waits=[(pe, t2)], incs=[(hzv, 1)])
        v2 = P.emit("vector", lambda e, w=w: e.reciprocal(
            out=self.misc[0:1, w:2 * w], in_=bsB[0:1, 0:w]),
            waits=[(hzv, v1)], incs=[(hzv, 1)])
        P.emit("tensor", lambda e, w=w: e.matmul(
            bbc[0:64, 0:w], self.cst[:, C_E0], self.misc[:, 0:w],
            start=True, stop=True), waits=[(hzv, v2), (ve, ve.count)])
        btok = P.emit("tensor", lambda e, w=w: e.matmul(
            bbc[64:128, 0:w], self.cst[:, C_E0], self.misc[:, w:2 * w],
            start=True, stop=True), incs=[(pe, 1)])
        return bbc[:, 0:w], btok

    def st_prep(self, hs):
        P = self.P
        ld, ve, pe, sc = P.sem("ld"), P.sem("ve"), P.sem("pe"), P.sem("sc")
        hzs = P.sem("hzs")
        etok = P.emit("scalar", lambda e: e.activation(
            out=self.ektmp, in_=self.k_raw, func=AF.Exp),
            waits=[(sc, sc.count), (ve, ve.count)], incs=[(sc, 1)])
        for pc in (0, 2):
            bb, btok = self.bcast_recip(self.ektmp[:, pc * 256:(pc + 2) * 256],
                                        waits=[(sc, etok), (ve, ve.count)], w=512)
            P.emit("vector", lambda e, pc=pc, bb=bb: e.tensor_tensor(
                out=self.klf32[:, pc * 256:(pc + 2) * 256],
                in0=self.ektmp[:, pc * 256:(pc + 2) * 256], in1=bb, op=ALU.mult),
                waits=[(pe, btok)])
            for z in range(2):
                hi, lo = self.klfz[z]
                msk = self.fconst[:, 1 + z:2 + z]
                sl = slice(pc * 256, (pc + 2) * 256)
                P.emit("vector", lambda e, sl=sl, hi=hi, msk=msk: e.tensor_scalar_mul(
                    out=hi[:, sl], in0=self.klf32[:, sl], scalar1=msk))
                P.emit("vector", lambda e, sl=sl, hi=hi, lo=lo, msk=msk:
                       e.scalar_tensor_tensor(
                    out=lo[:, sl], in0=self.klf32[:, sl], scalar=msk, in1=hi[:, sl],
                    op0=ALU.mult, op1=ALU.subtract), incs=[(ve, 1)])
        g1 = P.emit("scalar", lambda e: e.activation(
            out=self.scr1, in_=self.k_raw, func=AF.Sigmoid), incs=[(hzs, 1)])
        gtok = P.emit("scalar", lambda e: e.activation(
            out=self.gl32, in_=self.scr1, func=AF.Ln), waits=[(hzs, g1)],
            incs=[(sc, 1)])
        P.emit("vector", lambda e: e.scalar_tensor_tensor(
            out=self.scr1, in0=self.klf32, scalar=1.0 / 16.0, in1=self.gl32,
            op0=ALU.mult, op1=ALU.mult), waits=[(sc, gtok), (ve, ve.count)])
        kbtok = self.split("vector", self.scr1, self.kbf_hi, self.kbf_lo,
                           incs=[(ve, 1)])
        for pc in range(4):
            for tt in range(2):
                bank = self.pb[(pc * 2 + tt) % 2]
                srcT = self.klf32[:, pc * 256 + tt * 128: pc * 256 + (tt + 1) * 128]
                ptok = P.emit("tensor", lambda e, s=srcT, bank=bank: e.transpose(
                    bank[:, 0:128], s, self.cst[:, C_IDENT]),
                    waits=[(ve, ve.count), (sc, sc.count)], incs=[(pe, 1)])
                dst = 512 * tt + pc * 128
                stok = P.emit("scalar", lambda e, bank=bank, dst=dst: e.activation(
                    out=self.scr1[:, dst:dst + 128], in_=bank[:, 0:128],
                    func=AF.Identity), waits=[(pe, ptok), (ve, kbtok)], incs=[(sc, 1)])
                self.split("vector", bank[:, 0:128],
                           self.kltm_hi[:, dst:dst + 128],
                           self.kltm_lo[:, dst:dst + 128],
                           waits=[(pe, ptok), (sc, stok)], incs=[(ve, 1)])
        for pc in range(4):
            for tt in range(2):
                bank = self.pb[(pc * 2 + tt) % 2 + 2]
                srcT = self.gl32[:, pc * 256 + tt * 128: pc * 256 + (tt + 1) * 128]
                ptok = P.emit("tensor", lambda e, s=srcT, bank=bank: e.transpose(
                    bank[:, 0:128], s, self.cst[:, C_IDENT]),
                    waits=[(ve, ve.count), (sc, sc.count)], incs=[(pe, 1)])
                dst = 512 * tt + pc * 128
                P.emit("scalar", lambda e, bank=bank, dst=dst: e.activation(
                    out=self.gltm[:, dst:dst + 128], in_=bank[:, 0:128],
                    func=AF.Identity), waits=[(pe, ptok)], incs=[(sc, 1)])
        P.emit("vector", lambda e: e.scalar_tensor_tensor(
            out=self.klf32, in0=self.scr1, scalar=1.0 / 16.0, in1=self.gltm,
            op0=ALU.mult, op1=ALU.mult),
            waits=[(sc, sc.count), (ve, ve.count), (pe, pe.count)])
        self.split("vector", self.klf32, self.kbtm_hi, self.kbtm_lo, incs=[(ve, 1)])
        P.emit("vector", lambda e: e.scalar_tensor_tensor(
            out=self.k_raw, in0=self.v_raw, scalar=1.0 / 16.0, in1=self.gltm,
            op0=ALU.mult, op1=ALU.mult))
        self.split("vector", self.k_raw, self.vbtm_hi, self.vbtm_lo, incs=[(ve, 1)])
        stt = P.sem("st")
        for z in range(2):
            hi, lo = self.klfz[z]
            P.emit("gpsimd", lambda e, hs=hs, z=z, hi=hi: e.dma_start(
                out=self.klf_scr.ap()[:, hs * 4096 + z * 2048: hs * 4096 + z * 2048 + 1024],
                in_=hi), waits=[(ve, ve.count)], incs=[(stt, 16)])
            P.emit("gpsimd", lambda e, hs=hs, z=z, lo=lo: e.dma_start(
                out=self.klf_scr.ap()[:, hs * 4096 + z * 2048 + 1024: hs * 4096 + (z + 1) * 2048],
                in_=lo), incs=[(stt, 16)])

    def st_delta(self, hs):
        for tt in range(2):
            self.delta_solve(hs, tt)
            for c in range(2):
                self.delta_scan(hs, tt, c)

    def delta_solve(self, hs, tt):
        P = self.P
        ve, pe = P.sem("ve"), P.sem("pe")
        bL, bY, bSq, bWT = self.pb[0], self.pb[1], self.pb[2], self.pb[3]
        P.emit("vector", lambda e: e.memset(self.slvP[:], 0.0),
               waits=[(pe, pe.count)])
        P.emit("vector", lambda e: e.memset(self.slvPT[:], 0.0))
        P.emit("vector", lambda e: e.memset(self.slvP2[:], 0.0))
        P.emit("vector", lambda e: e.memset(self.slvPT2[:], 0.0), incs=[(ve, 1)])
        for kv in range(8):
            z = kv % 2
            base = z * 64
            pc = kv // 2
            colc = [pc * 256 + tt * 128 + c * 64 for c in range(2)]
            klz_hi, klz_lo = self.klfz[z]
            for c in range(2):
                kb = self.kbf_hi[:, colc[c]:colc[c] + 64]
                kl = klz_hi[:, colc[c]:colc[c] + 64]
                P.emit("tensor", lambda e, kb=kb, kl=kl, c=c: e.matmul(
                    bL[c * 64:(c + 1) * 64, 0:64], kb, kl, start=True, stop=True),
                    waits=[(ve, ve.count)])
                P.emit("tensor", lambda e, kb=kb, kl=kl, c=c: e.matmul(
                    bL[c * 64:(c + 1) * 64, 64:128], kl, kb, start=True, stop=True),
                    incs=[(pe, 1)])
            ltok = pe.count
            P.emit("vector", lambda e: e.scalar_tensor_tensor(
                out=self.slvP[0:64, 0:64], in0=bL[0:64, 0:64], scalar=-1.0,
                in1=self.cst[0:64, C_STRICTL2], op0=ALU.mult, op1=ALU.mult),
                waits=[(pe, ltok)])
            P.emit("vector", lambda e: e.scalar_tensor_tensor(
                out=self.slvP[64:128, 64:128], in0=bL[64:128, 0:64], scalar=-1.0,
                in1=self.cst[64:128, C_STRICTL2], op0=ALU.mult, op1=ALU.mult))
            P.emit("vector", lambda e: e.scalar_tensor_tensor(
                out=self.slvPT[0:64, 0:64], in0=bL[0:64, 64:128], scalar=-1.0,
                in1=self.cst[0:64, C_STRICTT2], op0=ALU.mult, op1=ALU.mult))
            vtok = P.emit("vector", lambda e: e.scalar_tensor_tensor(
                out=self.slvPT[64:128, 64:128], in0=bL[64:128, 64:128], scalar=-1.0,
                in1=self.cst[64:128, C_STRICTT2], op0=ALU.mult, op1=ALU.mult),
                incs=[(ve, 1)])
            ucol = 64 if kv % 2 == 0 else 0
            wcol = 64 - ucol
            y32 = self.uw_f[:, kv * 128: kv * 128 + 128]
            vb_h = self.vbtm_hi[:, tt * 512 + kv * 64: tt * 512 + (kv + 1) * 64]
            vb_l = self.vbtm_lo[:, tt * 512 + kv * 64: tt * 512 + (kv + 1) * 64]
            kb_h = self.kbtm_hi[:, tt * 512 + kv * 64: tt * 512 + (kv + 1) * 64]
            kb_l = self.kbtm_lo[:, tt * 512 + kv * 64: tt * 512 + (kv + 1) * 64]
            P.emit("vector", lambda e, y=y32, u=ucol, h=vb_h, l=vb_l: e.tensor_add(
                out=y[:, u:u + 64], in0=h, in1=l), waits=[(ve, ve.count)])
            P.emit("vector", lambda e, y=y32, w=wcol, h=kb_h, l=kb_l: e.tensor_add(
                out=y[:, w:w + 64], in0=h, in1=l), incs=[(ve, 1)])
            cur = (self.slvP, self.slvPT)
            nxt = (self.slvP2, self.slvPT2)
            for k in range(6):
                ytok = P.emit("vector", lambda e, y=y32: e.tensor_copy(
                    out=self.ybf, in_=y), waits=[(pe, pe.count), (ve, ve.count)],
                    incs=[(ve, 1)])
                atok = P.emit("tensor", lambda e, cur=cur: e.matmul(
                    bY[:, 0:128], cur[1], self.ybf, start=True, stop=True),
                    waits=[(ve, ytok)], incs=[(pe, 1)])
                P.emit("vector", lambda e, y=y32: e.tensor_add(
                    out=y, in0=y, in1=bY[:, 0:128]), waits=[(pe, atok)],
                    incs=[(ve, 1)])
                if k < 5:
                    P.emit("tensor", lambda e, cur=cur: e.matmul(
                        bSq[:, 0:128], cur[1], cur[0], start=True, stop=True))
                    sqt = P.emit("tensor", lambda e, cur=cur: e.matmul(
                        bSq[:, 128:256], cur[0], cur[1], start=True, stop=True),
                        incs=[(pe, 1)])
                    P.emit("vector", lambda e, nxt=nxt: e.tensor_copy(
                        out=nxt[0], in_=bSq[:, 0:128]), waits=[(pe, sqt)])
                    P.emit("vector", lambda e, nxt=nxt: e.tensor_copy(
                        out=nxt[1], in_=bSq[:, 128:256]), incs=[(ve, 1)])
                    cur, nxt = nxt, cur
            uwtok = P.emit("vector", lambda e: None, incs=[(ve, 1)])
            srcT = self.uw_f[:, kv * 128: kv * 128 + 128]
            wtok = P.emit("tensor", lambda e, s=srcT: e.transpose(
                bWT[:, 0:128], s, self.cst[:, C_IDENT]),
                waits=[(ve, uwtok)], incs=[(pe, 1)])
            rows = slice(0, 64) if kv % 2 == 0 else slice(64, 128)
            wsl = slice((kv // 2) * 128, (kv // 2) * 128 + 128)
            P.emit("vector", lambda e, rows=rows, wsl=wsl: e.tensor_scalar_mul(
                out=self.wTs_hi[rows, wsl], in0=bWT[rows, 0:128], scalar1=-1.0),
                waits=[(pe, wtok)])
            P.emit("vector", lambda e, rows=rows, wsl=wsl: e.scalar_tensor_tensor(
                out=self.wTs_lo[rows, wsl], in0=bWT[rows, 0:128],
                scalar=-1.0, in1=self.wTs_hi[rows, wsl],
                op0=ALU.mult, op1=ALU.subtract), incs=[(ve, 1)])

    def delta_scan(self, hs, tt, c):
        P = self.P
        ve, pe, sc = P.sem("ve"), P.sem("pe"), P.sem("sc")
        stt = P.sem("st")
        bUP, bS = self.pb[4], self.pb[5]
        cp = slice(c * 64, (c + 1) * 64)
        op = slice(64 - c * 64, 128 - c * 64)
        gc = hs * 4 + tt * 2 + c
        def s32ap(rows):
            return bass.AP(self.fp, rows * 13568 + 11776 + 1536,
                           [[13568, 64], [64, 4], [1, 64]])

        def sdap(t, rows, coff):
            return bass.AP(self.bp1, rows * 49152 + t + coff,
                           [[49152, 64], [128, 4], [1, 64]])
        P.emit("vector", lambda e: e.memset(self.Sd_hi[:], 0.0),
               waits=[(pe, pe.count)])
        P.emit("vector", lambda e: e.memset(self.Sd_lo[:], 0.0))
        for rows, coff in ((0, 0), (64, 64)):
            P.emit("vector", lambda e, rows=rows, coff=coff: e.tensor_copy(
                out=sdap(38912, rows, coff), in_=s32ap(rows)))
            P.emit("vector", lambda e, rows=rows, coff=coff: e.tensor_tensor(
                out=sdap(39424, rows, coff), in0=s32ap(rows),
                in1=sdap(38912, rows, coff), op=ALU.subtract), incs=[(ve, 1)])
        P.emit("gpsimd", lambda e, gc=gc: e.dma_start(
            out=self.ssnap_scr.ap()[:, gc * 1024: gc * 1024 + 512], in_=self.Sd_hi),
            waits=[(ve, ve.count)], incs=[(stt, 16)])
        P.emit("gpsimd", lambda e, gc=gc: e.dma_start(
            out=self.ssnap_scr.ap()[:, gc * 1024 + 512: (gc + 1) * 1024],
            in_=self.Sd_lo), incs=[(stt, 16)])
        P.emit("vector", lambda e: e.memset(self.ups_hi[op, :], 0.0),
               waits=[(pe, pe.count)])
        P.emit("vector", lambda e: e.memset(self.ups_lo[op, :], 0.0), incs=[(ve, 1)])
        for m in range(4):
            wh = self.wTs_hi[:, m * 128 + c * 64: m * 128 + (c + 1) * 64]
            wl = self.wTs_lo[:, m * 128 + c * 64: m * 128 + (c + 1) * 64]
            sh = self.Sd_hi[:, m * 128:(m + 1) * 128]
            sl = self.Sd_lo[:, m * 128:(m + 1) * 128]
            mtok = self.mm3(bUP[cp, m * 128:(m + 1) * 128], wh, wl, sh, sl,
                            start=True, stop=True,
                            waits=[(ve, ve.count)], incs=[(pe, 1)])
            for z in range(2):
                kv = 2 * m + z
                ucol = 64 if z == 0 else 0
                up32 = self.scr1[cp, kv * 64:(kv + 1) * 64]
                u_sl = self.uw_f[cp, kv * 128 + ucol: kv * 128 + ucol + 64]
                P.emit("vector", lambda e, up32=up32, u=u_sl, m=m, z=z: e.tensor_add(
                    out=up32, in0=u,
                    in1=bUP[cp, m * 128 + z * 64: m * 128 + (z + 1) * 64]),
                    waits=[(pe, mtok)])
                self.split("vector", up32,
                           self.ups_hi[cp, kv * 64:(kv + 1) * 64],
                           self.ups_lo[cp, kv * 64:(kv + 1) * 64], incs=[(ve, 1)])
        P.emit("gpsimd", lambda e, gc=gc: e.dma_start(
            out=self.ups_scr.ap()[:, gc * 1024: gc * 1024 + 512], in_=self.ups_hi),
            waits=[(ve, ve.count)], incs=[(stt, 16)])
        P.emit("gpsimd", lambda e, gc=gc: e.dma_start(
            out=self.ups_scr.ap()[:, gc * 1024 + 512: (gc + 1) * 1024],
            in_=self.ups_lo), incs=[(stt, 16)])
        for kv in range(8):
            base = (kv % 2) * 64
            pc = kv // 2
            kcol = slice(tt * 512 + pc * 128 + (kv % 2) * 64,
                         tt * 512 + pc * 128 + (kv % 2) * 64 + 64)
            kh = self.kltm_hi[:, kcol]
            kl = self.kltm_lo[:, kcol]
            uh = self.ups_hi[:, kv * 64:(kv + 1) * 64]
            ul = self.ups_lo[:, kv * 64:(kv + 1) * 64]
            self.mm3(bS[slice(base, base + 64), (kv // 2) * 64:(kv // 2 + 1) * 64],
                     kh, kl, uh, ul, start=True, stop=True,
                     waits=[(ve, ve.count), (stt, stt.count)],
                     incs=[(pe, 1)] if kv == 7 else ())
        stok = P.sem("pe").count
        P.emit("vector", lambda e: e.tensor_add(
            out=self.S32[:], in0=self.S32[:], in1=bS[:, 0:256]),
            waits=[(pe, stok)], incs=[(ve, 1)])

    def st_q(self, hs):
        P = self.P
        ld, ve, pe, sc = P.sem("ld"), P.sem("ve"), P.sem("pe"), P.sem("sc")
        for qh in range(2):
            last_mm = [pe.count, pe.count]
            last_split = [0, 0]
            qtok = {}
            for dc in range(NC):
                bi = dc % 2
                sbuf = self.wstrip[bi]
                sph = self.wsp_hi[:, bi * 1024:(bi + 1) * 1024]
                spl = self.wsp_lo_x[:, bi * 1024:(bi + 1) * 1024]
                tok_ld = P.emit("sync", lambda e, b=sbuf, dc=dc, qh=qh: e.dma_start(
                    out=b, in_=bass.AP(self.wq_in, dc * 128 * DM + qh * 1024,
                                       [[DM, 128], [1, 1024]])),
                    waits=[(ve, last_split[bi]), (pe, last_mm[bi])],
                    incs=[(ld, 16)])
                vtok = self.split("vector", sbuf, sph, spl,
                                  waits=[(ld, tok_ld), (pe, last_mm[bi])],
                                  incs=[(ve, 1)])
                last_split[bi] = vtok
                for qi in range(8):
                    qc = qh * 8 + qi
                    lh = sph[:, qi * 128:(qi + 1) * 128]
                    ll = spl[:, qi * 128:(qi + 1) * 128]
                    rh = self.xs_hi[:, dc * HS:(dc + 1) * HS]
                    rl = self.xs_lo[:, dc * HS:(dc + 1) * HS]
                    tok = self.mm3(self.pb[qi][:, 0:256], lh, ll, rh, rl,
                                   start=(dc == 0), stop=(dc == NC - 1),
                                   waits=[(ve, vtok)], incs=[(pe, 1)])
                    if dc == NC - 1:
                        qtok[qc] = tok
                last_mm[bi] = tok
            for qi in range(8):
                qc = qh * 8 + qi
                bank = self.pb[qi][:, 0:256]
                w = [(pe, qtok[qc])]
                co = qc * HS
                stok = P.emit("scalar", lambda e, b=bank, qi=qi: e.activation(
                    out=self.eqt[:, qi * 256:(qi + 1) * 256], in_=b, func=AF.Exp),
                    waits=w, incs=[(sc, 1)])
                self.split("vector", bank, self.q_hi[:, co:co + 256],
                           self.q_lo[:, co:co + 256], waits=w + [(sc, stok)],
                           incs=[(ve, 1)])
            for qi in (0, 2, 4, 6):
                src = self.eqt[:, qi * 256:(qi + 2) * 256]
                bb, btok = self.bcast_recip(src, waits=[(sc, sc.count),
                                                        (ve, ve.count)], w=512)
                P.emit("vector", lambda e, s=src, bb=bb: e.tensor_tensor(
                    out=s, in0=s, in1=bb, op=ALU.mult), waits=[(pe, btok)])
                for q2 in range(2):
                    qc = qh * 8 + qi + q2
                    sap = bass.AP(self.fp, 4096 + (qi + q2) * 256,
                                  [[13568, 128], [64, 4], [1, 64]])
                    hap = bass.AP(self.bp2, self.eqb + qc * 64,
                                  [[22912, 128], [1024, 4], [1, 64]])
                    lap = bass.AP(self.bp2, self.eqb + 4096 + qc * 64,
                                  [[22912, 128], [1024, 4], [1, 64]])
                    P.emit("vector", lambda e, s=sap, h=hap: e.tensor_copy(
                        out=h, in_=s))
                    P.emit("vector", lambda e, s=sap, h=hap, l=lap: e.tensor_tensor(
                        out=l, in0=s, in1=h, op=ALU.subtract), incs=[(ve, 1)])

    MASK_BASE = (0, 6, 14, 28)
    NJT = ((6, 8), (14, 16))

    def st_std(self, oi, slot, h2):
        P = self.P
        ld, ve, pe, sc = P.sem("ld"), P.sem("ve"), P.sem("pe"), P.sem("sc")
        hzv = P.sem("hzv")
        njt = self.NJT[slot][h2]
        mbase = self.MASK_BASE[oi]
        bSc = [self.pb[0], self.pb[1]]
        bO = self.pb[2]
        bRs = [self.pb[3], self.pb[4]]
        bBc = self.pb[5]
        mld = 0
        for jt in range(njt):
            mld = P.emit("sync", lambda e, jt=jt: e.dma_start(
                out=self.bp2[:, self.mkb + jt * 256:self.mkb + (jt + 1) * 256],
                in_=self.masks_in.ap()[:, (mbase + jt) * 256:(mbase + jt + 1) * 256]),
                waits=[(pe, pe.count), (ve, ve.count)] if jt == 0 else (),
                incs=[(ld, 16)])
        PtA = (self.bp1[:, 32768:33280], self.bp1[:, 33280:33792])
        PtB = (self.bp1[:, 33792:34304], self.bp1[:, 34304:34816])
        Pt = (PtA, PtB)
        pexp_h = (self.pexp, self.scr1[:, 0:512])
        t1 = self.uw_f[:, 0:512]
        for qp in range(8):
            kvA = 2 * (qp // 2)
            otok = [None, None]
            rstok = [None, None]
            sp_prev = 0
            pv_prev = 0
            qsl = slice(2 * qp * HS, (2 * qp + 2) * HS)
            for jt in range(njt):
                stok = [0, 0]
                for hi_ in range(2):
                    kv = kvA + hi_
                    base = (kv % 2) * 64
                    rows = slice(base, base + 64)
                    pc = kv // 2
                    kc = slice(pc * 2048 + jt * 128, pc * 2048 + (jt + 1) * 128)
                    tok = self.mm3(bSc[hi_][:, 0:512], self.k_hi[rows, kc],
                                   self.k_lo[rows, kc],
                                   self.q_hi[rows, qsl], self.q_lo[rows, qsl],
                                   start=True, stop=True,
                                   tile_position=(base, 0),
                                   waits=[(ve, ve.count), (sc, sc.count)],
                                   incs=[(pe, 1)])
                    stok[hi_] = P.emit("scalar", lambda e, hi_=hi_: e.activation(
                        out=pexp_h[hi_], in_=bSc[hi_][:, 0:512],
                        func=AF.Exp, scale=0.125),
                        waits=[(pe, tok), (ve, sp_prev)], incs=[(sc, 1)])
                mask_ap = bass.AP(self.bp2, self.mkb + jt * 256,
                                  [[22912, 128], [0, 2], [1, 256]])
                for hi_ in range(2):
                    P.emit("vector", lambda e, hi_=hi_, m=mask_ap: e.tensor_tensor(
                        out=pexp_h[hi_], in0=pexp_h[hi_], in1=m, op=ALU.mult),
                        waits=[(sc, stok[hi_]), (pe, pv_prev), (ld, mld)],
                        incs=[(ve, 1)])
                    sp_prev = self.split("vector", pexp_h[hi_],
                                         Pt[hi_][0], Pt[hi_][1], incs=[(ve, 1)])
                for hi_ in range(2):
                    P.emit("tensor", lambda e, hi_=hi_, jt=jt: e.matmul(
                        bRs[hi_][0:1, 0:512], self.onesb[:, 0:1],
                        Pt[hi_][0], start=(jt == 0), stop=False),
                        waits=[(ve, sp_prev)])
                    tok = P.emit("tensor", lambda e, hi_=hi_, jt=jt: e.matmul(
                        bRs[hi_][0:1, 0:512], self.onesb[:, 0:1],
                        Pt[hi_][1], start=False, stop=(jt == njt - 1)),
                        incs=[(pe, 1)] if jt == njt - 1 else ())
                    if jt == njt - 1:
                        rstok[hi_] = tok
                    kv = kvA + hi_
                    base = (kv % 2) * 64
                    vc = slice(jt * 512 + kv * 64, jt * 512 + (kv + 1) * 64)
                    pv_prev = self.mm3(bO[slice(base, base + 64), 0:512],
                                       self.v_hi[:, vc], self.v_lo[:, vc],
                                       Pt[hi_][0], Pt[hi_][1],
                                       start=(jt == 0), stop=(jt == njt - 1),
                                       incs=[(pe, 1)])
                    if jt == njt - 1:
                        otok[hi_] = pv_prev
            v1 = P.emit("vector", lambda e: e.reciprocal(
                out=self.misc[0:1, 0:512], in_=bRs[0][0:1, 0:512]),
                waits=[(pe, rstok[0])], incs=[(hzv, 1)])
            v2 = P.emit("vector", lambda e: e.reciprocal(
                out=self.misc[0:1, 512:1024], in_=bRs[1][0:1, 0:512]),
                waits=[(pe, rstok[1]), (hzv, v1)], incs=[(hzv, 1)])
            P.emit("tensor", lambda e: e.matmul(
                bBc[0:64, 0:512], self.cst[:, C_E0], self.misc[:, 0:512],
                start=True, stop=True), waits=[(hzv, v2)])
            btok = P.emit("tensor", lambda e: e.matmul(
                bBc[64:128, 0:512], self.cst[:, C_E0], self.misc[:, 512:1024],
                start=True, stop=True), incs=[(pe, 1)])
            P.emit("vector", lambda e, t1=t1: e.tensor_copy(
                out=t1, in_=bBc[:, 0:512]), waits=[(pe, btok)])
            P.emit("vector", lambda e, t1=t1: e.tensor_tensor(
                out=t1, in0=bO[:, 0:512], in1=t1, op=ALU.mult),
                waits=[(pe, otok[0]), (pe, otok[1])])
            osl = slice(2 * qp * HS, (2 * qp + 2) * HS)
            P.emit("vector", lambda e, osl=osl, t1=t1: e.scalar_tensor_tensor(
                out=self.o_comb[:, osl], in0=t1, scalar=0.5,
                in1=self.o_comb[:, osl], op0=ALU.mult, op1=ALU.add),
                incs=[(ve, 1)])

    def st_wo(self, oi):
        P = self.P
        ld, ve, pe, sc = P.sem("ld"), P.sem("ve"), P.sem("pe"), P.sem("sc")
        stt = P.sem("st")
        for i in range(4):
            self.split("vector", self.o_comb[:, i * 1024:(i + 1) * 1024],
                       self.oc_hi[:, i * 1024:(i + 1) * 1024],
                       self.oc_lo[:, i * 1024:(i + 1) * 1024],
                       waits=[(ve, ve.count)], incs=[(ve, 1)])
        for wv in range(2):
            wtok = {}
            last_mm = [pe.count, pe.count]
            last_split = [0, 0]
            for ic in range(NC):
                bi = ic % 2
                sbuf = self.wstrip[bi]
                sph = self.wsp_hi[:, bi * 1024:(bi + 1) * 1024]
                spl = self.wsp_lo_x[:, bi * 1024:(bi + 1) * 1024]
                tok_ld = P.emit("sync", lambda e, b=sbuf, ic=ic, wv=wv: e.dma_start(
                    out=b, in_=bass.AP(self.wo_in, ic * 128 * DM + wv * 1024,
                                       [[DM, 128], [1, 1024]])),
                    waits=[(ve, last_split[bi]), (pe, last_mm[bi])],
                    incs=[(ld, 16)])
                vtok = self.split("vector", sbuf, sph, spl,
                                  waits=[(ld, tok_ld), (pe, last_mm[bi])],
                                  incs=[(ve, 1)])
                last_split[bi] = vtok
                for oc in range(8):
                    dchunk = wv * 8 + oc
                    lh = sph[:, oc * 128:(oc + 1) * 128]
                    ll = spl[:, oc * 128:(oc + 1) * 128]
                    rh = self.oc_hi[:, ic * 256:(ic + 1) * 256]
                    rl = self.oc_lo[:, ic * 256:(ic + 1) * 256]
                    tok = self.mm3(self.pb[oc][:, 0:256], lh, ll, rh, rl,
                                   start=(ic == 0), stop=(ic == NC - 1),
                                   waits=[(ve, vtok)], incs=[(pe, 1)])
                    if ic == NC - 1:
                        wtok[dchunk] = tok
                last_mm[bi] = tok
            for oc in range(8):
                dchunk = wv * 8 + oc
                bank = self.pb[oc][:, 0:256]
                sl = oc % 2
                vtok = P.emit("vector", lambda e, b=bank, sl=sl: e.tensor_copy(
                    out=self.att_st[:, sl * 256:(sl + 1) * 256], in_=b),
                    waits=[(pe, wtok[dchunk]), (stt, max(0, stt.count - 16))],
                    incs=[(ve, 1)])
                off = oi * NC * HS + dchunk * HS
                P.emit("gpsimd", lambda e, sl=sl, off=off: e.dma_start(
                    out=self.attn_scr.ap()[:, off:off + HS],
                    in_=self.att_st[:, sl * 256:(sl + 1) * 256]),
                    waits=[(ve, vtok)], incs=[(stt, 16)])

    def st_slot(self, slot):
        P = self.P
        ve, pe = P.sem("ve"), P.sem("pe")
        for h2 in range(2):
            oi = slot * 2 + h2
            for t in range(2):
                self.ln_tile(self.xg_in, oi * HS + t * 128, t)
            P.emit("vector", lambda e: e.memset(self.o_comb[:], 0.0),
                   waits=[(pe, pe.count)], incs=[(ve, 1)])
            self.st_q(oi)
            self.slot_delta_o(oi)
            self.st_std(oi, slot, h2)
            self.st_wo(oi)

    def slot_delta_o(self, oi):
        P = self.P
        ld, ve, pe, sc = P.sem("ld"), P.sem("ve"), P.sem("pe"), P.sem("sc")
        stt = P.sem("st")
        bA, bO = self.pb[6], self.pb[7]

        def dyn_dma(dst, scr, item, width):
            def fn(e, dst=dst, scr=scr, item=item, width=width):
                v = e.value_load(self.gofs_sb[0:1, item:item + 1])
                inst = e.dma_start(out=dst, in_=bass.AP(scr, v,
                                                        [[scr.shape[1], 128],
                                                         [1, width]]))
                self.nc.free_register(v.val)
                return inst
            return fn
        ktok = P.emit("sync", dyn_dma(self.bp1[:, 32768:36864], self.klf_scr,
                                      oi * 16 + 8, 4096),
                      waits=[(stt, stt.count), (ve, ve.count), (pe, pe.count)],
                      incs=[(ld, 16)])
        for tt in range(2):
            for c in range(2):
                cp = slice(c * 64, (c + 1) * 64)
                ch = tt * 2 + c
                it = oi * 16 + tt * 2 + c
                stok_ = P.emit("sync", dyn_dma(self.bp1[:, 38912:39936],
                                               self.ssnap_scr, it, 1024),
                               waits=[(ve, ve.count)], incs=[(ld, 16)])
                utok = P.emit("sync", dyn_dma(self.bp1[:, 37888:38912],
                                              self.ups_scr, it + 4, 1024),
                              incs=[(ld, 16)])
                for m in range(4):
                    eqo = ch * 1024 + 4 * m * 64
                    eh = self.bp2[:, self.eqb + eqo:self.eqb + eqo + 256]
                    el = self.bp2[:, self.eqb + 4096 + eqo:
                                  self.eqb + 4096 + eqo + 256]
                    for z in range(2):
                        kv = 2 * m + z
                        base = z * 64
                        kcol = slice(m * 256 + tt * 128 + c * 64,
                                     m * 256 + tt * 128 + (c + 1) * 64)
                        klh, kll = self.klfz[z][0][:, kcol], self.klfz[z][1][:, kcol]
                        atok = self.mm3(bA[cp, 0:256], klh, kll, eh, el,
                                        start=True, stop=True,
                                        waits=[(ve, ve.count),
                                               (ld, max(ktok, utok))],
                                        incs=[(pe, 1)])
                        cmask = bass.AP(self.cst, cp.start * CONST_W + 384,
                                        [[CONST_W, 64], [0, 4], [1, 64]])
                        P.emit("vector", lambda e, cp=cp, cm=cmask: e.tensor_tensor(
                            out=self.scr1[cp, 0:256], in0=bA[cp, 0:256],
                            in1=cm, op=ALU.mult),
                            waits=[(pe, atok)])
                        attok = self.split("vector", self.scr1[cp, 0:256],
                                           self.bp1[cp, 39936:40192],
                                           self.bp1[cp, 40192:40448],
                                           incs=[(ve, 1)])
                        sh = self.bp1[:, 38912 + m * 128 + z * 64:
                                      38912 + m * 128 + z * 64 + 64]
                        sl = self.bp1[:, 39424 + m * 128 + z * 64:
                                      39424 + m * 128 + z * 64 + 64]
                        self.mm3(bO[slice(base, base + 64), 0:256], sh, sl, eh, el,
                                 start=True, stop=False)
                        uh = self.ups_hi[:, kv * 64:(kv + 1) * 64]
                        ul = self.ups_lo[:, kv * 64:(kv + 1) * 64]
                        otok = self.mm3(bO[slice(base, base + 64), 0:256], uh, ul,
                                        self.bp1[:, 39936:40192],
                                        self.bp1[:, 40192:40448],
                                        start=False, stop=True,
                                        waits=[(ve, attok)], incs=[(pe, 1)])
                    oap = bass.AP(self.fp, 4 * m * HS + tt * 128 + c * 64,
                                  [[13568, 128], [HS, 4], [1, 64]])
                    bOap = bass.AP(self.pb[7], 0, [[2048 // 4, 128], [64, 4], [1, 64]])
                    P.emit("vector", lambda e, oap=oap, bOap=bOap:
                           e.scalar_tensor_tensor(
                        out=oap, in0=bOap, scalar=0.5,
                        in1=oap, op0=ALU.mult, op1=ALU.add),
                        waits=[(pe, pe.count)], incs=[(ve, 1)])

    def st_ffn(self, slot):
        P = self.P
        ld, ve, pe, sc = P.sem("ld"), P.sem("ve"), P.sem("pe"), P.sem("sc")
        gld, stt = P.sem("gld"), P.sem("st")
        f = self.fp
        outstage = [f[:, t * 2048:(t + 1) * 2048] for t in range(4)]
        gw_f32 = f[:, 6144:8192]
        dw_f32 = f[:, 10240:11264]
        uw_f32 = f[:, 8192:10240]
        gate32 = f[:, 4096:4608]
        fuse32 = self.misc[:, 0:512]
        att_re = self.misc[:, 512:1024]
        xnr_hi = self.bp2[:, 0:8192]
        xnr_lo = self.bp2[:, 8192:16384]
        ffw_g = self.bp2[:, 16384:18432]
        ffw_u = self.bp2[:, 18432:20480]
        ffw_d = self.bp2[:, 20480:21504]
        oi0 = slot * 2
        for t in range(4):
            self.ln_tile(self.xg_in, slot * SEG + t * 128, t,
                         dst_hi=xnr_hi, dst_lo=xnr_lo, cstride=SEG)
        ic_prev = [0, 0, 0]
        tok = 0
        for ic in range(NIC):
            c1 = P.emit("gpsimd", lambda e, ic=ic: e.dma_start(
                out=self.w8[:, 0:2048].bitcast(F8),
                in_=bass.AP(self.gw_in, ic * 128 * 2048,
                            [[2048, 128], [1, 2048]])),
                waits=[(gld, ic_prev[1])], incs=[(gld, 16)])
            c2 = P.emit("gpsimd", lambda e, ic=ic: e.dma_start(
                out=self.w8[:, 2048:4096].bitcast(F8),
                in_=bass.AP(self.uw_in, ic * 128 * 2048,
                            [[2048, 128], [1, 2048]])),
                incs=[(gld, 16)])
            w1 = P.emit("gpsimd", lambda e: e.dma_start(
                out=ffw_g, in_=self.w8[:, 0:2048].bitcast(F8)),
                waits=[(gld, c1), (pe, ic_prev[2])], incs=[(gld, 16)])
            w2 = P.emit("gpsimd", lambda e: e.dma_start(
                out=ffw_u, in_=self.w8[:, 2048:4096].bitcast(F8)),
                waits=[(gld, c2)], incs=[(gld, 16)])
            ic_prev[1] = w2
            for dc in range(NC):
                self.mm3(self.pb[0][:], ffw_g[:, dc * 128:(dc + 1) * 128], None,
                         xnr_hi[:, dc * 512:(dc + 1) * 512],
                         xnr_lo[:, dc * 512:(dc + 1) * 512],
                         start=(dc == 0), stop=(dc == NC - 1),
                         waits=[(gld, w2), (sc, sc.count), (ve, ve.count)]
                         if dc == 0 else (),
                         incs=[(pe, 1)] if dc == NC - 1 else (), passes=2)
                tok = self.mm3(self.pb[1][:], ffw_u[:, dc * 128:(dc + 1) * 128],
                               None, xnr_hi[:, dc * 512:(dc + 1) * 512],
                               xnr_lo[:, dc * 512:(dc + 1) * 512],
                               start=(dc == 0), stop=(dc == NC - 1),
                               waits=[(gld, w2), (ve, ve.count)] if dc == 0
                               else (),
                               incs=[(pe, 1)] if dc == NC - 1 else (), passes=2)
            ic_prev[2] = tok
            gtok = P.emit("scalar", lambda e, ic=ic: e.activation(
                out=gate32, in_=self.pb[0][:], func=AF.Silu,
                scale=self.scl[:, ic:ic + 1]), waits=[(pe, tok)], incs=[(sc, 1)])
            P.emit("vector", lambda e, ic=ic: e.scalar_tensor_tensor(
                out=fuse32, in0=self.pb[1][:], scalar=self.scl[:, 48 + ic:49 + ic],
                in1=gate32, op0=ALU.mult, op1=ALU.mult),
                waits=[(sc, gtok), (ve, ve.count)])
            self.split("vector", fuse32,
                       self.fuse_hi[:, ic * 512:(ic + 1) * 512],
                       self.fuse_lo[:, ic * 512:(ic + 1) * 512], incs=[(ve, 1)])
        ostok = {}
        for t in range(4):
            row0 = slot * SEG + t * 128
            ostok[t] = P.emit("sync", lambda e, t=t, o=row0: e.dma_start(
                out=outstage[t], in_=self.xg_in.ap()[o:o + 128, :]),
                waits=[(pe, pe.count), (ve, ve.count), (sc, sc.count),
                       (stt, stt.count)],
                incs=[(ld, 16)])
        for dh in range(2):
            dtok = {}
            dprev = [P.sem("gld").count, P.sem("pe").count]
            for ic in range(NIC):
                c1 = P.emit("gpsimd", lambda e, ic=ic, dh=dh: e.dma_start(
                    out=self.w8[:, 0:1024].bitcast(F8),
                    in_=bass.AP(self.dw_in, (dh * NIC + ic) * 128 * 1024,
                                [[1024, 128], [1, 1024]])),
                    waits=[(gld, dprev[0])], incs=[(gld, 16)])
                w1 = P.emit("gpsimd", lambda e: e.dma_start(
                    out=ffw_d, in_=self.w8[:, 0:1024].bitcast(F8)),
                    waits=[(gld, c1), (pe, dprev[1])], incs=[(gld, 16)])
                dprev[0] = w1
                for od in range(8):
                    tok = self.mm3(self.pb[od][:],
                                   ffw_d[:, od * 128:(od + 1) * 128], None,
                                   self.fuse_hi[:, ic * 512:(ic + 1) * 512],
                                   self.fuse_lo[:, ic * 512:(ic + 1) * 512],
                                   start=(ic == 0), stop=(ic == NIC - 1),
                                   waits=[(gld, w1), (ve, ve.count)]
                                   if ic == 0 else ([(gld, w1)] if od == 0 else ()),
                                   incs=[(pe, 1)] if (ic == NIC - 1 or od == 7)
                                   else (), passes=2)
                    if ic == NIC - 1:
                        dtok[od] = tok
                dprev[1] = tok
            for od in range(8):
                dchunk = dh * 8 + od
                atok = P.emit("sync", lambda e, dchunk=dchunk: e.dma_start(
                    out=att_re, in_=bass.AP(self.attn_scr,
                                            oi0 * 4096 + dchunk * 256,
                                            [[16384, 128], [4096, 2], [1, 256]])),
                    waits=[(ve, ve.count)], incs=[(ld, 16)])
                ttok = P.emit("vector", lambda e, od=od, dchunk=dchunk:
                              e.scalar_tensor_tensor(
                    out=self.stage[:, 0:512], in0=self.pb[od][:],
                    scalar=self.scl[:, 96 + dchunk:97 + dchunk], in1=att_re,
                    op0=ALU.mult, op1=ALU.add),
                    waits=[(pe, dtok[od]), (ld, atok)], incs=[(ve, 1)])
                for t in range(4):
                    ptok = P.emit("tensor", lambda e, od=od, t=t: e.transpose(
                        self.pb[od][:, 0:128],
                        self.stage[:, t * 128:(t + 1) * 128],
                        self.cst[:, C_IDENT]),
                        waits=[(ve, ve.count), (pe, pe.count)], incs=[(pe, 1)])
                    P.emit("vector", lambda e, od=od, t=t, dchunk=dchunk:
                           e.tensor_add(
                        out=outstage[t][:, dchunk * 128:(dchunk + 1) * 128],
                        in0=self.pb[od][:, 0:128],
                        in1=outstage[t][:, dchunk * 128:(dchunk + 1) * 128]),
                        waits=[(pe, ptok), (ld, ostok[t])], incs=[(ve, 1)])
        for t in range(4):
            row0 = slot * SEG + t * 128
            P.emit("gpsimd", lambda e, t=t, row0=row0: e.dma_start(
                out=self.out.ap()[row0:row0 + 128, :], in_=outstage[t]),
                waits=[(ve, ve.count)], incs=[(stt, 16)])

    def finish(self):
        P = self.P
        stt = P.sem("st")
        for name, t, ap in self.taps:
            P.emit("gpsimd", lambda e, t=t, ap=ap: e.dma_start(out=t.ap(), in_=ap),
                   waits=P.barrier_counts(), incs=[(stt, 16)])
        P.emit("sync", lambda e: None, waits=[(stt, stt.count)])

    def build(self, upto=None):
        self.alloc()
        self.st_const()
        for hs in range(NHS):
            self.st_xn(hs)
            self.st_proj(hs)
            self.st_prep(hs)
            self.st_delta(hs)
            if upto is not None and hs >= upto:
                self.finish()
                self.P.build_into(self.nc, self.ctx)
                return self.nc
            if hs == 3:
                self.st_slot(0)
            if hs == 7:
                self.st_slot(1)
        self.st_ffn(0)
        self.st_ffn(1)
        self.finish()
        self.P.build_into(self.nc, self.ctx)
        return self.nc


def make_masks(own):
    m = np.zeros((128, 44 * 256), dtype=ml_dtypes.bfloat16)
    base = CoreBuilder.MASK_BASE
    njts = CoreBuilder.NJT
    j = np.arange(128)[:, None]
    r = np.arange(256)[None, :]
    for slot in range(2):
        for h2 in range(2):
            oi = slot * 2 + h2
            g = own[slot]
            ghs = 2 * g + h2
            for jt in range(njts[slot][h2]):
                tile = ((ghs * 256 + r) >= (jt * 128 + j))
                off = (base[oi] + jt) * 256
                m[:, off:off + 256] = tile.astype(ml_dtypes.bfloat16)
    return m


def make_gofs(own):
    g = np.zeros((1, 64), dtype=np.int32)
    for slot in range(2):
        for h2 in range(2):
            oi = slot * 2 + h2
            ghs = 2 * own[slot] + h2
            for tt in range(2):
                for c in range(2):
                    gc = 4 * ghs + 2 * tt + c
                    g[0, oi * 16 + tt * 2 + c] = gc * 1024
                    g[0, oi * 16 + 4 + tt * 2 + c] = gc * 1024
            g[0, oi * 16 + 8] = ghs * 4096
    return g


_CACHE = {}


def _get_nc():
    if "nc" not in _CACHE:
        _CACHE["nc"] = CoreBuilder(0).build()
    return _CACHE["nc"]


def _shard_inputs(inputs):
    perm = QCOL_PERM
    consts = make_consts()
    base = {
        "wq": np.ascontiguousarray(inputs["L0_attn_Wq"][:, perm]),
        "wk": np.ascontiguousarray(inputs["L0_attn_Wk"]),
        "wv": np.ascontiguousarray(inputs["L0_attn_Wv"]),
        "wo": np.ascontiguousarray(inputs["L0_attn_Wo"][perm, :]),
        "gw": np.ascontiguousarray(
            inputs["L0_ffn_gate_w"].reshape(16, 128, NIC, 128)
            .transpose(2, 1, 0, 3).reshape(NIC * 128, 2048)),
        "uw": np.ascontiguousarray(
            inputs["L0_ffn_up_w"].reshape(16, 128, NIC, 128)
            .transpose(2, 1, 0, 3).reshape(NIC * 128, 2048)),
        "dw": np.ascontiguousarray(
            inputs["L0_ffn_down_w"].reshape(NIC, 128, 2, 1024)
            .transpose(2, 0, 1, 3).reshape(2 * NIC * 128, 1024)),
        "gs": np.ascontiguousarray(inputs["L0_ffn_gate_scale"]),
        "us": np.ascontiguousarray(inputs["L0_ffn_up_scale"]),
        "ds": np.ascontiguousarray(inputs["L0_ffn_down_scale"]),
        "consts": consts,
    }
    masks = {h: make_masks(OWN[h]) for h in (0, 1)}
    gofs = {h: make_gofs(OWN[h]) for h in (0, 1)}
    x = inputs["L0_attn_hidden_states"]
    in_maps = []
    for core in range(8):
        b, h = core // 2, core % 2
        m = dict(base)
        m["x"] = np.ascontiguousarray(x[b])
        rows = np.r_[OWN[h][0] * SEG:(OWN[h][0] + 1) * SEG,
                     OWN[h][1] * SEG:(OWN[h][1] + 1) * SEG]
        m["xg"] = np.ascontiguousarray(x[b][rows])
        m["masks"] = masks[h]
        m["gofs"] = gofs[h]
        in_maps.append(m)
    return in_maps


def kernel(L0_attn_hidden_states, L0_attn_Wq, L0_attn_Wk, L0_attn_Wv, L0_attn_Wo,
           L0_ffn_gate_w, L0_ffn_up_w, L0_ffn_down_w, L0_ffn_gate_scale,
           L0_ffn_up_scale, L0_ffn_down_scale):
    inputs = dict(
        L0_attn_hidden_states=np.asarray(L0_attn_hidden_states),
        L0_attn_Wq=np.asarray(L0_attn_Wq), L0_attn_Wk=np.asarray(L0_attn_Wk),
        L0_attn_Wv=np.asarray(L0_attn_Wv), L0_attn_Wo=np.asarray(L0_attn_Wo),
        L0_ffn_gate_w=np.asarray(L0_ffn_gate_w),
        L0_ffn_up_w=np.asarray(L0_ffn_up_w),
        L0_ffn_down_w=np.asarray(L0_ffn_down_w),
        L0_ffn_gate_scale=np.asarray(L0_ffn_gate_scale),
        L0_ffn_up_scale=np.asarray(L0_ffn_up_scale),
        L0_ffn_down_scale=np.asarray(L0_ffn_down_scale))
    in_maps = _shard_inputs(inputs)
    nc = _get_nc()
    res = run_bass_kernel_spmd(nc, in_maps, core_ids=list(range(8)))
    out = np.empty((B, N, DM), dtype=np.float32)
    for core in range(8):
        b, h = core // 2, core % 2
        o = res.results[core]["out"]
        for slot in range(2):
            g = OWN[h][slot]
            out[b, g * SEG:(g + 1) * SEG] = o[slot * SEG:(slot + 1) * SEG]
    return out


# version 06
# speedup vs baseline: 1.1172x; 1.0792x over previous
import numpy as np
import ml_dtypes
import concourse.bass as bass
import concourse.mybir as mybir
from concourse.bass_utils import run_bass_kernel_spmd
from contextlib import ExitStack

F32 = mybir.dt.float32
BF16 = mybir.dt.bfloat16
F8 = mybir.dt.float8e4
I8 = mybir.dt.int8
AF = mybir.ActivationFunctionType
ALU = mybir.AluOpType
AX = mybir.AxisListType

B, N, DM = 4, 2048, 2048
H, HD, HKV = 32, 64, 8
SEG = 512
HS = 256
NHS = N // HS
NC = DM // 128
NI = 6144
NIC = NI // 128
OWN = {0: (0, 3), 1: (1, 2)}

QA = [8 * (j // 4) + j % 4 for j in range(16)]
QB = [8 * (j // 4) + 4 + j % 4 for j in range(16)]
QCOL_PERM = np.concatenate(
    [np.r_[QA[j] * 64:QA[j] * 64 + 64, QB[j] * 64:QB[j] * 64 + 64]
     for j in range(16)])

ENGS = ("sync", "tensor", "vector", "scalar", "gpsimd")


class Sem:
    __slots__ = ("name", "count", "handle")

    def __init__(self, name):
        self.name = name
        self.count = 0
        self.handle = None


class Prog:
    def __init__(self):
        self.ops = {e: [] for e in ENGS}
        self.sems = {}

    def sem(self, name):
        if name not in self.sems:
            self.sems[name] = Sem(name)
        return self.sems[name]

    def emit(self, eng, fn, waits=(), incs=()):
        incs = tuple(incs)
        for s, n in incs:
            s.count += n
        self.ops[eng].append((fn, tuple(waits), incs))
        return incs[0][0].count if incs else None

    def barrier_counts(self):
        return [(s, s.count) for s in self.sems.values() if s.count > 0]

    def build_into(self, nc, ctx):
        for s in self.sems.values():
            s.handle = ctx.enter_context(nc.semaphore(s.name))
        block = ctx.enter_context(nc.Block())
        for eng in ENGS:
            ops = self.ops[eng]

            def emit_all(engine, ops=ops):
                seen = {}
                for fn, waits, incs in ops:
                    for s, v in waits:
                        if seen.get(s.name, -1) < v:
                            engine.wait_ge(s.handle, v)
                            seen[s.name] = v
                    inst = fn(engine)
                    for s, n in incs:
                        if inst is None:
                            inst = engine.nop()
                        inst.then_inc(s.handle, n)
                        inst = None
            getattr(block, eng)(emit_all)


C_IDENT = slice(0, 128)
C_ONES = slice(128, 256)
C_STRICTL2 = slice(256, 320)
C_STRICTT2 = slice(320, 384)
C_CMASK2 = slice(384, 448)
C_CM256A = slice(448, 704)
C_CM256B = slice(704, 960)
C_HONES = slice(960, 962)
C_E0 = slice(962, 1026)
CONST_W = 1088


def make_consts():
    c = np.zeros((128, CONST_W), dtype=np.float32)
    p = np.arange(128)[:, None]
    f64 = np.arange(64)[None, :]
    f256 = np.arange(256)[None, :]
    c[:, C_IDENT] = np.eye(128)
    c[:, C_ONES] = 1.0
    c[:, C_STRICTL2] = (f64 < (p % 64)).astype(np.float32)
    c[:, C_STRICTT2] = ((p % 64) < f64).astype(np.float32)
    c[:, C_CMASK2] = (f64 >= (p % 64)).astype(np.float32)
    c[:, C_CM256A] = (f256 >= p).astype(np.float32)
    c[:, C_CM256B] = (f256 >= p + 128).astype(np.float32)
    c[0:64, 960:961] = 1.0
    c[64:128, 961:962] = 1.0
    c[0:1, 962:1026] = 1.0
    return c


class CoreBuilder:
    def __init__(self, half):
        self.half = half
        self.own = OWN[half]
        self.nc = bass.Bass("TRN2", target_bir_lowering=False,
                            detect_race_conditions=False)
        self.P = Prog()
        self.ctx = ExitStack()
        self.taps = []
        self._pc_prev = [0, 0]

    def sb(self, name, shape, dt):
        return self.ctx.enter_context(self.nc.sbuf_tensor(name, list(shape), dt))

    def ps(self, name, shape, dt=F32):
        return self.ctx.enter_context(self.nc.psum_tensor(name, list(shape), dt))

    def tap(self, name, ap, shape, dt=F32):
        t = self.nc.dram_tensor(name, list(shape), dt, kind="ExternalOutput")
        self.taps.append((name, t, ap))

    def split(self, eng, src, hi, lo, waits=(), incs=()):
        self.P.emit(eng, lambda e, s=src, h=hi: e.tensor_copy(out=h, in_=s),
                    waits=waits)
        return self.P.emit(eng, lambda e, s=src, h=hi, l=lo: e.tensor_tensor(
            out=l, in0=s, in1=h, op=ALU.subtract), incs=incs)

    def mm3(self, psum, lh, ll, rh, rl, start, stop, tile_position=None,
            waits=(), incs=(), passes=3):
        P = self.P
        tp = tile_position
        P.emit("tensor", lambda e: e.matmul(psum, lh, rh, start=start, stop=False,
                                            tile_position=tp), waits=waits)
        if passes == 2:
            return P.emit("tensor", lambda e: e.matmul(
                psum, lh, rl, start=False, stop=stop, tile_position=tp), incs=incs)
        P.emit("tensor", lambda e: e.matmul(psum, lh, rl, start=False, stop=False,
                                            tile_position=tp))
        return P.emit("tensor", lambda e: e.matmul(
            psum, ll, rh, start=False, stop=stop, tile_position=tp), incs=incs)

    def reset_pe(self, waits=()):
        return self.P.emit("tensor", lambda e: e.ldweights(self.identb[:]),
                           waits=waits)

    def alloc(self):
        nc = self.nc
        din = lambda n, s, d=F32: nc.dram_tensor(n, list(s), d, kind="ExternalInput")
        self.x_in = din("x", [N, DM])
        self.wq_in = din("wq", [DM, DM])
        self.wk_in = din("wk", [DM, 512])
        self.wv_in = din("wv", [DM, 512])
        self.wo_in = din("wo", [DM, DM])
        self.gw_in = din("gw", [NIC * 128, 2048])
        self.uw_in = din("uw", [NIC * 128, 2048])
        self.dw_in = din("dw", [2 * NIC * 128, 1024])
        self.gs_in = din("gs", [NI])
        self.us_in = din("us", [NI])
        self.ds_in = din("ds", [DM])
        self.const_in = din("consts", [128, CONST_W])
        self.xg_in = din("xg", [2 * SEG, DM])
        self.masks_in = nc.dram_tensor("masks", [128, 44 * 256], BF16,
                                       kind="ExternalInput")
        self.gofs_in = nc.dram_tensor("gofs", [1, 64], mybir.dt.int32,
                                      kind="ExternalInput")
        self.out = nc.dram_tensor("out", [2 * SEG, DM], F32, kind="ExternalOutput")
        self.attn_scr = nc.dram_tensor("attn_scr", [128, 4 * NC * HS], F32)
        self.gw8_scr = nc.dram_tensor("gw8_scr", [NIC * 128, 2048], I8)
        self.uw8_scr = nc.dram_tensor("uw8_scr", [NIC * 128, 2048], I8)
        self.dw8_scr = nc.dram_tensor("dw8_scr", [2 * NIC * 128, 1024], I8)
        self.ssnap_scr = nc.dram_tensor("ssnap_scr", [128, 32 * 1024], BF16)
        self.ups_scr = nc.dram_tensor("ups_scr", [128, 32 * 1024], BF16)
        self.klf_scr = nc.dram_tensor("klf_scr", [128, 8 * 4096], BF16)

        self.cst = self.sb("cst", [128, CONST_W], F32)
        self.onesb = self.sb("onesb", [128, 2], BF16)
        self.identb = self.sb("identb", [128, 128], BF16)
        self.fconst = self.sb("fconst", [128, 4], F32)
        self.lnstat = self.sb("lnstat", [128, 4], F32)
        self.scl = self.sb("scl", [128, 112], F32)
        self.bp1 = self.sb("bp1", [128, 49152], BF16)
        t = self.bp1
        self.k_hi = t[:, 0:8192]; self.k_lo = t[:, 8192:16384]
        self.v_hi = t[:, 16384:24576]; self.v_lo = t[:, 24576:32768]
        self.xs_hi = t[:, 32768:36864]; self.xs_lo = t[:, 36864:40960]
        self.klfz = [(t[:, 32768:33792], t[:, 33792:34816]),
                     (t[:, 34816:35840], t[:, 35840:36864])]
        self.wTs_hi = t[:, 36864:37376]; self.wTs_lo = t[:, 37376:37888]
        self.ups_hi = t[:, 37888:38400]; self.ups_lo = t[:, 38400:38912]
        self.Sd_hi = t[:, 38912:39424]; self.Sd_lo = t[:, 39424:39936]
        self.slvP = t[:, 39936:40064]; self.slvPT = t[:, 40064:40192]
        self.slvP2 = t[:, 40192:40320]; self.slvPT2 = t[:, 40320:40448]
        self.ybf = t[:, 40448:40576]
        self.Pt_hi = t[:, 32768:33280]; self.Pt_lo = t[:, 33280:33792]
        self.q_hi = t[:, 40960:45056]; self.q_lo = t[:, 45056:49152]
        self.eq_hi = t[:, 45056 - 4096:45056 - 2048] if False else None
        self.fuse_hi = t[:, 0:24576]; self.fuse_lo = t[:, 24576:49152]
        self.bp2 = self.sb("bp2", [128, 21504], BF16)
        u = self.bp2
        self.kbf_hi = u[:, 0:1024]; self.kbf_lo = u[:, 1024:2048]
        self.kltm_hi = u[:, 2048:3072]; self.kltm_lo = u[:, 3072:4096]
        self.kbtm_hi = u[:, 4096:5120]; self.kbtm_lo = u[:, 5120:6144]
        self.vbtm_hi = u[:, 6144:7168]; self.vbtm_lo = u[:, 7168:8192]
        self.oc_hi = u[:, 0:4096]; self.oc_lo = u[:, 4096:8192]
        self.wsp_hi = u[:, 9216:11264]; self.wsp_lo_x = u[:, 11264:13312]
        self.mkb = 13312
        self.eqb = 0
        self.eq_hi = u[:, 0:4096]; self.eq_lo = u[:, 4096:8192]
        self.xnr_hi = u[:, 0:8192]; self.xnr_lo = u[:, 8192:16384]
        self.w8 = self.sb("w8", [128, 8192], I8)
        self.fp = self.sb("fp", [128, 13568], F32)
        f = self.fp
        self.xt = [f[:, 0:2048], f[:, 2048:4096]]
        self.o_comb = f[:, 0:4096]
        self.sq_scr = f[:, 4096:6144]
        self.k_raw = f[:, 4096:5120]; self.v_raw = f[:, 5120:6144]
        self.ektmp = f[:, 6144:7168]
        self.gltm = f[:, 6144:7168]
        self.klf32 = f[:, 7168:8192]
        self.gl32 = f[:, 8192:9216]
        self.scr1 = f[:, 9216:10240]
        self.uw_f = f[:, 10240:11264]
        self.pexp = f[:, 11264:11776]
        self.wstrip = [f[:, 6144:7168], f[:, 7168:8192]]
        self.eqt = f[:, 4096:6144]
        self.stage = f[:, 11264:11776]
        self.misc = f[:, 11776:13568]
        self.rcpA = self.misc[0:1, 0:256]
        self.rcpB = self.misc[0:1, 256:512]
        self.bcast_sb = self.misc[:, 512:768]
        self.t1sb = self.misc[:, 768:1024]
        self.att_st = self.misc[:, 1024:1536]
        self.S32 = self.misc[:, 1536:1792]
        self.gofs_sb = self.sb("gofs_sb", [1, 64], mybir.dt.int32)
        self.pb = [self.ps(f"pb{i}", [128, 512]) for i in range(8)]

    def st_const(self):
        P = self.P
        ld = P.sem("ld")
        ve = P.sem("ve")
        P.emit("sync", lambda e: e.dma_start(out=self.cst[:], in_=self.const_in.ap()),
               incs=[(ld, 16)])
        P.emit("sync", lambda e: e.dma_start(
            out=self.scl[:, 0:48],
            allow_slow_non_contiguous=True,
            in_=bass.AP(self.gs_in, 0, [[1, 128], [128, 48]])), incs=[(ld, 16)])
        P.emit("sync", lambda e: e.dma_start(
            out=self.scl[:, 48:96],
            allow_slow_non_contiguous=True,
            in_=bass.AP(self.us_in, 0, [[1, 128], [128, 48]])), incs=[(ld, 16)])
        P.emit("sync", lambda e: e.dma_start(
            out=self.scl[:, 96:112],
            allow_slow_non_contiguous=True,
            in_=bass.AP(self.ds_in, 0, [[1, 128], [128, 16]])), incs=[(ld, 16)])
        tok = P.emit("sync", lambda e: e.dma_start(
            out=self.gofs_sb[:], in_=self.gofs_in.ap()), incs=[(ld, 16)])
        P.emit("vector", lambda e: e.memset(self.onesb[:], 1.0), waits=[(ld, tok)])
        P.emit("vector", lambda e: e.memset(self.fconst[:, 0:1], 1e-5))
        P.emit("vector", lambda e: e.tensor_copy(out=self.identb[:],
                                                 in_=self.cst[:, C_IDENT]))
        P.emit("vector", lambda e: e.memset(self.misc[:], 0.0))
        P.emit("vector", lambda e: e.memset(self.Sd_hi[:], 0.0))
        P.emit("vector", lambda e: e.memset(self.Sd_lo[:], 0.0))
        P.emit("vector", lambda e: e.memset(self.slvP[:], 0.0))
        P.emit("vector", lambda e: e.memset(self.slvPT[:], 0.0))
        P.emit("vector", lambda e: e.memset(self.slvP2[:], 0.0))
        P.emit("vector", lambda e: e.memset(self.slvPT2[:], 0.0))
        P.emit("vector", lambda e: e.memset(self.ups_hi[:], 0.0))
        P.emit("vector", lambda e: e.memset(self.ups_lo[:], 0.0))
        P.emit("vector", lambda e: e.memset(self.fconst[0:64, 1:2], 1.0))
        P.emit("vector", lambda e: e.memset(self.fconst[64:128, 1:2], 0.0))
        P.emit("vector", lambda e: e.memset(self.fconst[0:64, 2:3], 0.0))
        vtok = P.emit("vector", lambda e: e.memset(self.fconst[64:128, 2:3], 1.0),
                      incs=[(ve, 1)])
        for e in ENGS:
            P.emit(e, lambda en: None, waits=[(ld, tok), (ve, vtok)])

    def ln_tile(self, src_dram, row0, t, dst_hi=None, dst_lo=None, cstride=HS):
        P = self.P
        ld, ve, pe, sc = P.sem("ld"), P.sem("ve"), P.sem("pe"), P.sem("sc")
        hzv, hzs = P.sem("hzv"), P.sem("hzs")
        if dst_hi is None:
            dst_hi, dst_lo = self.xs_hi, self.xs_lo
        buf = self.xt[t % 2]
        tok_ld = P.emit("sync", lambda e, b=buf, o=row0, s=src_dram: e.dma_start(
            out=b, in_=s.ap()[o:o + 128, :]),
            waits=[(pe, pe.count), (P.sem("st"), P.sem("st").count)],
            incs=[(ld, 16)])
        st = self.lnstat
        h1 = P.emit("vector", lambda e, b=buf: e.reduce_sum(
            out=st[:, 0:1], in_=b, axis=AX.X, negate=True),
            waits=[(ld, tok_ld)], incs=[(hzv, 1)])
        h2 = P.emit("vector", lambda e: e.tensor_scalar_mul(
            out=st[:, 1:2], in0=st[:, 0:1], scalar1=1.0 / DM),
            waits=[(hzv, h1)], incs=[(hzv, 1)])
        vtok = P.emit("vector", lambda e, b=buf: e.tensor_scalar(
            out=b, in0=b, scalar1=st[:, 1:2], scalar2=None, op0=ALU.add),
            waits=[(hzv, h2)], incs=[(ve, 1)])
        h3 = P.emit("scalar", lambda e, b=buf: e.activation(
            out=self.sq_scr, in_=b, func=AF.Square, accum_out=st[:, 2:3]),
            waits=[(ve, vtok)], incs=[(hzs, 1)])
        h4 = P.emit("scalar", lambda e: e.activation(
            out=st[:, 3:4], in_=st[:, 2:3], func=AF.Ln, scale=1.0 / DM,
            bias=self.fconst[:, 0:1]), waits=[(hzs, h3)], incs=[(hzs, 1)])
        stok = P.emit("scalar", lambda e: e.activation(
            out=st[:, 3:4], in_=st[:, 3:4], func=AF.Exp, scale=-0.5),
            waits=[(hzs, h4)], incs=[(sc, 1)])
        vtok = P.emit("vector", lambda e, b=buf: e.tensor_scalar_mul(
            out=b, in0=b, scalar1=st[:, 3:4]), waits=[(sc, stok)],
            incs=[(ve, 1)])
        for c in range(NC):
            bank = self.pb[c % 2]
            ptok = P.emit("tensor", lambda e, b=buf, c=c, bank=bank: e.transpose(
                bank[:, 0:128], b[:, c * 128:(c + 1) * 128], self.cst[:, C_IDENT]),
                waits=[(ve, ve.count)], incs=[(pe, 1)])
            lo_c = c * cstride + t * 128
            self.split("vector", bank[:, 0:128],
                       dst_hi[:, lo_c:lo_c + 128], dst_lo[:, lo_c:lo_c + 128],
                       waits=[(pe, ptok)], incs=[(ve, 1)])

    def st_xn(self, hs):
        for t in range(2):
            self.ln_tile(self.x_in, hs * HS + t * 128, t)

    def st_proj(self, hs):
        P = self.P
        ld, ve, pe, sc = P.sem("ld"), P.sem("ve"), P.sem("pe"), P.sem("sc")
        kv_tok = {}

        def src(dc):
            return bass.AP(self.wk_in, dc * 128 * 512, [[512, 128], [1, 512]]), \
                   bass.AP(self.wv_in, dc * 128 * 512, [[512, 128], [1, 512]])

        last_mm = [pe.count, pe.count]
        last_split = [0, 0]
        for dc in range(NC):
            bi = dc % 2
            sbuf = self.wstrip[bi]
            sph = self.wsp_hi[:, bi * 1024:(bi + 1) * 1024]
            spl = self.wsp_lo_x[:, bi * 1024:(bi + 1) * 1024]
            ksrc, vsrc = src(dc)
            P.emit("sync", lambda e, b=sbuf, s=ksrc: e.dma_start(
                out=b[:, 0:512], in_=s),
                waits=[(ve, last_split[bi]), (pe, last_mm[bi])], incs=[(ld, 16)])
            tok_ld = P.emit("sync", lambda e, b=sbuf, s=vsrc: e.dma_start(
                out=b[:, 512:1024], in_=s), incs=[(ld, 16)])
            vtok = self.split("vector", sbuf, sph, spl,
                              waits=[(ld, tok_ld), (pe, last_mm[bi])], incs=[(ve, 1)])
            last_split[bi] = vtok
            for pc in range(4):
                lh = sph[:, pc * 128:(pc + 1) * 128]
                ll = spl[:, pc * 128:(pc + 1) * 128]
                rh = self.xs_hi[:, dc * HS:(dc + 1) * HS]
                rl = self.xs_lo[:, dc * HS:(dc + 1) * HS]
                tok = self.mm3(self.pb[pc][:, 0:256], lh, ll, rh, rl,
                               start=(dc == 0), stop=(dc == NC - 1),
                               waits=[(ve, vtok)],
                               incs=[(pe, 1)])
                if dc == NC - 1:
                    kv_tok[("k", pc)] = tok
            for tt in range(2):
                lh = self.xs_hi[:, dc * HS + tt * 128: dc * HS + (tt + 1) * 128]
                ll = self.xs_lo[:, dc * HS + tt * 128: dc * HS + (tt + 1) * 128]
                rh = sph[:, 512:1024]
                rl = spl[:, 512:1024]
                tok = self.mm3(self.pb[4 + tt][:], lh, ll, rh, rl,
                               start=(dc == 0), stop=(dc == NC - 1),
                               incs=[(pe, 1)])
                if dc == NC - 1:
                    kv_tok[("v", tt)] = tok
            last_mm[bi] = tok
        for pc in range(4):
            bank = self.pb[pc][:, 0:256]
            w = [(pe, kv_tok[("k", pc)])]
            stok = P.emit("scalar", lambda e, b=bank, pc=pc: e.activation(
                out=self.k_raw[:, pc * 256:(pc + 1) * 256], in_=b, func=AF.Identity),
                waits=w, incs=[(sc, 1)])
            co = pc * 2048 + hs * HS
            self.split("vector", bank, self.k_hi[:, co:co + 256],
                       self.k_lo[:, co:co + 256], waits=w + [(sc, stok)],
                       incs=[(ve, 1)])
        for tt in range(2):
            bank = self.pb[4 + tt][:]
            w = [(pe, kv_tok[("v", tt)])]
            stok = P.emit("scalar", lambda e, b=bank, tt=tt: e.activation(
                out=self.v_raw[:, tt * 512:(tt + 1) * 512], in_=b, func=AF.Identity),
                waits=w, incs=[(sc, 1)])
            co = (hs * 2 + tt) * 512
            self.split("vector", bank, self.v_hi[:, co:co + 512],
                       self.v_lo[:, co:co + 512], waits=w + [(sc, stok)],
                       incs=[(ve, 1)])

    def bcast_recip(self, src, waits=(), w=256):
        P = self.P
        pe, hzv, ve = P.sem("pe"), P.sem("hzv"), P.sem("ve")
        bsA, bsB, bbc = self.pb[5], self.pb[6], self.pb[7]
        t1 = P.emit("tensor", lambda e, s=src, w=w: e.matmul(
            bsA[0:1, 0:w], self.cst[:, 960:961], s, start=True, stop=True),
            waits=waits, incs=[(pe, 1)])
        t2 = P.emit("tensor", lambda e, s=src, w=w: e.matmul(
            bsB[0:1, 0:w], self.cst[:, 961:962], s, start=True, stop=True),
            incs=[(pe, 1)])
        v1 = P.emit("vector", lambda e, w=w: e.reciprocal(
            out=self.misc[0:1, 0:w], in_=bsA[0:1, 0:w]),
            waits=[(pe, t2)], incs=[(hzv, 1)])
        v2 = P.emit("vector", lambda e, w=w: e.reciprocal(
            out=self.misc[0:1, w:2 * w], in_=bsB[0:1, 0:w]),
            waits=[(hzv, v1)], incs=[(hzv, 1)])
        P.emit("tensor", lambda e, w=w: e.matmul(
            bbc[0:64, 0:w], self.cst[:, C_E0], self.misc[:, 0:w],
            start=True, stop=True), waits=[(hzv, v2), (ve, ve.count)])
        btok = P.emit("tensor", lambda e, w=w: e.matmul(
            bbc[64:128, 0:w], self.cst[:, C_E0], self.misc[:, w:2 * w],
            start=True, stop=True), incs=[(pe, 1)])
        return bbc[:, 0:w], btok

    def st_prep(self, hs):
        P = self.P
        ld, ve, pe, sc = P.sem("ld"), P.sem("ve"), P.sem("pe"), P.sem("sc")
        hzs = P.sem("hzs")
        etok = P.emit("scalar", lambda e: e.activation(
            out=self.ektmp, in_=self.k_raw, func=AF.Exp),
            waits=[(sc, sc.count), (ve, ve.count)], incs=[(sc, 1)])
        for pc in (0, 2):
            bb, btok = self.bcast_recip(self.ektmp[:, pc * 256:(pc + 2) * 256],
                                        waits=[(sc, etok), (ve, ve.count)], w=512)
            P.emit("vector", lambda e, pc=pc, bb=bb: e.tensor_tensor(
                out=self.klf32[:, pc * 256:(pc + 2) * 256],
                in0=self.ektmp[:, pc * 256:(pc + 2) * 256], in1=bb, op=ALU.mult),
                waits=[(pe, btok)])
            for z in range(2):
                hi, lo = self.klfz[z]
                msk = self.fconst[:, 1 + z:2 + z]
                sl = slice(pc * 256, (pc + 2) * 256)
                P.emit("vector", lambda e, sl=sl, hi=hi, msk=msk: e.tensor_scalar_mul(
                    out=hi[:, sl], in0=self.klf32[:, sl], scalar1=msk))
                P.emit("vector", lambda e, sl=sl, hi=hi, lo=lo, msk=msk:
                       e.scalar_tensor_tensor(
                    out=lo[:, sl], in0=self.klf32[:, sl], scalar=msk, in1=hi[:, sl],
                    op0=ALU.mult, op1=ALU.subtract), incs=[(ve, 1)])
        g1 = P.emit("scalar", lambda e: e.activation(
            out=self.scr1, in_=self.k_raw, func=AF.Sigmoid), incs=[(hzs, 1)])
        gtok = P.emit("scalar", lambda e: e.activation(
            out=self.gl32, in_=self.scr1, func=AF.Ln), waits=[(hzs, g1)],
            incs=[(sc, 1)])
        P.emit("vector", lambda e: e.scalar_tensor_tensor(
            out=self.scr1, in0=self.klf32, scalar=1.0 / 16.0, in1=self.gl32,
            op0=ALU.mult, op1=ALU.mult), waits=[(sc, gtok), (ve, ve.count)])
        kbtok = self.split("vector", self.scr1, self.kbf_hi, self.kbf_lo,
                           incs=[(ve, 1)])
        for pc in range(4):
            for tt in range(2):
                bank = self.pb[(pc * 2 + tt) % 2]
                srcT = self.klf32[:, pc * 256 + tt * 128: pc * 256 + (tt + 1) * 128]
                ptok = P.emit("tensor", lambda e, s=srcT, bank=bank: e.transpose(
                    bank[:, 0:128], s, self.cst[:, C_IDENT]),
                    waits=[(ve, ve.count), (sc, sc.count)], incs=[(pe, 1)])
                dst = 512 * tt + pc * 128
                stok = P.emit("scalar", lambda e, bank=bank, dst=dst: e.activation(
                    out=self.scr1[:, dst:dst + 128], in_=bank[:, 0:128],
                    func=AF.Identity), waits=[(pe, ptok), (ve, kbtok)], incs=[(sc, 1)])
                self.split("vector", bank[:, 0:128],
                           self.kltm_hi[:, dst:dst + 128],
                           self.kltm_lo[:, dst:dst + 128],
                           waits=[(pe, ptok), (sc, stok)], incs=[(ve, 1)])
        for pc in range(4):
            for tt in range(2):
                bank = self.pb[(pc * 2 + tt) % 2 + 2]
                srcT = self.gl32[:, pc * 256 + tt * 128: pc * 256 + (tt + 1) * 128]
                ptok = P.emit("tensor", lambda e, s=srcT, bank=bank: e.transpose(
                    bank[:, 0:128], s, self.cst[:, C_IDENT]),
                    waits=[(ve, ve.count), (sc, sc.count)], incs=[(pe, 1)])
                dst = 512 * tt + pc * 128
                P.emit("scalar", lambda e, bank=bank, dst=dst: e.activation(
                    out=self.gltm[:, dst:dst + 128], in_=bank[:, 0:128],
                    func=AF.Identity), waits=[(pe, ptok)], incs=[(sc, 1)])
        P.emit("vector", lambda e: e.scalar_tensor_tensor(
            out=self.klf32, in0=self.scr1, scalar=1.0 / 16.0, in1=self.gltm,
            op0=ALU.mult, op1=ALU.mult),
            waits=[(sc, sc.count), (ve, ve.count), (pe, pe.count)])
        self.split("vector", self.klf32, self.kbtm_hi, self.kbtm_lo, incs=[(ve, 1)])
        P.emit("vector", lambda e: e.scalar_tensor_tensor(
            out=self.k_raw, in0=self.v_raw, scalar=1.0 / 16.0, in1=self.gltm,
            op0=ALU.mult, op1=ALU.mult))
        self.split("vector", self.k_raw, self.vbtm_hi, self.vbtm_lo, incs=[(ve, 1)])
        stt = P.sem("st")
        for z in range(2):
            hi, lo = self.klfz[z]
            P.emit("gpsimd", lambda e, hs=hs, z=z, hi=hi: e.dma_start(
                out=self.klf_scr.ap()[:, hs * 4096 + z * 2048: hs * 4096 + z * 2048 + 1024],
                in_=hi), waits=[(ve, ve.count)], incs=[(stt, 16)])
            P.emit("gpsimd", lambda e, hs=hs, z=z, lo=lo: e.dma_start(
                out=self.klf_scr.ap()[:, hs * 4096 + z * 2048 + 1024: hs * 4096 + (z + 1) * 2048],
                in_=lo), incs=[(stt, 16)])

    def st_delta(self, hs):
        for tt in range(2):
            self.delta_solve(hs, tt)
            for c in range(2):
                self.delta_scan(hs, tt, c)

    def delta_solve(self, hs, tt):
        P = self.P
        ve, pe = P.sem("ve"), P.sem("pe")
        bL, bY, bSq, bWT = self.pb[0], self.pb[1], self.pb[2], self.pb[3]
        P.emit("vector", lambda e: e.memset(self.slvP[:], 0.0),
               waits=[(pe, pe.count)])
        P.emit("vector", lambda e: e.memset(self.slvPT[:], 0.0))
        P.emit("vector", lambda e: e.memset(self.slvP2[:], 0.0))
        P.emit("vector", lambda e: e.memset(self.slvPT2[:], 0.0), incs=[(ve, 1)])
        for kv in range(8):
            z = kv % 2
            base = z * 64
            pc = kv // 2
            colc = [pc * 256 + tt * 128 + c * 64 for c in range(2)]
            klz_hi, klz_lo = self.klfz[z]
            for c in range(2):
                kb = self.kbf_hi[:, colc[c]:colc[c] + 64]
                kl = klz_hi[:, colc[c]:colc[c] + 64]
                P.emit("tensor", lambda e, kb=kb, kl=kl, c=c: e.matmul(
                    bL[c * 64:(c + 1) * 64, 0:64], kb, kl, start=True, stop=True),
                    waits=[(ve, ve.count)])
                P.emit("tensor", lambda e, kb=kb, kl=kl, c=c: e.matmul(
                    bL[c * 64:(c + 1) * 64, 64:128], kl, kb, start=True, stop=True),
                    incs=[(pe, 1)])
            ltok = pe.count
            P.emit("vector", lambda e: e.scalar_tensor_tensor(
                out=self.slvP[0:64, 0:64], in0=bL[0:64, 0:64], scalar=-1.0,
                in1=self.cst[0:64, C_STRICTL2], op0=ALU.mult, op1=ALU.mult),
                waits=[(pe, ltok)])
            P.emit("vector", lambda e: e.scalar_tensor_tensor(
                out=self.slvP[64:128, 64:128], in0=bL[64:128, 0:64], scalar=-1.0,
                in1=self.cst[64:128, C_STRICTL2], op0=ALU.mult, op1=ALU.mult))
            P.emit("vector", lambda e: e.scalar_tensor_tensor(
                out=self.slvPT[0:64, 0:64], in0=bL[0:64, 64:128], scalar=-1.0,
                in1=self.cst[0:64, C_STRICTT2], op0=ALU.mult, op1=ALU.mult))
            vtok = P.emit("vector", lambda e: e.scalar_tensor_tensor(
                out=self.slvPT[64:128, 64:128], in0=bL[64:128, 64:128], scalar=-1.0,
                in1=self.cst[64:128, C_STRICTT2], op0=ALU.mult, op1=ALU.mult),
                incs=[(ve, 1)])
            ucol = 64 if kv % 2 == 0 else 0
            wcol = 64 - ucol
            y32 = self.uw_f[:, kv * 128: kv * 128 + 128]
            vb_h = self.vbtm_hi[:, tt * 512 + kv * 64: tt * 512 + (kv + 1) * 64]
            vb_l = self.vbtm_lo[:, tt * 512 + kv * 64: tt * 512 + (kv + 1) * 64]
            kb_h = self.kbtm_hi[:, tt * 512 + kv * 64: tt * 512 + (kv + 1) * 64]
            kb_l = self.kbtm_lo[:, tt * 512 + kv * 64: tt * 512 + (kv + 1) * 64]
            P.emit("vector", lambda e, y=y32, u=ucol, h=vb_h, l=vb_l: e.tensor_add(
                out=y[:, u:u + 64], in0=h, in1=l), waits=[(ve, ve.count)])
            P.emit("vector", lambda e, y=y32, w=wcol, h=kb_h, l=kb_l: e.tensor_add(
                out=y[:, w:w + 64], in0=h, in1=l), incs=[(ve, 1)])
            cur = (self.slvP, self.slvPT)
            nxt = (self.slvP2, self.slvPT2)
            for k in range(6):
                ytok = P.emit("vector", lambda e, y=y32: e.tensor_copy(
                    out=self.ybf, in_=y), waits=[(pe, pe.count), (ve, ve.count)],
                    incs=[(ve, 1)])
                atok = P.emit("tensor", lambda e, cur=cur: e.matmul(
                    bY[:, 0:128], cur[1], self.ybf, start=True, stop=True),
                    waits=[(ve, ytok)], incs=[(pe, 1)])
                P.emit("vector", lambda e, y=y32: e.tensor_add(
                    out=y, in0=y, in1=bY[:, 0:128]), waits=[(pe, atok)],
                    incs=[(ve, 1)])
                if k < 5:
                    P.emit("tensor", lambda e, cur=cur: e.matmul(
                        bSq[:, 0:128], cur[1], cur[0], start=True, stop=True))
                    sqt = P.emit("tensor", lambda e, cur=cur: e.matmul(
                        bSq[:, 128:256], cur[0], cur[1], start=True, stop=True),
                        incs=[(pe, 1)])
                    P.emit("vector", lambda e, nxt=nxt: e.tensor_copy(
                        out=nxt[0], in_=bSq[:, 0:128]), waits=[(pe, sqt)])
                    P.emit("vector", lambda e, nxt=nxt: e.tensor_copy(
                        out=nxt[1], in_=bSq[:, 128:256]), incs=[(ve, 1)])
                    cur, nxt = nxt, cur
            uwtok = P.emit("vector", lambda e: None, incs=[(ve, 1)])
            srcT = self.uw_f[:, kv * 128: kv * 128 + 128]
            wtok = P.emit("tensor", lambda e, s=srcT: e.transpose(
                bWT[:, 0:128], s, self.cst[:, C_IDENT]),
                waits=[(ve, uwtok)], incs=[(pe, 1)])
            rows = slice(0, 64) if kv % 2 == 0 else slice(64, 128)
            wsl = slice((kv // 2) * 128, (kv // 2) * 128 + 128)
            P.emit("vector", lambda e, rows=rows, wsl=wsl: e.tensor_scalar_mul(
                out=self.wTs_hi[rows, wsl], in0=bWT[rows, 0:128], scalar1=-1.0),
                waits=[(pe, wtok)])
            P.emit("vector", lambda e, rows=rows, wsl=wsl: e.scalar_tensor_tensor(
                out=self.wTs_lo[rows, wsl], in0=bWT[rows, 0:128],
                scalar=-1.0, in1=self.wTs_hi[rows, wsl],
                op0=ALU.mult, op1=ALU.subtract), incs=[(ve, 1)])

    def delta_scan(self, hs, tt, c):
        P = self.P
        ve, pe, sc = P.sem("ve"), P.sem("pe"), P.sem("sc")
        stt = P.sem("st")
        bUP, bS = self.pb[4], self.pb[5]
        cp = slice(c * 64, (c + 1) * 64)
        op = slice(64 - c * 64, 128 - c * 64)
        gc = hs * 4 + tt * 2 + c
        def s32ap(rows):
            return bass.AP(self.fp, rows * 13568 + 11776 + 1536,
                           [[13568, 64], [64, 4], [1, 64]])

        def sdap(t, rows, coff):
            return bass.AP(self.bp1, rows * 49152 + t + coff,
                           [[49152, 64], [128, 4], [1, 64]])
        P.emit("vector", lambda e: e.memset(self.Sd_hi[:], 0.0),
               waits=[(pe, pe.count)])
        P.emit("vector", lambda e: e.memset(self.Sd_lo[:], 0.0))
        for rows, coff in ((0, 0), (64, 64)):
            P.emit("vector", lambda e, rows=rows, coff=coff: e.tensor_copy(
                out=sdap(38912, rows, coff), in_=s32ap(rows)))
            P.emit("vector", lambda e, rows=rows, coff=coff: e.tensor_tensor(
                out=sdap(39424, rows, coff), in0=s32ap(rows),
                in1=sdap(38912, rows, coff), op=ALU.subtract), incs=[(ve, 1)])
        P.emit("gpsimd", lambda e, gc=gc: e.dma_start(
            out=self.ssnap_scr.ap()[:, gc * 1024: gc * 1024 + 512], in_=self.Sd_hi),
            waits=[(ve, ve.count)], incs=[(stt, 16)])
        P.emit("gpsimd", lambda e, gc=gc: e.dma_start(
            out=self.ssnap_scr.ap()[:, gc * 1024 + 512: (gc + 1) * 1024],
            in_=self.Sd_lo), incs=[(stt, 16)])
        P.emit("vector", lambda e: e.memset(self.ups_hi[op, :], 0.0),
               waits=[(pe, pe.count)])
        P.emit("vector", lambda e: e.memset(self.ups_lo[op, :], 0.0), incs=[(ve, 1)])
        for m in range(4):
            wh = self.wTs_hi[:, m * 128 + c * 64: m * 128 + (c + 1) * 64]
            wl = self.wTs_lo[:, m * 128 + c * 64: m * 128 + (c + 1) * 64]
            sh = self.Sd_hi[:, m * 128:(m + 1) * 128]
            sl = self.Sd_lo[:, m * 128:(m + 1) * 128]
            mtok = self.mm3(bUP[cp, m * 128:(m + 1) * 128], wh, wl, sh, sl,
                            start=True, stop=True,
                            waits=[(ve, ve.count)], incs=[(pe, 1)])
            for z in range(2):
                kv = 2 * m + z
                ucol = 64 if z == 0 else 0
                up32 = self.scr1[cp, kv * 64:(kv + 1) * 64]
                u_sl = self.uw_f[cp, kv * 128 + ucol: kv * 128 + ucol + 64]
                P.emit("vector", lambda e, up32=up32, u=u_sl, m=m, z=z: e.tensor_add(
                    out=up32, in0=u,
                    in1=bUP[cp, m * 128 + z * 64: m * 128 + (z + 1) * 64]),
                    waits=[(pe, mtok)])
                self.split("vector", up32,
                           self.ups_hi[cp, kv * 64:(kv + 1) * 64],
                           self.ups_lo[cp, kv * 64:(kv + 1) * 64], incs=[(ve, 1)])
        P.emit("gpsimd", lambda e, gc=gc: e.dma_start(
            out=self.ups_scr.ap()[:, gc * 1024: gc * 1024 + 512], in_=self.ups_hi),
            waits=[(ve, ve.count)], incs=[(stt, 16)])
        P.emit("gpsimd", lambda e, gc=gc: e.dma_start(
            out=self.ups_scr.ap()[:, gc * 1024 + 512: (gc + 1) * 1024],
            in_=self.ups_lo), incs=[(stt, 16)])
        for kv in range(8):
            base = (kv % 2) * 64
            pc = kv // 2
            kcol = slice(tt * 512 + pc * 128 + (kv % 2) * 64,
                         tt * 512 + pc * 128 + (kv % 2) * 64 + 64)
            kh = self.kltm_hi[:, kcol]
            kl = self.kltm_lo[:, kcol]
            uh = self.ups_hi[:, kv * 64:(kv + 1) * 64]
            ul = self.ups_lo[:, kv * 64:(kv + 1) * 64]
            self.mm3(bS[slice(base, base + 64), (kv // 2) * 64:(kv // 2 + 1) * 64],
                     kh, kl, uh, ul, start=True, stop=True,
                     waits=[(ve, ve.count), (stt, stt.count)],
                     incs=[(pe, 1)] if kv == 7 else ())
        stok = P.sem("pe").count
        P.emit("vector", lambda e: e.tensor_add(
            out=self.S32[:], in0=self.S32[:], in1=bS[:, 0:256]),
            waits=[(pe, stok)], incs=[(ve, 1)])

    def st_q(self, hs):
        P = self.P
        ld, ve, pe, sc = P.sem("ld"), P.sem("ve"), P.sem("pe"), P.sem("sc")
        for qh in range(2):
            last_mm = [pe.count, pe.count]
            last_split = [0, 0]
            qtok = {}
            for dc in range(NC):
                bi = dc % 2
                sbuf = self.wstrip[bi]
                sph = self.wsp_hi[:, bi * 1024:(bi + 1) * 1024]
                spl = self.wsp_lo_x[:, bi * 1024:(bi + 1) * 1024]
                tok_ld = P.emit("sync", lambda e, b=sbuf, dc=dc, qh=qh: e.dma_start(
                    out=b, in_=bass.AP(self.wq_in, dc * 128 * DM + qh * 1024,
                                       [[DM, 128], [1, 1024]])),
                    waits=[(ve, last_split[bi]), (pe, last_mm[bi])],
                    incs=[(ld, 16)])
                vtok = self.split("vector", sbuf, sph, spl,
                                  waits=[(ld, tok_ld), (pe, last_mm[bi])],
                                  incs=[(ve, 1)])
                last_split[bi] = vtok
                for qi in range(8):
                    qc = qh * 8 + qi
                    lh = sph[:, qi * 128:(qi + 1) * 128]
                    ll = spl[:, qi * 128:(qi + 1) * 128]
                    rh = self.xs_hi[:, dc * HS:(dc + 1) * HS]
                    rl = self.xs_lo[:, dc * HS:(dc + 1) * HS]
                    tok = self.mm3(self.pb[qi][:, 0:256], lh, ll, rh, rl,
                                   start=(dc == 0), stop=(dc == NC - 1),
                                   waits=[(ve, vtok)], incs=[(pe, 1)])
                    if dc == NC - 1:
                        qtok[qc] = tok
                last_mm[bi] = tok
            for qi in range(8):
                qc = qh * 8 + qi
                bank = self.pb[qi][:, 0:256]
                w = [(pe, qtok[qc])]
                co = qc * HS
                stok = P.emit("scalar", lambda e, b=bank, qi=qi: e.activation(
                    out=self.eqt[:, qi * 256:(qi + 1) * 256], in_=b, func=AF.Exp),
                    waits=w, incs=[(sc, 1)])
                self.split("vector", bank, self.q_hi[:, co:co + 256],
                           self.q_lo[:, co:co + 256], waits=w + [(sc, stok)],
                           incs=[(ve, 1)])
            for qi in (0, 2, 4, 6):
                src = self.eqt[:, qi * 256:(qi + 2) * 256]
                bb, btok = self.bcast_recip(src, waits=[(sc, sc.count),
                                                        (ve, ve.count)], w=512)
                P.emit("vector", lambda e, s=src, bb=bb: e.tensor_tensor(
                    out=s, in0=s, in1=bb, op=ALU.mult), waits=[(pe, btok)])
                for q2 in range(2):
                    qc = qh * 8 + qi + q2
                    sap = bass.AP(self.fp, 4096 + (qi + q2) * 256,
                                  [[13568, 128], [64, 4], [1, 64]])
                    hap = bass.AP(self.bp2, self.eqb + qc * 64,
                                  [[21504, 128], [1024, 4], [1, 64]])
                    lap = bass.AP(self.bp2, self.eqb + 4096 + qc * 64,
                                  [[21504, 128], [1024, 4], [1, 64]])
                    P.emit("vector", lambda e, s=sap, h=hap: e.tensor_copy(
                        out=h, in_=s))
                    P.emit("vector", lambda e, s=sap, h=hap, l=lap: e.tensor_tensor(
                        out=l, in0=s, in1=h, op=ALU.subtract), incs=[(ve, 1)])

    MASK_BASE = (0, 6, 14, 28)
    NJT = ((6, 8), (14, 16))

    def st_std(self, oi, slot, h2):
        P = self.P
        ld, ve, pe, sc = P.sem("ld"), P.sem("ve"), P.sem("pe"), P.sem("sc")
        hzv = P.sem("hzv")
        njt = self.NJT[slot][h2]
        mbase = self.MASK_BASE[oi]
        bSc = [self.pb[0], self.pb[1]]
        bO = self.pb[2]
        bRs = [self.pb[3], self.pb[4]]
        bBc = self.pb[5]
        mld = 0
        for jt in range(njt):
            mld = P.emit("sync", lambda e, jt=jt: e.dma_start(
                out=self.bp2[:, self.mkb + jt * 256:self.mkb + (jt + 1) * 256],
                in_=self.masks_in.ap()[:, (mbase + jt) * 256:(mbase + jt + 1) * 256]),
                waits=[(pe, pe.count), (ve, ve.count)] if jt == 0 else (),
                incs=[(ld, 16)])
        PtA = (self.bp1[:, 32768:33280], self.bp1[:, 33280:33792])
        PtB = (self.bp1[:, 33792:34304], self.bp1[:, 34304:34816])
        Pt = (PtA, PtB)
        pexp_h = (self.pexp, self.scr1[:, 0:512])
        t1 = self.uw_f[:, 0:512]
        for qp in range(8):
            kvA = 2 * (qp // 2)
            otok = [None, None]
            rstok = [None, None]
            sp_prev = 0
            pv_prev = 0
            qsl = slice(2 * qp * HS, (2 * qp + 2) * HS)
            for jt in range(njt):
                stok = [0, 0]
                for hi_ in range(2):
                    kv = kvA + hi_
                    base = (kv % 2) * 64
                    rows = slice(base, base + 64)
                    pc = kv // 2
                    kc = slice(pc * 2048 + jt * 128, pc * 2048 + (jt + 1) * 128)
                    tok = self.mm3(bSc[hi_][:, 0:512], self.k_hi[rows, kc],
                                   self.k_lo[rows, kc],
                                   self.q_hi[rows, qsl], self.q_lo[rows, qsl],
                                   start=True, stop=True,
                                   tile_position=(base, 0),
                                   waits=[(ve, ve.count), (sc, sc.count)],
                                   incs=[(pe, 1)])
                    stok[hi_] = P.emit("scalar", lambda e, hi_=hi_: e.activation(
                        out=pexp_h[hi_], in_=bSc[hi_][:, 0:512],
                        func=AF.Exp, scale=0.125),
                        waits=[(pe, tok), (ve, sp_prev)], incs=[(sc, 1)])
                mask_ap = bass.AP(self.bp2, self.mkb + jt * 256,
                                  [[21504, 128], [0, 2], [1, 256]])
                for hi_ in range(2):
                    P.emit("vector", lambda e, hi_=hi_, m=mask_ap: e.tensor_tensor(
                        out=pexp_h[hi_], in0=pexp_h[hi_], in1=m, op=ALU.mult),
                        waits=[(sc, stok[hi_]), (pe, pv_prev), (ld, mld)],
                        incs=[(ve, 1)])
                    sp_prev = self.split("vector", pexp_h[hi_],
                                         Pt[hi_][0], Pt[hi_][1], incs=[(ve, 1)])
                for hi_ in range(2):
                    P.emit("tensor", lambda e, hi_=hi_, jt=jt: e.matmul(
                        bRs[hi_][0:1, 0:512], self.onesb[:, 0:1],
                        Pt[hi_][0], start=(jt == 0), stop=False),
                        waits=[(ve, sp_prev)])
                    tok = P.emit("tensor", lambda e, hi_=hi_, jt=jt: e.matmul(
                        bRs[hi_][0:1, 0:512], self.onesb[:, 0:1],
                        Pt[hi_][1], start=False, stop=(jt == njt - 1)),
                        incs=[(pe, 1)] if jt == njt - 1 else ())
                    if jt == njt - 1:
                        rstok[hi_] = tok
                    kv = kvA + hi_
                    base = (kv % 2) * 64
                    vc = slice(jt * 512 + kv * 64, jt * 512 + (kv + 1) * 64)
                    pv_prev = self.mm3(bO[slice(base, base + 64), 0:512],
                                       self.v_hi[:, vc], self.v_lo[:, vc],
                                       Pt[hi_][0], Pt[hi_][1],
                                       start=(jt == 0), stop=(jt == njt - 1),
                                       incs=[(pe, 1)])
                    if jt == njt - 1:
                        otok[hi_] = pv_prev
            v1 = P.emit("vector", lambda e: e.reciprocal(
                out=self.misc[0:1, 0:512], in_=bRs[0][0:1, 0:512]),
                waits=[(pe, rstok[0])], incs=[(hzv, 1)])
            v2 = P.emit("vector", lambda e: e.reciprocal(
                out=self.misc[0:1, 512:1024], in_=bRs[1][0:1, 0:512]),
                waits=[(pe, rstok[1]), (hzv, v1)], incs=[(hzv, 1)])
            P.emit("tensor", lambda e: e.matmul(
                bBc[0:64, 0:512], self.cst[:, C_E0], self.misc[:, 0:512],
                start=True, stop=True), waits=[(hzv, v2)])
            btok = P.emit("tensor", lambda e: e.matmul(
                bBc[64:128, 0:512], self.cst[:, C_E0], self.misc[:, 512:1024],
                start=True, stop=True), incs=[(pe, 1)])
            P.emit("vector", lambda e, t1=t1: e.tensor_copy(
                out=t1, in_=bBc[:, 0:512]), waits=[(pe, btok)])
            P.emit("vector", lambda e, t1=t1: e.tensor_tensor(
                out=t1, in0=bO[:, 0:512], in1=t1, op=ALU.mult),
                waits=[(pe, otok[0]), (pe, otok[1])])
            osl = slice(2 * qp * HS, (2 * qp + 2) * HS)
            P.emit("vector", lambda e, osl=osl, t1=t1: e.scalar_tensor_tensor(
                out=self.o_comb[:, osl], in0=t1, scalar=0.5,
                in1=self.o_comb[:, osl], op0=ALU.mult, op1=ALU.add),
                incs=[(ve, 1)])

    def st_wo(self, oi):
        P = self.P
        ld, ve, pe, sc = P.sem("ld"), P.sem("ve"), P.sem("pe"), P.sem("sc")
        stt = P.sem("st")
        for i in range(4):
            self.split("vector", self.o_comb[:, i * 1024:(i + 1) * 1024],
                       self.oc_hi[:, i * 1024:(i + 1) * 1024],
                       self.oc_lo[:, i * 1024:(i + 1) * 1024],
                       waits=[(ve, ve.count)], incs=[(ve, 1)])
        for wv in range(2):
            wtok = {}
            last_mm = [pe.count, pe.count]
            last_split = [0, 0]
            for ic in range(NC):
                bi = ic % 2
                sbuf = self.wstrip[bi]
                sph = self.wsp_hi[:, bi * 1024:(bi + 1) * 1024]
                spl = self.wsp_lo_x[:, bi * 1024:(bi + 1) * 1024]
                tok_ld = P.emit("sync", lambda e, b=sbuf, ic=ic, wv=wv: e.dma_start(
                    out=b, in_=bass.AP(self.wo_in, ic * 128 * DM + wv * 1024,
                                       [[DM, 128], [1, 1024]])),
                    waits=[(ve, last_split[bi]), (pe, last_mm[bi])],
                    incs=[(ld, 16)])
                vtok = self.split("vector", sbuf, sph, spl,
                                  waits=[(ld, tok_ld), (pe, last_mm[bi])],
                                  incs=[(ve, 1)])
                last_split[bi] = vtok
                for oc in range(8):
                    dchunk = wv * 8 + oc
                    lh = sph[:, oc * 128:(oc + 1) * 128]
                    ll = spl[:, oc * 128:(oc + 1) * 128]
                    rh = self.oc_hi[:, ic * 256:(ic + 1) * 256]
                    rl = self.oc_lo[:, ic * 256:(ic + 1) * 256]
                    tok = self.mm3(self.pb[oc][:, 0:256], lh, ll, rh, rl,
                                   start=(ic == 0), stop=(ic == NC - 1),
                                   waits=[(ve, vtok)], incs=[(pe, 1)])
                    if ic == NC - 1:
                        wtok[dchunk] = tok
                last_mm[bi] = tok
            for oc in range(8):
                dchunk = wv * 8 + oc
                bank = self.pb[oc][:, 0:256]
                sl = oc % 2
                vtok = P.emit("vector", lambda e, b=bank, sl=sl: e.tensor_copy(
                    out=self.att_st[:, sl * 256:(sl + 1) * 256], in_=b),
                    waits=[(pe, wtok[dchunk]), (stt, max(0, stt.count - 16))],
                    incs=[(ve, 1)])
                off = oi * NC * HS + dchunk * HS
                P.emit("gpsimd", lambda e, sl=sl, off=off: e.dma_start(
                    out=self.attn_scr.ap()[:, off:off + HS],
                    in_=self.att_st[:, sl * 256:(sl + 1) * 256]),
                    waits=[(ve, vtok)], incs=[(stt, 16)])

    def st_slot(self, slot):
        P = self.P
        ve, pe = P.sem("ve"), P.sem("pe")
        for h2 in range(2):
            oi = slot * 2 + h2
            for t in range(2):
                self.ln_tile(self.xg_in, oi * HS + t * 128, t)
            P.emit("vector", lambda e: e.memset(self.o_comb[:], 0.0),
                   waits=[(pe, pe.count)], incs=[(ve, 1)])
            self.st_q(oi)
            self.slot_delta_o(oi)
            self.st_std(oi, slot, h2)
            self.st_wo(oi)

    def slot_delta_o(self, oi):
        P = self.P
        ld, ve, pe, sc = P.sem("ld"), P.sem("ve"), P.sem("pe"), P.sem("sc")
        stt = P.sem("st")
        bA, bO = self.pb[6], self.pb[7]

        def dyn_dma(dst, scr, item, width):
            def fn(e, dst=dst, scr=scr, item=item, width=width):
                v = e.value_load(self.gofs_sb[0:1, item:item + 1])
                inst = e.dma_start(out=dst, in_=bass.AP(scr, v,
                                                        [[scr.shape[1], 128],
                                                         [1, width]]))
                self.nc.free_register(v.val)
                return inst
            return fn
        ktok = P.emit("sync", dyn_dma(self.bp1[:, 32768:36864], self.klf_scr,
                                      oi * 16 + 8, 4096),
                      waits=[(stt, stt.count), (ve, ve.count), (pe, pe.count)],
                      incs=[(ld, 16)])
        for tt in range(2):
            for c in range(2):
                cp = slice(c * 64, (c + 1) * 64)
                ch = tt * 2 + c
                it = oi * 16 + tt * 2 + c
                stok_ = P.emit("sync", dyn_dma(self.bp1[:, 38912:39936],
                                               self.ssnap_scr, it, 1024),
                               waits=[(ve, ve.count)], incs=[(ld, 16)])
                utok = P.emit("sync", dyn_dma(self.bp1[:, 37888:38912],
                                              self.ups_scr, it + 4, 1024),
                              incs=[(ld, 16)])
                for m in range(4):
                    eqo = ch * 1024 + 4 * m * 64
                    eh = self.bp2[:, self.eqb + eqo:self.eqb + eqo + 256]
                    el = self.bp2[:, self.eqb + 4096 + eqo:
                                  self.eqb + 4096 + eqo + 256]
                    for z in range(2):
                        kv = 2 * m + z
                        base = z * 64
                        kcol = slice(m * 256 + tt * 128 + c * 64,
                                     m * 256 + tt * 128 + (c + 1) * 64)
                        klh, kll = self.klfz[z][0][:, kcol], self.klfz[z][1][:, kcol]
                        atok = self.mm3(bA[cp, 0:256], klh, kll, eh, el,
                                        start=True, stop=True,
                                        waits=[(ve, ve.count),
                                               (ld, max(ktok, utok))],
                                        incs=[(pe, 1)])
                        cmask = bass.AP(self.cst, cp.start * CONST_W + 384,
                                        [[CONST_W, 64], [0, 4], [1, 64]])
                        P.emit("vector", lambda e, cp=cp, cm=cmask: e.tensor_tensor(
                            out=self.scr1[cp, 0:256], in0=bA[cp, 0:256],
                            in1=cm, op=ALU.mult),
                            waits=[(pe, atok)])
                        attok = self.split("vector", self.scr1[cp, 0:256],
                                           self.bp1[cp, 39936:40192],
                                           self.bp1[cp, 40192:40448],
                                           incs=[(ve, 1)])
                        sh = self.bp1[:, 38912 + m * 128 + z * 64:
                                      38912 + m * 128 + z * 64 + 64]
                        sl = self.bp1[:, 39424 + m * 128 + z * 64:
                                      39424 + m * 128 + z * 64 + 64]
                        self.mm3(bO[slice(base, base + 64), 0:256], sh, sl, eh, el,
                                 start=True, stop=False)
                        uh = self.ups_hi[:, kv * 64:(kv + 1) * 64]
                        ul = self.ups_lo[:, kv * 64:(kv + 1) * 64]
                        otok = self.mm3(bO[slice(base, base + 64), 0:256], uh, ul,
                                        self.bp1[:, 39936:40192],
                                        self.bp1[:, 40192:40448],
                                        start=False, stop=True,
                                        waits=[(ve, attok)], incs=[(pe, 1)])
                    oap = bass.AP(self.fp, 4 * m * HS + tt * 128 + c * 64,
                                  [[13568, 128], [HS, 4], [1, 64]])
                    bOap = bass.AP(self.pb[7], 0, [[2048 // 4, 128], [64, 4], [1, 64]])
                    P.emit("vector", lambda e, oap=oap, bOap=bOap:
                           e.scalar_tensor_tensor(
                        out=oap, in0=bOap, scalar=0.5,
                        in1=oap, op0=ALU.mult, op1=ALU.add),
                        waits=[(pe, pe.count)], incs=[(ve, 1)])

    def st_precast(self, hs):
        P = self.P
        gld = P.sem("gld")
        jobs = []
        for ic in range(hs * 6, (hs + 1) * 6):
            jobs.append((bass.AP(self.gw_in, ic * 128 * 2048,
                                 [[2048, 128], [1, 2048]]),
                         self.gw8_scr, ic * 128 * 2048, 2048))
            jobs.append((bass.AP(self.uw_in, ic * 128 * 2048,
                                 [[2048, 128], [1, 2048]]),
                         self.uw8_scr, ic * 128 * 2048, 2048))
        for k in range(hs * 12, (hs + 1) * 12):
            jobs.append((bass.AP(self.dw_in, k * 128 * 1024,
                                 [[1024, 128], [1, 1024]]),
                         self.dw8_scr, k * 128 * 1024, 1024))
        for j, (src_ap, scr, off, w) in enumerate(jobs):
            bi = j % 2
            base = 4096 * bi
            ctok = P.emit("gpsimd", lambda e, s=src_ap, base=base, w=w:
                          e.dma_start(out=self.w8[:, base:base + w].bitcast(F8),
                                      in_=s),
                          waits=[(gld, self._pc_prev[bi])], incs=[(gld, 16)])
            self._pc_prev[bi] = P.emit(
                "gpsimd", lambda e, scr=scr, off=off, base=base, w=w:
                e.dma_start(out=bass.AP(scr, off, [[w, 128], [1, w]]),
                            in_=self.w8[:, base:base + w]),
                waits=[(gld, ctok)], incs=[(gld, 16)])

    def st_ffn(self, slot):
        P = self.P
        ld, ve, pe, sc = P.sem("ld"), P.sem("ve"), P.sem("pe"), P.sem("sc")
        gld, stt = P.sem("gld"), P.sem("st")
        f = self.fp
        outstage = [f[:, t * 2048:(t + 1) * 2048] for t in range(4)]
        gate32 = f[:, 4096:4608]
        fuse32 = self.misc[:, 0:512]
        att_re = self.misc[:, 512:1024]
        xnr_hi = self.bp2[:, 0:8192]
        xnr_lo = self.bp2[:, 8192:16384]
        oi0 = slot * 2
        for t in range(4):
            self.ln_tile(self.xg_in, slot * SEG + t * 128, t,
                         dst_hi=xnr_hi, dst_lo=xnr_lo, cstride=SEG)
        mm_hist = [0, 0]
        tok = 0
        for ic in range(NIC):
            bi = ic % 2
            base = 4096 * bi
            P.emit("sync", lambda e, ic=ic, base=base: e.dma_start(
                out=self.w8[:, base:base + 2048],
                in_=bass.AP(self.gw8_scr, ic * 128 * 2048,
                            [[2048, 128], [1, 2048]])),
                waits=[(pe, mm_hist[bi]), (gld, gld.count)] if ic < 2
                else [(pe, mm_hist[bi])],
                incs=[(ld, 16)])
            s2 = P.emit("sync", lambda e, ic=ic, base=base: e.dma_start(
                out=self.w8[:, base + 2048:base + 4096],
                in_=bass.AP(self.uw8_scr, ic * 128 * 2048,
                            [[2048, 128], [1, 2048]])),
                incs=[(ld, 16)])
            for dc in range(NC):
                lg = self.w8[:, base + dc * 128:base + (dc + 1) * 128].bitcast(F8)
                lu = self.w8[:, base + 2048 + dc * 128:
                             base + 2048 + (dc + 1) * 128].bitcast(F8)
                self.mm3(self.pb[0][:], lg, None,
                         xnr_hi[:, dc * 512:(dc + 1) * 512],
                         xnr_lo[:, dc * 512:(dc + 1) * 512],
                         start=(dc == 0), stop=(dc == NC - 1),
                         waits=[(ld, s2), (sc, sc.count), (ve, ve.count)]
                         if dc == 0 else (),
                         incs=[(pe, 1)] if dc == NC - 1 else (), passes=2)
                tok = self.mm3(self.pb[1][:], lu,
                               None, xnr_hi[:, dc * 512:(dc + 1) * 512],
                               xnr_lo[:, dc * 512:(dc + 1) * 512],
                               start=(dc == 0), stop=(dc == NC - 1),
                               incs=[(pe, 1)] if dc == NC - 1 else (), passes=2)
            mm_hist[bi] = tok
            gtok = P.emit("scalar", lambda e, ic=ic: e.activation(
                out=gate32, in_=self.pb[0][:], func=AF.Silu,
                scale=self.scl[:, ic:ic + 1]), waits=[(pe, tok)], incs=[(sc, 1)])
            P.emit("vector", lambda e, ic=ic: e.scalar_tensor_tensor(
                out=fuse32, in0=self.pb[1][:], scalar=self.scl[:, 48 + ic:49 + ic],
                in1=gate32, op0=ALU.mult, op1=ALU.mult),
                waits=[(sc, gtok), (ve, ve.count)])
            self.split("vector", fuse32,
                       self.fuse_hi[:, ic * 512:(ic + 1) * 512],
                       self.fuse_lo[:, ic * 512:(ic + 1) * 512], incs=[(ve, 1)])
        ostok = {}
        for t in range(4):
            row0 = slot * SEG + t * 128
            ostok[t] = P.emit("sync", lambda e, t=t, o=row0: e.dma_start(
                out=outstage[t], in_=self.xg_in.ap()[o:o + 128, :]),
                waits=[(pe, pe.count), (ve, ve.count), (sc, sc.count),
                       (stt, stt.count)],
                incs=[(ld, 16)])
        for dh in range(2):
            dtok = {}
            dprev = [P.sem("gld").count, P.sem("pe").count]
            dhist = [dprev[1], dprev[1]]
            for ic in range(NIC):
                bi = ic % 2
                base = 4096 * bi
                s1 = P.emit("sync", lambda e, ic=ic, dh=dh, base=base:
                            e.dma_start(
                    out=self.w8[:, base:base + 1024],
                    in_=bass.AP(self.dw8_scr, (dh * NIC + ic) * 128 * 1024,
                                [[1024, 128], [1, 1024]])),
                    waits=[(pe, dhist[bi])], incs=[(ld, 16)])
                for od in range(8):
                    ldw = self.w8[:, base + od * 128:
                                  base + (od + 1) * 128].bitcast(F8)
                    tok = self.mm3(self.pb[od][:],
                                   ldw, None,
                                   self.fuse_hi[:, ic * 512:(ic + 1) * 512],
                                   self.fuse_lo[:, ic * 512:(ic + 1) * 512],
                                   start=(ic == 0), stop=(ic == NIC - 1),
                                   waits=[(ld, s1), (ve, ve.count)]
                                   if ic == 0 else ([(ld, s1)] if od == 0 else ()),
                                   incs=[(pe, 1)] if (ic == NIC - 1 or od == 7)
                                   else (), passes=2)
                    if ic == NIC - 1:
                        dtok[od] = tok
                dhist[bi] = tok
            for od in range(8):
                dchunk = dh * 8 + od
                atok = P.emit("sync", lambda e, dchunk=dchunk: e.dma_start(
                    out=att_re, in_=bass.AP(self.attn_scr,
                                            oi0 * 4096 + dchunk * 256,
                                            [[16384, 128], [4096, 2], [1, 256]])),
                    waits=[(ve, ve.count)], incs=[(ld, 16)])
                ttok = P.emit("vector", lambda e, od=od, dchunk=dchunk:
                              e.scalar_tensor_tensor(
                    out=self.stage[:, 0:512], in0=self.pb[od][:],
                    scalar=self.scl[:, 96 + dchunk:97 + dchunk], in1=att_re,
                    op0=ALU.mult, op1=ALU.add),
                    waits=[(pe, dtok[od]), (ld, atok)], incs=[(ve, 1)])
                for t in range(4):
                    ptok = P.emit("tensor", lambda e, od=od, t=t: e.transpose(
                        self.pb[od][:, 0:128],
                        self.stage[:, t * 128:(t + 1) * 128],
                        self.cst[:, C_IDENT]),
                        waits=[(ve, ve.count), (pe, pe.count)], incs=[(pe, 1)])
                    P.emit("vector", lambda e, od=od, t=t, dchunk=dchunk:
                           e.tensor_add(
                        out=outstage[t][:, dchunk * 128:(dchunk + 1) * 128],
                        in0=self.pb[od][:, 0:128],
                        in1=outstage[t][:, dchunk * 128:(dchunk + 1) * 128]),
                        waits=[(pe, ptok), (ld, ostok[t])], incs=[(ve, 1)])
        for t in range(4):
            row0 = slot * SEG + t * 128
            P.emit("gpsimd", lambda e, t=t, row0=row0: e.dma_start(
                out=self.out.ap()[row0:row0 + 128, :], in_=outstage[t]),
                waits=[(ve, ve.count)], incs=[(stt, 16)])

    def finish(self):
        P = self.P
        stt = P.sem("st")
        for name, t, ap in self.taps:
            P.emit("gpsimd", lambda e, t=t, ap=ap: e.dma_start(out=t.ap(), in_=ap),
                   waits=P.barrier_counts(), incs=[(stt, 16)])
        P.emit("sync", lambda e: None, waits=[(stt, stt.count)])

    def build(self, upto=None):
        self.alloc()
        self.st_const()
        for hs in range(NHS):
            self.st_xn(hs)
            self.st_proj(hs)
            self.st_prep(hs)
            self.st_delta(hs)
            self.st_precast(hs)
            if upto is not None and hs >= upto:
                self.finish()
                self.P.build_into(self.nc, self.ctx)
                return self.nc
            if hs == 3:
                self.st_slot(0)
            if hs == 7:
                self.st_slot(1)
        self.st_ffn(0)
        self.st_ffn(1)
        self.finish()
        self.P.build_into(self.nc, self.ctx)
        return self.nc


def make_masks(own):
    m = np.zeros((128, 44 * 256), dtype=ml_dtypes.bfloat16)
    base = CoreBuilder.MASK_BASE
    njts = CoreBuilder.NJT
    j = np.arange(128)[:, None]
    r = np.arange(256)[None, :]
    for slot in range(2):
        for h2 in range(2):
            oi = slot * 2 + h2
            g = own[slot]
            ghs = 2 * g + h2
            for jt in range(njts[slot][h2]):
                tile = ((ghs * 256 + r) >= (jt * 128 + j))
                off = (base[oi] + jt) * 256
                m[:, off:off + 256] = tile.astype(ml_dtypes.bfloat16)
    return m


def make_gofs(own):
    g = np.zeros((1, 64), dtype=np.int32)
    for slot in range(2):
        for h2 in range(2):
            oi = slot * 2 + h2
            ghs = 2 * own[slot] + h2
            for tt in range(2):
                for c in range(2):
                    gc = 4 * ghs + 2 * tt + c
                    g[0, oi * 16 + tt * 2 + c] = gc * 1024
                    g[0, oi * 16 + 4 + tt * 2 + c] = gc * 1024
            g[0, oi * 16 + 8] = ghs * 4096
    return g


_CACHE = {}


def _get_nc():
    if "nc" not in _CACHE:
        _CACHE["nc"] = CoreBuilder(0).build()
    return _CACHE["nc"]


def _shard_inputs(inputs):
    perm = QCOL_PERM
    consts = make_consts()
    base = {
        "wq": np.ascontiguousarray(inputs["L0_attn_Wq"][:, perm]),
        "wk": np.ascontiguousarray(inputs["L0_attn_Wk"]),
        "wv": np.ascontiguousarray(inputs["L0_attn_Wv"]),
        "wo": np.ascontiguousarray(inputs["L0_attn_Wo"][perm, :]),
        "gw": np.ascontiguousarray(
            inputs["L0_ffn_gate_w"].reshape(16, 128, NIC, 128)
            .transpose(2, 1, 0, 3).reshape(NIC * 128, 2048)),
        "uw": np.ascontiguousarray(
            inputs["L0_ffn_up_w"].reshape(16, 128, NIC, 128)
            .transpose(2, 1, 0, 3).reshape(NIC * 128, 2048)),
        "dw": np.ascontiguousarray(
            inputs["L0_ffn_down_w"].reshape(NIC, 128, 2, 1024)
            .transpose(2, 0, 1, 3).reshape(2 * NIC * 128, 1024)),
        "gs": np.ascontiguousarray(inputs["L0_ffn_gate_scale"]),
        "us": np.ascontiguousarray(inputs["L0_ffn_up_scale"]),
        "ds": np.ascontiguousarray(inputs["L0_ffn_down_scale"]),
        "consts": consts,
    }
    masks = {h: make_masks(OWN[h]) for h in (0, 1)}
    gofs = {h: make_gofs(OWN[h]) for h in (0, 1)}
    x = inputs["L0_attn_hidden_states"]
    in_maps = []
    for core in range(8):
        b, h = core // 2, core % 2
        m = dict(base)
        m["x"] = np.ascontiguousarray(x[b])
        rows = np.r_[OWN[h][0] * SEG:(OWN[h][0] + 1) * SEG,
                     OWN[h][1] * SEG:(OWN[h][1] + 1) * SEG]
        m["xg"] = np.ascontiguousarray(x[b][rows])
        m["masks"] = masks[h]
        m["gofs"] = gofs[h]
        in_maps.append(m)
    return in_maps


def kernel(L0_attn_hidden_states, L0_attn_Wq, L0_attn_Wk, L0_attn_Wv, L0_attn_Wo,
           L0_ffn_gate_w, L0_ffn_up_w, L0_ffn_down_w, L0_ffn_gate_scale,
           L0_ffn_up_scale, L0_ffn_down_scale):
    inputs = dict(
        L0_attn_hidden_states=np.asarray(L0_attn_hidden_states),
        L0_attn_Wq=np.asarray(L0_attn_Wq), L0_attn_Wk=np.asarray(L0_attn_Wk),
        L0_attn_Wv=np.asarray(L0_attn_Wv), L0_attn_Wo=np.asarray(L0_attn_Wo),
        L0_ffn_gate_w=np.asarray(L0_ffn_gate_w),
        L0_ffn_up_w=np.asarray(L0_ffn_up_w),
        L0_ffn_down_w=np.asarray(L0_ffn_down_w),
        L0_ffn_gate_scale=np.asarray(L0_ffn_gate_scale),
        L0_ffn_up_scale=np.asarray(L0_ffn_up_scale),
        L0_ffn_down_scale=np.asarray(L0_ffn_down_scale))
    in_maps = _shard_inputs(inputs)
    nc = _get_nc()
    res = run_bass_kernel_spmd(nc, in_maps, core_ids=list(range(8)))
    out = np.empty((B, N, DM), dtype=np.float32)
    for core in range(8):
        b, h = core // 2, core % 2
        o = res.results[core]["out"]
        for slot in range(2):
            g = OWN[h][slot]
            out[b, g * SEG:(g + 1) * SEG] = o[slot * SEG:(slot + 1) * SEG]
    return out


# version 07 (final)
# speedup vs baseline: 1.1183x; 1.0010x over previous
import numpy as np
import ml_dtypes
import concourse.bass as bass
import concourse.mybir as mybir
from concourse.bass_utils import run_bass_kernel_spmd
from contextlib import ExitStack

F32 = mybir.dt.float32
BF16 = mybir.dt.bfloat16
F8 = mybir.dt.float8e4
I8 = mybir.dt.int8
AF = mybir.ActivationFunctionType
ALU = mybir.AluOpType
AX = mybir.AxisListType

B, N, DM = 4, 2048, 2048
H, HD, HKV = 32, 64, 8
SEG = 512
HS = 256
NHS = N // HS
NC = DM // 128
NI = 6144
NIC = NI // 128
OWN = {0: (0, 3), 1: (1, 2)}

QA = [8 * (j // 4) + j % 4 for j in range(16)]
QB = [8 * (j // 4) + 4 + j % 4 for j in range(16)]
QCOL_PERM = np.concatenate(
    [np.r_[QA[j] * 64:QA[j] * 64 + 64, QB[j] * 64:QB[j] * 64 + 64]
     for j in range(16)])

ENGS = ("sync", "tensor", "vector", "scalar", "gpsimd")


class Sem:
    __slots__ = ("name", "count", "handle")

    def __init__(self, name):
        self.name = name
        self.count = 0
        self.handle = None


class Prog:
    def __init__(self):
        self.ops = {e: [] for e in ENGS}
        self.sems = {}

    def sem(self, name):
        if name not in self.sems:
            self.sems[name] = Sem(name)
        return self.sems[name]

    def emit(self, eng, fn, waits=(), incs=()):
        incs = tuple(incs)
        for s, n in incs:
            s.count += n
        self.ops[eng].append((fn, tuple(waits), incs))
        return incs[0][0].count if incs else None

    def barrier_counts(self):
        return [(s, s.count) for s in self.sems.values() if s.count > 0]

    def build_into(self, nc, ctx):
        for s in self.sems.values():
            s.handle = ctx.enter_context(nc.semaphore(s.name))
        block = ctx.enter_context(nc.Block())
        for eng in ENGS:
            ops = self.ops[eng]

            def emit_all(engine, ops=ops):
                seen = {}
                for fn, waits, incs in ops:
                    for s, v in waits:
                        if seen.get(s.name, -1) < v:
                            engine.wait_ge(s.handle, v)
                            seen[s.name] = v
                    inst = fn(engine)
                    for s, n in incs:
                        if inst is None:
                            inst = engine.nop()
                        inst.then_inc(s.handle, n)
                        inst = None
            getattr(block, eng)(emit_all)


C_IDENT = slice(0, 128)
C_ONES = slice(128, 256)
C_STRICTL2 = slice(256, 320)
C_STRICTT2 = slice(320, 384)
C_CMASK2 = slice(384, 448)
C_CM256A = slice(448, 704)
C_CM256B = slice(704, 960)
C_HONES = slice(960, 962)
C_E0 = slice(962, 1026)
CONST_W = 1088


def make_consts():
    c = np.zeros((128, CONST_W), dtype=np.float32)
    p = np.arange(128)[:, None]
    f64 = np.arange(64)[None, :]
    f256 = np.arange(256)[None, :]
    c[:, C_IDENT] = np.eye(128)
    c[:, C_ONES] = 1.0
    c[:, C_STRICTL2] = (f64 < (p % 64)).astype(np.float32)
    c[:, C_STRICTT2] = ((p % 64) < f64).astype(np.float32)
    c[:, C_CMASK2] = (f64 >= (p % 64)).astype(np.float32)
    c[:, C_CM256A] = (f256 >= p).astype(np.float32)
    c[:, C_CM256B] = (f256 >= p + 128).astype(np.float32)
    c[0:64, 960:961] = 1.0
    c[64:128, 961:962] = 1.0
    c[0:1, 962:1026] = 1.0
    return c


class CoreBuilder:
    def __init__(self, half):
        self.half = half
        self.own = OWN[half]
        self.nc = bass.Bass("TRN2", target_bir_lowering=False,
                            detect_race_conditions=False)
        self.P = Prog()
        self.ctx = ExitStack()
        self.taps = []
        self._pc_prev = [0, 0]

    def sb(self, name, shape, dt):
        return self.ctx.enter_context(self.nc.sbuf_tensor(name, list(shape), dt))

    def ps(self, name, shape, dt=F32):
        return self.ctx.enter_context(self.nc.psum_tensor(name, list(shape), dt))

    def tap(self, name, ap, shape, dt=F32):
        t = self.nc.dram_tensor(name, list(shape), dt, kind="ExternalOutput")
        self.taps.append((name, t, ap))

    def split(self, eng, src, hi, lo, waits=(), incs=()):
        self.P.emit(eng, lambda e, s=src, h=hi: e.tensor_copy(out=h, in_=s),
                    waits=waits)
        return self.P.emit(eng, lambda e, s=src, h=hi, l=lo: e.tensor_tensor(
            out=l, in0=s, in1=h, op=ALU.subtract), incs=incs)

    def mm3(self, psum, lh, ll, rh, rl, start, stop, tile_position=None,
            waits=(), incs=(), passes=3):
        P = self.P
        tp = tile_position
        P.emit("tensor", lambda e: e.matmul(psum, lh, rh, start=start, stop=False,
                                            tile_position=tp), waits=waits)
        if passes == 2:
            return P.emit("tensor", lambda e: e.matmul(
                psum, lh, rl, start=False, stop=stop, tile_position=tp), incs=incs)
        P.emit("tensor", lambda e: e.matmul(psum, lh, rl, start=False, stop=False,
                                            tile_position=tp))
        return P.emit("tensor", lambda e: e.matmul(
            psum, ll, rh, start=False, stop=stop, tile_position=tp), incs=incs)

    def reset_pe(self, waits=()):
        return self.P.emit("tensor", lambda e: e.ldweights(self.identb[:]),
                           waits=waits)

    def alloc(self):
        nc = self.nc
        din = lambda n, s, d=F32: nc.dram_tensor(n, list(s), d, kind="ExternalInput")
        self.x_in = din("x", [N, DM])
        self.wq_in = din("wq", [DM, DM])
        self.wk_in = din("wk", [DM, 512])
        self.wv_in = din("wv", [DM, 512])
        self.wo_in = din("wo", [DM, DM])
        self.gw_in = din("gw", [NIC * 128, 2048])
        self.uw_in = din("uw", [NIC * 128, 2048])
        self.dw_in = din("dw", [2 * NIC * 128, 1024])
        self.gs_in = din("gs", [NI])
        self.us_in = din("us", [NI])
        self.ds_in = din("ds", [DM])
        self.const_in = din("consts", [128, CONST_W])
        self.xg_in = din("xg", [2 * SEG, DM])
        self.masks_in = nc.dram_tensor("masks", [128, 44 * 256], BF16,
                                       kind="ExternalInput")
        self.gofs_in = nc.dram_tensor("gofs", [1, 64], mybir.dt.int32,
                                      kind="ExternalInput")
        self.out = nc.dram_tensor("out", [2 * SEG, DM], F32, kind="ExternalOutput")
        self.attn_scr = nc.dram_tensor("attn_scr", [128, 4 * NC * HS], F32)
        self.gw8_scr = nc.dram_tensor("gw8_scr", [NIC * 128, 2048], I8)
        self.uw8_scr = nc.dram_tensor("uw8_scr", [NIC * 128, 2048], I8)
        self.dw8_scr = nc.dram_tensor("dw8_scr", [2 * NIC * 128, 1024], I8)
        self.ssnap_scr = nc.dram_tensor("ssnap_scr", [128, 32 * 1024], BF16)
        self.ups_scr = nc.dram_tensor("ups_scr", [128, 32 * 1024], BF16)
        self.klf_scr = nc.dram_tensor("klf_scr", [128, 8 * 4096], BF16)

        self.cst = self.sb("cst", [128, CONST_W], F32)
        self.onesb = self.sb("onesb", [128, 2], BF16)
        self.identb = self.sb("identb", [128, 128], BF16)
        self.fconst = self.sb("fconst", [128, 4], F32)
        self.lnstat = self.sb("lnstat", [128, 4], F32)
        self.scl = self.sb("scl", [128, 112], F32)
        self.bp1 = self.sb("bp1", [128, 49152], BF16)
        t = self.bp1
        self.k_hi = t[:, 0:8192]; self.k_lo = t[:, 8192:16384]
        self.v_hi = t[:, 16384:24576]; self.v_lo = t[:, 24576:32768]
        self.xs_hi = t[:, 32768:36864]; self.xs_lo = t[:, 36864:40960]
        self.klfz = [(t[:, 32768:33792], t[:, 33792:34816]),
                     (t[:, 34816:35840], t[:, 35840:36864])]
        self.wTs_hi = t[:, 36864:37376]; self.wTs_lo = t[:, 37376:37888]
        self.ups_hi = t[:, 37888:38400]; self.ups_lo = t[:, 38400:38912]
        self.Sd_hi = t[:, 38912:39424]; self.Sd_lo = t[:, 39424:39936]
        self.slvP = t[:, 39936:40064]; self.slvPT = t[:, 40064:40192]
        self.slvP2 = t[:, 40192:40320]; self.slvPT2 = t[:, 40320:40448]
        self.ybf = t[:, 40448:40576]
        self.Pt_hi = t[:, 32768:33280]; self.Pt_lo = t[:, 33280:33792]
        self.q_hi = t[:, 40960:45056]; self.q_lo = t[:, 45056:49152]
        self.eq_hi = t[:, 45056 - 4096:45056 - 2048] if False else None
        self.fuse_hi = t[:, 0:24576]; self.fuse_lo = t[:, 24576:49152]
        self.bp2 = self.sb("bp2", [128, 21504], BF16)
        u = self.bp2
        self.kbf_hi = u[:, 0:1024]; self.kbf_lo = u[:, 1024:2048]
        self.kltm_hi = u[:, 2048:3072]; self.kltm_lo = u[:, 3072:4096]
        self.kbtm_hi = u[:, 4096:5120]; self.kbtm_lo = u[:, 5120:6144]
        self.vbtm_hi = u[:, 6144:7168]; self.vbtm_lo = u[:, 7168:8192]
        self.oc_hi = u[:, 0:4096]; self.oc_lo = u[:, 4096:8192]
        self.wsp_hi = u[:, 9216:11264]; self.wsp_lo_x = u[:, 11264:13312]
        self.mkb = 13312
        self.eqb = 0
        self.eq_hi = u[:, 0:4096]; self.eq_lo = u[:, 4096:8192]
        self.xnr_hi = u[:, 0:8192]; self.xnr_lo = u[:, 8192:16384]
        self.w8 = self.sb("w8", [128, 8192], I8)
        self.fp = self.sb("fp", [128, 13568], F32)
        f = self.fp
        self.xt = [f[:, 0:2048], f[:, 2048:4096]]
        self.o_comb = f[:, 0:4096]
        self.sq_scr = f[:, 4096:6144]
        self.k_raw = f[:, 4096:5120]; self.v_raw = f[:, 5120:6144]
        self.ektmp = f[:, 6144:7168]
        self.gltm = f[:, 6144:7168]
        self.klf32 = f[:, 7168:8192]
        self.gl32 = f[:, 8192:9216]
        self.scr1 = f[:, 9216:10240]
        self.uw_f = f[:, 10240:11264]
        self.pexp = f[:, 11264:11776]
        self.wstrip = [f[:, 6144:7168], f[:, 7168:8192]]
        self.eqt = f[:, 4096:6144]
        self.stage = f[:, 11264:11776]
        self.misc = f[:, 11776:13568]
        self.rcpA = self.misc[0:1, 0:256]
        self.rcpB = self.misc[0:1, 256:512]
        self.bcast_sb = self.misc[:, 512:768]
        self.t1sb = self.misc[:, 768:1024]
        self.att_st = self.misc[:, 1024:1536]
        self.S32 = self.misc[:, 1536:1792]
        self.gofs_sb = self.sb("gofs_sb", [1, 64], mybir.dt.int32)
        self.pb = [self.ps(f"pb{i}", [128, 512]) for i in range(8)]

    def st_const(self):
        P = self.P
        ld = P.sem("ld")
        ve = P.sem("ve")
        P.emit("sync", lambda e: e.dma_start(out=self.cst[:], in_=self.const_in.ap()),
               incs=[(ld, 16)])
        P.emit("sync", lambda e: e.dma_start(
            out=self.scl[:, 0:48],
            allow_slow_non_contiguous=True,
            in_=bass.AP(self.gs_in, 0, [[1, 128], [128, 48]])), incs=[(ld, 16)])
        P.emit("sync", lambda e: e.dma_start(
            out=self.scl[:, 48:96],
            allow_slow_non_contiguous=True,
            in_=bass.AP(self.us_in, 0, [[1, 128], [128, 48]])), incs=[(ld, 16)])
        P.emit("sync", lambda e: e.dma_start(
            out=self.scl[:, 96:112],
            allow_slow_non_contiguous=True,
            in_=bass.AP(self.ds_in, 0, [[1, 128], [128, 16]])), incs=[(ld, 16)])
        tok = P.emit("sync", lambda e: e.dma_start(
            out=self.gofs_sb[:], in_=self.gofs_in.ap()), incs=[(ld, 16)])
        P.emit("vector", lambda e: e.memset(self.onesb[:], 1.0), waits=[(ld, tok)])
        P.emit("vector", lambda e: e.memset(self.fconst[:, 0:1], 1e-5))
        P.emit("vector", lambda e: e.tensor_copy(out=self.identb[:],
                                                 in_=self.cst[:, C_IDENT]))
        P.emit("vector", lambda e: e.memset(self.misc[:], 0.0))
        P.emit("vector", lambda e: e.memset(self.Sd_hi[:], 0.0))
        P.emit("vector", lambda e: e.memset(self.Sd_lo[:], 0.0))
        P.emit("vector", lambda e: e.memset(self.slvP[:], 0.0))
        P.emit("vector", lambda e: e.memset(self.slvPT[:], 0.0))
        P.emit("vector", lambda e: e.memset(self.slvP2[:], 0.0))
        P.emit("vector", lambda e: e.memset(self.slvPT2[:], 0.0))
        P.emit("vector", lambda e: e.memset(self.ups_hi[:], 0.0))
        P.emit("vector", lambda e: e.memset(self.ups_lo[:], 0.0))
        P.emit("vector", lambda e: e.memset(self.fconst[0:64, 1:2], 1.0))
        P.emit("vector", lambda e: e.memset(self.fconst[64:128, 1:2], 0.0))
        P.emit("vector", lambda e: e.memset(self.fconst[0:64, 2:3], 0.0))
        vtok = P.emit("vector", lambda e: e.memset(self.fconst[64:128, 2:3], 1.0),
                      incs=[(ve, 1)])
        for e in ENGS:
            P.emit(e, lambda en: None, waits=[(ld, tok), (ve, vtok)])

    def ln_tile(self, src_dram, row0, t, dst_hi=None, dst_lo=None, cstride=HS):
        P = self.P
        ld, ve, pe, sc = P.sem("ld"), P.sem("ve"), P.sem("pe"), P.sem("sc")
        hzv, hzs = P.sem("hzv"), P.sem("hzs")
        if dst_hi is None:
            dst_hi, dst_lo = self.xs_hi, self.xs_lo
        buf = self.xt[t % 2]
        tok_ld = P.emit("sync", lambda e, b=buf, o=row0, s=src_dram: e.dma_start(
            out=b, in_=s.ap()[o:o + 128, :]),
            waits=[(pe, pe.count), (P.sem("st"), P.sem("st").count)],
            incs=[(ld, 16)])
        st = self.lnstat
        h1 = P.emit("vector", lambda e, b=buf: e.reduce_sum(
            out=st[:, 0:1], in_=b, axis=AX.X, negate=True),
            waits=[(ld, tok_ld)], incs=[(hzv, 1)])
        h2 = P.emit("vector", lambda e: e.tensor_scalar_mul(
            out=st[:, 1:2], in0=st[:, 0:1], scalar1=1.0 / DM),
            waits=[(hzv, h1)], incs=[(hzv, 1)])
        vtok = P.emit("vector", lambda e, b=buf: e.tensor_scalar(
            out=b, in0=b, scalar1=st[:, 1:2], scalar2=None, op0=ALU.add),
            waits=[(hzv, h2)], incs=[(ve, 1)])
        h3 = P.emit("scalar", lambda e, b=buf: e.activation(
            out=self.sq_scr, in_=b, func=AF.Square, accum_out=st[:, 2:3]),
            waits=[(ve, vtok)], incs=[(hzs, 1)])
        h4 = P.emit("scalar", lambda e: e.activation(
            out=st[:, 3:4], in_=st[:, 2:3], func=AF.Ln, scale=1.0 / DM,
            bias=self.fconst[:, 0:1]), waits=[(hzs, h3)], incs=[(hzs, 1)])
        stok = P.emit("scalar", lambda e: e.activation(
            out=st[:, 3:4], in_=st[:, 3:4], func=AF.Exp, scale=-0.5),
            waits=[(hzs, h4)], incs=[(sc, 1)])
        vtok = P.emit("vector", lambda e, b=buf: e.tensor_scalar_mul(
            out=b, in0=b, scalar1=st[:, 3:4]), waits=[(sc, stok)],
            incs=[(ve, 1)])
        for c in range(NC):
            bank = self.pb[c % 2]
            ptok = P.emit("tensor", lambda e, b=buf, c=c, bank=bank: e.transpose(
                bank[:, 0:128], b[:, c * 128:(c + 1) * 128], self.cst[:, C_IDENT]),
                waits=[(ve, ve.count)], incs=[(pe, 1)])
            lo_c = c * cstride + t * 128
            self.split("vector", bank[:, 0:128],
                       dst_hi[:, lo_c:lo_c + 128], dst_lo[:, lo_c:lo_c + 128],
                       waits=[(pe, ptok)], incs=[(ve, 1)])

    def st_xn(self, hs):
        for t in range(2):
            self.ln_tile(self.x_in, hs * HS + t * 128, t)

    def st_proj(self, hs):
        P = self.P
        ld, ve, pe, sc = P.sem("ld"), P.sem("ve"), P.sem("pe"), P.sem("sc")
        kv_tok = {}

        def src(dc):
            return bass.AP(self.wk_in, dc * 128 * 512, [[512, 128], [1, 512]]), \
                   bass.AP(self.wv_in, dc * 128 * 512, [[512, 128], [1, 512]])

        last_mm = [pe.count, pe.count]
        last_split = [0, 0]
        for dc in range(NC):
            bi = dc % 2
            sbuf = self.wstrip[bi]
            sph = self.wsp_hi[:, bi * 1024:(bi + 1) * 1024]
            spl = self.wsp_lo_x[:, bi * 1024:(bi + 1) * 1024]
            ksrc, vsrc = src(dc)
            P.emit("sync", lambda e, b=sbuf, s=ksrc: e.dma_start(
                out=b[:, 0:512], in_=s),
                waits=[(ve, last_split[bi]), (pe, last_mm[bi])], incs=[(ld, 16)])
            tok_ld = P.emit("sync", lambda e, b=sbuf, s=vsrc: e.dma_start(
                out=b[:, 512:1024], in_=s), incs=[(ld, 16)])
            vtok = self.split("vector", sbuf, sph, spl,
                              waits=[(ld, tok_ld), (pe, last_mm[bi])], incs=[(ve, 1)])
            last_split[bi] = vtok
            for pc in range(4):
                lh = sph[:, pc * 128:(pc + 1) * 128]
                ll = spl[:, pc * 128:(pc + 1) * 128]
                rh = self.xs_hi[:, dc * HS:(dc + 1) * HS]
                rl = self.xs_lo[:, dc * HS:(dc + 1) * HS]
                tok = self.mm3(self.pb[pc][:, 0:256], lh, ll, rh, rl,
                               start=(dc == 0), stop=(dc == NC - 1),
                               waits=[(ve, vtok)],
                               incs=[(pe, 1)])
                if dc == NC - 1:
                    kv_tok[("k", pc)] = tok
            for tt in range(2):
                lh = self.xs_hi[:, dc * HS + tt * 128: dc * HS + (tt + 1) * 128]
                ll = self.xs_lo[:, dc * HS + tt * 128: dc * HS + (tt + 1) * 128]
                rh = sph[:, 512:1024]
                rl = spl[:, 512:1024]
                tok = self.mm3(self.pb[4 + tt][:], lh, ll, rh, rl,
                               start=(dc == 0), stop=(dc == NC - 1),
                               incs=[(pe, 1)])
                if dc == NC - 1:
                    kv_tok[("v", tt)] = tok
            last_mm[bi] = tok
        for pc in range(4):
            bank = self.pb[pc][:, 0:256]
            w = [(pe, kv_tok[("k", pc)])]
            stok = P.emit("scalar", lambda e, b=bank, pc=pc: e.activation(
                out=self.k_raw[:, pc * 256:(pc + 1) * 256], in_=b, func=AF.Identity),
                waits=w, incs=[(sc, 1)])
            co = pc * 2048 + hs * HS
            self.split("vector", bank, self.k_hi[:, co:co + 256],
                       self.k_lo[:, co:co + 256], waits=w + [(sc, stok)],
                       incs=[(ve, 1)])
        for tt in range(2):
            bank = self.pb[4 + tt][:]
            w = [(pe, kv_tok[("v", tt)])]
            stok = P.emit("scalar", lambda e, b=bank, tt=tt: e.activation(
                out=self.v_raw[:, tt * 512:(tt + 1) * 512], in_=b, func=AF.Identity),
                waits=w, incs=[(sc, 1)])
            co = (hs * 2 + tt) * 512
            self.split("vector", bank, self.v_hi[:, co:co + 512],
                       self.v_lo[:, co:co + 512], waits=w + [(sc, stok)],
                       incs=[(ve, 1)])

    def bcast_recip(self, src, waits=(), w=256):
        P = self.P
        pe, hzv, ve = P.sem("pe"), P.sem("hzv"), P.sem("ve")
        bsA, bsB, bbc = self.pb[5], self.pb[6], self.pb[7]
        t1 = P.emit("tensor", lambda e, s=src, w=w: e.matmul(
            bsA[0:1, 0:w], self.cst[:, 960:961], s, start=True, stop=True),
            waits=waits, incs=[(pe, 1)])
        t2 = P.emit("tensor", lambda e, s=src, w=w: e.matmul(
            bsB[0:1, 0:w], self.cst[:, 961:962], s, start=True, stop=True),
            incs=[(pe, 1)])
        v1 = P.emit("vector", lambda e, w=w: e.reciprocal(
            out=self.misc[0:1, 0:w], in_=bsA[0:1, 0:w]),
            waits=[(pe, t2)], incs=[(hzv, 1)])
        v2 = P.emit("vector", lambda e, w=w: e.reciprocal(
            out=self.misc[0:1, w:2 * w], in_=bsB[0:1, 0:w]),
            waits=[(hzv, v1)], incs=[(hzv, 1)])
        P.emit("tensor", lambda e, w=w: e.matmul(
            bbc[0:64, 0:w], self.cst[:, C_E0], self.misc[:, 0:w],
            start=True, stop=True), waits=[(hzv, v2), (ve, ve.count)])
        btok = P.emit("tensor", lambda e, w=w: e.matmul(
            bbc[64:128, 0:w], self.cst[:, C_E0], self.misc[:, w:2 * w],
            start=True, stop=True), incs=[(pe, 1)])
        return bbc[:, 0:w], btok

    def st_prep(self, hs):
        P = self.P
        ld, ve, pe, sc = P.sem("ld"), P.sem("ve"), P.sem("pe"), P.sem("sc")
        hzs = P.sem("hzs")
        etok = P.emit("scalar", lambda e: e.activation(
            out=self.ektmp, in_=self.k_raw, func=AF.Exp),
            waits=[(sc, sc.count), (ve, ve.count)], incs=[(sc, 1)])
        for pc in (0, 2):
            bb, btok = self.bcast_recip(self.ektmp[:, pc * 256:(pc + 2) * 256],
                                        waits=[(sc, etok), (ve, ve.count)], w=512)
            P.emit("vector", lambda e, pc=pc, bb=bb: e.tensor_tensor(
                out=self.klf32[:, pc * 256:(pc + 2) * 256],
                in0=self.ektmp[:, pc * 256:(pc + 2) * 256], in1=bb, op=ALU.mult),
                waits=[(pe, btok)])
            for z in range(2):
                hi, lo = self.klfz[z]
                msk = self.fconst[:, 1 + z:2 + z]
                sl = slice(pc * 256, (pc + 2) * 256)
                P.emit("vector", lambda e, sl=sl, hi=hi, msk=msk: e.tensor_scalar_mul(
                    out=hi[:, sl], in0=self.klf32[:, sl], scalar1=msk))
                P.emit("vector", lambda e, sl=sl, hi=hi, lo=lo, msk=msk:
                       e.scalar_tensor_tensor(
                    out=lo[:, sl], in0=self.klf32[:, sl], scalar=msk, in1=hi[:, sl],
                    op0=ALU.mult, op1=ALU.subtract), incs=[(ve, 1)])
        g1 = P.emit("scalar", lambda e: e.activation(
            out=self.scr1, in_=self.k_raw, func=AF.Sigmoid), incs=[(hzs, 1)])
        gtok = P.emit("scalar", lambda e: e.activation(
            out=self.gl32, in_=self.scr1, func=AF.Ln), waits=[(hzs, g1)],
            incs=[(sc, 1)])
        P.emit("vector", lambda e: e.scalar_tensor_tensor(
            out=self.scr1, in0=self.klf32, scalar=1.0 / 16.0, in1=self.gl32,
            op0=ALU.mult, op1=ALU.mult), waits=[(sc, gtok), (ve, ve.count)])
        kbtok = self.split("vector", self.scr1, self.kbf_hi, self.kbf_lo,
                           incs=[(ve, 1)])
        for pc in range(4):
            for tt in range(2):
                bank = self.pb[(pc * 2 + tt) % 2]
                srcT = self.klf32[:, pc * 256 + tt * 128: pc * 256 + (tt + 1) * 128]
                ptok = P.emit("tensor", lambda e, s=srcT, bank=bank: e.transpose(
                    bank[:, 0:128], s, self.cst[:, C_IDENT]),
                    waits=[(ve, ve.count), (sc, sc.count)], incs=[(pe, 1)])
                dst = 512 * tt + pc * 128
                stok = P.emit("scalar", lambda e, bank=bank, dst=dst: e.activation(
                    out=self.scr1[:, dst:dst + 128], in_=bank[:, 0:128],
                    func=AF.Identity), waits=[(pe, ptok), (ve, kbtok)], incs=[(sc, 1)])
                self.split("vector", bank[:, 0:128],
                           self.kltm_hi[:, dst:dst + 128],
                           self.kltm_lo[:, dst:dst + 128],
                           waits=[(pe, ptok), (sc, stok)], incs=[(ve, 1)])
        for pc in range(4):
            for tt in range(2):
                bank = self.pb[(pc * 2 + tt) % 2 + 2]
                srcT = self.gl32[:, pc * 256 + tt * 128: pc * 256 + (tt + 1) * 128]
                ptok = P.emit("tensor", lambda e, s=srcT, bank=bank: e.transpose(
                    bank[:, 0:128], s, self.cst[:, C_IDENT]),
                    waits=[(ve, ve.count), (sc, sc.count)], incs=[(pe, 1)])
                dst = 512 * tt + pc * 128
                P.emit("scalar", lambda e, bank=bank, dst=dst: e.activation(
                    out=self.gltm[:, dst:dst + 128], in_=bank[:, 0:128],
                    func=AF.Identity), waits=[(pe, ptok)], incs=[(sc, 1)])
        P.emit("vector", lambda e: e.scalar_tensor_tensor(
            out=self.klf32, in0=self.scr1, scalar=1.0 / 16.0, in1=self.gltm,
            op0=ALU.mult, op1=ALU.mult),
            waits=[(sc, sc.count), (ve, ve.count), (pe, pe.count)])
        self.split("vector", self.klf32, self.kbtm_hi, self.kbtm_lo, incs=[(ve, 1)])
        P.emit("vector", lambda e: e.scalar_tensor_tensor(
            out=self.k_raw, in0=self.v_raw, scalar=1.0 / 16.0, in1=self.gltm,
            op0=ALU.mult, op1=ALU.mult))
        self.split("vector", self.k_raw, self.vbtm_hi, self.vbtm_lo, incs=[(ve, 1)])
        stt = P.sem("st")
        for z in range(2):
            hi, lo = self.klfz[z]
            P.emit("gpsimd", lambda e, hs=hs, z=z, hi=hi: e.dma_start(
                out=self.klf_scr.ap()[:, hs * 4096 + z * 2048: hs * 4096 + z * 2048 + 1024],
                in_=hi), waits=[(ve, ve.count)], incs=[(stt, 16)])
            P.emit("gpsimd", lambda e, hs=hs, z=z, lo=lo: e.dma_start(
                out=self.klf_scr.ap()[:, hs * 4096 + z * 2048 + 1024: hs * 4096 + (z + 1) * 2048],
                in_=lo), incs=[(stt, 16)])

    def st_delta(self, hs):
        for tt in range(2):
            self.delta_solve(hs, tt)
            for c in range(2):
                self.delta_scan(hs, tt, c)

    def delta_solve(self, hs, tt):
        P = self.P
        ve, pe = P.sem("ve"), P.sem("pe")
        bL, bY, bSq, bWT = self.pb[0], self.pb[1], self.pb[2], self.pb[3]
        P.emit("vector", lambda e: e.memset(self.slvP[:], 0.0),
               waits=[(pe, pe.count)])
        P.emit("vector", lambda e: e.memset(self.slvPT[:], 0.0))
        P.emit("vector", lambda e: e.memset(self.slvP2[:], 0.0))
        P.emit("vector", lambda e: e.memset(self.slvPT2[:], 0.0), incs=[(ve, 1)])
        for kv in range(8):
            z = kv % 2
            base = z * 64
            pc = kv // 2
            colc = [pc * 256 + tt * 128 + c * 64 for c in range(2)]
            klz_hi, klz_lo = self.klfz[z]
            for c in range(2):
                kb = self.kbf_hi[:, colc[c]:colc[c] + 64]
                kl = klz_hi[:, colc[c]:colc[c] + 64]
                P.emit("tensor", lambda e, kb=kb, kl=kl, c=c: e.matmul(
                    bL[c * 64:(c + 1) * 64, 0:64], kb, kl, start=True, stop=True),
                    waits=[(ve, ve.count)])
                P.emit("tensor", lambda e, kb=kb, kl=kl, c=c: e.matmul(
                    bL[c * 64:(c + 1) * 64, 64:128], kl, kb, start=True, stop=True),
                    incs=[(pe, 1)])
            ltok = pe.count
            P.emit("vector", lambda e: e.scalar_tensor_tensor(
                out=self.slvP[0:64, 0:64], in0=bL[0:64, 0:64], scalar=-1.0,
                in1=self.cst[0:64, C_STRICTL2], op0=ALU.mult, op1=ALU.mult),
                waits=[(pe, ltok)])
            P.emit("vector", lambda e: e.scalar_tensor_tensor(
                out=self.slvP[64:128, 64:128], in0=bL[64:128, 0:64], scalar=-1.0,
                in1=self.cst[64:128, C_STRICTL2], op0=ALU.mult, op1=ALU.mult))
            P.emit("vector", lambda e: e.scalar_tensor_tensor(
                out=self.slvPT[0:64, 0:64], in0=bL[0:64, 64:128], scalar=-1.0,
                in1=self.cst[0:64, C_STRICTT2], op0=ALU.mult, op1=ALU.mult))
            vtok = P.emit("vector", lambda e: e.scalar_tensor_tensor(
                out=self.slvPT[64:128, 64:128], in0=bL[64:128, 64:128], scalar=-1.0,
                in1=self.cst[64:128, C_STRICTT2], op0=ALU.mult, op1=ALU.mult),
                incs=[(ve, 1)])
            ucol = 64 if kv % 2 == 0 else 0
            wcol = 64 - ucol
            y32 = self.uw_f[:, kv * 128: kv * 128 + 128]
            vb_h = self.vbtm_hi[:, tt * 512 + kv * 64: tt * 512 + (kv + 1) * 64]
            vb_l = self.vbtm_lo[:, tt * 512 + kv * 64: tt * 512 + (kv + 1) * 64]
            kb_h = self.kbtm_hi[:, tt * 512 + kv * 64: tt * 512 + (kv + 1) * 64]
            kb_l = self.kbtm_lo[:, tt * 512 + kv * 64: tt * 512 + (kv + 1) * 64]
            P.emit("vector", lambda e, y=y32, u=ucol, h=vb_h, l=vb_l: e.tensor_add(
                out=y[:, u:u + 64], in0=h, in1=l), waits=[(ve, ve.count)])
            P.emit("vector", lambda e, y=y32, w=wcol, h=kb_h, l=kb_l: e.tensor_add(
                out=y[:, w:w + 64], in0=h, in1=l), incs=[(ve, 1)])
            cur = (self.slvP, self.slvPT)
            nxt = (self.slvP2, self.slvPT2)
            for k in range(6):
                ytok = P.emit("vector", lambda e, y=y32: e.tensor_copy(
                    out=self.ybf, in_=y), waits=[(pe, pe.count), (ve, ve.count)],
                    incs=[(ve, 1)])
                atok = P.emit("tensor", lambda e, cur=cur: e.matmul(
                    bY[:, 0:128], cur[1], self.ybf, start=True, stop=True),
                    waits=[(ve, ytok)], incs=[(pe, 1)])
                P.emit("vector", lambda e, y=y32: e.tensor_add(
                    out=y, in0=y, in1=bY[:, 0:128]), waits=[(pe, atok)],
                    incs=[(ve, 1)])
                if k < 5:
                    P.emit("tensor", lambda e, cur=cur: e.matmul(
                        bSq[:, 0:128], cur[1], cur[0], start=True, stop=True))
                    sqt = P.emit("tensor", lambda e, cur=cur: e.matmul(
                        bSq[:, 128:256], cur[0], cur[1], start=True, stop=True),
                        incs=[(pe, 1)])
                    P.emit("vector", lambda e, nxt=nxt: e.tensor_copy(
                        out=nxt[0], in_=bSq[:, 0:128]), waits=[(pe, sqt)])
                    P.emit("vector", lambda e, nxt=nxt: e.tensor_copy(
                        out=nxt[1], in_=bSq[:, 128:256]), incs=[(ve, 1)])
                    cur, nxt = nxt, cur
            uwtok = P.emit("vector", lambda e: None, incs=[(ve, 1)])
            srcT = self.uw_f[:, kv * 128: kv * 128 + 128]
            wtok = P.emit("tensor", lambda e, s=srcT: e.transpose(
                bWT[:, 0:128], s, self.cst[:, C_IDENT]),
                waits=[(ve, uwtok)], incs=[(pe, 1)])
            rows = slice(0, 64) if kv % 2 == 0 else slice(64, 128)
            wsl = slice((kv // 2) * 128, (kv // 2) * 128 + 128)
            P.emit("vector", lambda e, rows=rows, wsl=wsl: e.tensor_scalar_mul(
                out=self.wTs_hi[rows, wsl], in0=bWT[rows, 0:128], scalar1=-1.0),
                waits=[(pe, wtok)])
            P.emit("vector", lambda e, rows=rows, wsl=wsl: e.scalar_tensor_tensor(
                out=self.wTs_lo[rows, wsl], in0=bWT[rows, 0:128],
                scalar=-1.0, in1=self.wTs_hi[rows, wsl],
                op0=ALU.mult, op1=ALU.subtract), incs=[(ve, 1)])

    def delta_scan(self, hs, tt, c):
        P = self.P
        ve, pe, sc = P.sem("ve"), P.sem("pe"), P.sem("sc")
        stt = P.sem("st")
        bUP, bS = self.pb[4], self.pb[5]
        cp = slice(c * 64, (c + 1) * 64)
        op = slice(64 - c * 64, 128 - c * 64)
        gc = hs * 4 + tt * 2 + c
        def s32ap(rows):
            return bass.AP(self.fp, rows * 13568 + 11776 + 1536,
                           [[13568, 64], [64, 4], [1, 64]])

        def sdap(t, rows, coff):
            return bass.AP(self.bp1, rows * 49152 + t + coff,
                           [[49152, 64], [128, 4], [1, 64]])
        P.emit("vector", lambda e: e.memset(self.Sd_hi[:], 0.0),
               waits=[(pe, pe.count)])
        P.emit("vector", lambda e: e.memset(self.Sd_lo[:], 0.0))
        for rows, coff in ((0, 0), (64, 64)):
            P.emit("vector", lambda e, rows=rows, coff=coff: e.tensor_copy(
                out=sdap(38912, rows, coff), in_=s32ap(rows)))
            P.emit("vector", lambda e, rows=rows, coff=coff: e.tensor_tensor(
                out=sdap(39424, rows, coff), in0=s32ap(rows),
                in1=sdap(38912, rows, coff), op=ALU.subtract), incs=[(ve, 1)])
        P.emit("gpsimd", lambda e, gc=gc: e.dma_start(
            out=self.ssnap_scr.ap()[:, gc * 1024: gc * 1024 + 512], in_=self.Sd_hi),
            waits=[(ve, ve.count)], incs=[(stt, 16)])
        P.emit("gpsimd", lambda e, gc=gc: e.dma_start(
            out=self.ssnap_scr.ap()[:, gc * 1024 + 512: (gc + 1) * 1024],
            in_=self.Sd_lo), incs=[(stt, 16)])
        P.emit("vector", lambda e: e.memset(self.ups_hi[op, :], 0.0),
               waits=[(pe, pe.count)])
        P.emit("vector", lambda e: e.memset(self.ups_lo[op, :], 0.0), incs=[(ve, 1)])
        for m in range(4):
            wh = self.wTs_hi[:, m * 128 + c * 64: m * 128 + (c + 1) * 64]
            wl = self.wTs_lo[:, m * 128 + c * 64: m * 128 + (c + 1) * 64]
            sh = self.Sd_hi[:, m * 128:(m + 1) * 128]
            sl = self.Sd_lo[:, m * 128:(m + 1) * 128]
            mtok = self.mm3(bUP[cp, m * 128:(m + 1) * 128], wh, wl, sh, sl,
                            start=True, stop=True,
                            waits=[(ve, ve.count)], incs=[(pe, 1)])
            for z in range(2):
                kv = 2 * m + z
                ucol = 64 if z == 0 else 0
                up32 = self.scr1[cp, kv * 64:(kv + 1) * 64]
                u_sl = self.uw_f[cp, kv * 128 + ucol: kv * 128 + ucol + 64]
                P.emit("vector", lambda e, up32=up32, u=u_sl, m=m, z=z: e.tensor_add(
                    out=up32, in0=u,
                    in1=bUP[cp, m * 128 + z * 64: m * 128 + (z + 1) * 64]),
                    waits=[(pe, mtok)])
                self.split("vector", up32,
                           self.ups_hi[cp, kv * 64:(kv + 1) * 64],
                           self.ups_lo[cp, kv * 64:(kv + 1) * 64], incs=[(ve, 1)])
        P.emit("gpsimd", lambda e, gc=gc: e.dma_start(
            out=self.ups_scr.ap()[:, gc * 1024: gc * 1024 + 512], in_=self.ups_hi),
            waits=[(ve, ve.count)], incs=[(stt, 16)])
        P.emit("gpsimd", lambda e, gc=gc: e.dma_start(
            out=self.ups_scr.ap()[:, gc * 1024 + 512: (gc + 1) * 1024],
            in_=self.ups_lo), incs=[(stt, 16)])
        for kv in range(8):
            base = (kv % 2) * 64
            pc = kv // 2
            kcol = slice(tt * 512 + pc * 128 + (kv % 2) * 64,
                         tt * 512 + pc * 128 + (kv % 2) * 64 + 64)
            kh = self.kltm_hi[:, kcol]
            kl = self.kltm_lo[:, kcol]
            uh = self.ups_hi[:, kv * 64:(kv + 1) * 64]
            ul = self.ups_lo[:, kv * 64:(kv + 1) * 64]
            self.mm3(bS[slice(base, base + 64), (kv // 2) * 64:(kv // 2 + 1) * 64],
                     kh, kl, uh, ul, start=True, stop=True,
                     waits=[(ve, ve.count), (stt, stt.count)],
                     incs=[(pe, 1)] if kv == 7 else ())
        stok = P.sem("pe").count
        P.emit("vector", lambda e: e.tensor_add(
            out=self.S32[:], in0=self.S32[:], in1=bS[:, 0:256]),
            waits=[(pe, stok)], incs=[(ve, 1)])

    def st_q(self, hs):
        P = self.P
        ld, ve, pe, sc = P.sem("ld"), P.sem("ve"), P.sem("pe"), P.sem("sc")
        for qh in range(2):
            last_mm = [pe.count, pe.count]
            last_split = [0, 0]
            qtok = {}
            for dc in range(NC):
                bi = dc % 2
                sbuf = self.wstrip[bi]
                sph = self.wsp_hi[:, bi * 1024:(bi + 1) * 1024]
                spl = self.wsp_lo_x[:, bi * 1024:(bi + 1) * 1024]
                tok_ld = P.emit("sync", lambda e, b=sbuf, dc=dc, qh=qh: e.dma_start(
                    out=b, in_=bass.AP(self.wq_in, dc * 128 * DM + qh * 1024,
                                       [[DM, 128], [1, 1024]])),
                    waits=[(ve, last_split[bi]), (pe, last_mm[bi])],
                    incs=[(ld, 16)])
                vtok = self.split("vector", sbuf, sph, spl,
                                  waits=[(ld, tok_ld), (pe, last_mm[bi])],
                                  incs=[(ve, 1)])
                last_split[bi] = vtok
                for qi in range(8):
                    qc = qh * 8 + qi
                    lh = sph[:, qi * 128:(qi + 1) * 128]
                    ll = spl[:, qi * 128:(qi + 1) * 128]
                    rh = self.xs_hi[:, dc * HS:(dc + 1) * HS]
                    rl = self.xs_lo[:, dc * HS:(dc + 1) * HS]
                    tok = self.mm3(self.pb[qi][:, 0:256], lh, ll, rh, rl,
                                   start=(dc == 0), stop=(dc == NC - 1),
                                   waits=[(ve, vtok)], incs=[(pe, 1)])
                    if dc == NC - 1:
                        qtok[qc] = tok
                last_mm[bi] = tok
            for qi in range(8):
                qc = qh * 8 + qi
                bank = self.pb[qi][:, 0:256]
                w = [(pe, qtok[qc])]
                co = qc * HS
                stok = P.emit("scalar", lambda e, b=bank, qi=qi: e.activation(
                    out=self.eqt[:, qi * 256:(qi + 1) * 256], in_=b, func=AF.Exp),
                    waits=w, incs=[(sc, 1)])
                self.split("vector", bank, self.q_hi[:, co:co + 256],
                           self.q_lo[:, co:co + 256], waits=w + [(sc, stok)],
                           incs=[(ve, 1)])
            for qi in (0, 2, 4, 6):
                src = self.eqt[:, qi * 256:(qi + 2) * 256]
                bb, btok = self.bcast_recip(src, waits=[(sc, sc.count),
                                                        (ve, ve.count)], w=512)
                P.emit("vector", lambda e, s=src, bb=bb: e.tensor_tensor(
                    out=s, in0=s, in1=bb, op=ALU.mult), waits=[(pe, btok)])
                for q2 in range(2):
                    qc = qh * 8 + qi + q2
                    sap = bass.AP(self.fp, 4096 + (qi + q2) * 256,
                                  [[13568, 128], [64, 4], [1, 64]])
                    hap = bass.AP(self.bp2, self.eqb + qc * 64,
                                  [[21504, 128], [1024, 4], [1, 64]])
                    lap = bass.AP(self.bp2, self.eqb + 4096 + qc * 64,
                                  [[21504, 128], [1024, 4], [1, 64]])
                    P.emit("vector", lambda e, s=sap, h=hap: e.tensor_copy(
                        out=h, in_=s))
                    P.emit("vector", lambda e, s=sap, h=hap, l=lap: e.tensor_tensor(
                        out=l, in0=s, in1=h, op=ALU.subtract), incs=[(ve, 1)])

    MASK_BASE = (0, 6, 14, 28)
    NJT = ((6, 8), (14, 16))

    def st_std(self, oi, slot, h2):
        P = self.P
        ld, ve, pe, sc = P.sem("ld"), P.sem("ve"), P.sem("pe"), P.sem("sc")
        hzv = P.sem("hzv")
        njt = self.NJT[slot][h2]
        mbase = self.MASK_BASE[oi]
        bSc = [self.pb[0], self.pb[1]]
        bO = self.pb[2]
        bRs = [self.pb[3], self.pb[4]]
        bBc = self.pb[5]
        mld = 0
        for jt in range(njt):
            mld = P.emit("sync", lambda e, jt=jt: e.dma_start(
                out=self.bp2[:, self.mkb + jt * 256:self.mkb + (jt + 1) * 256],
                in_=self.masks_in.ap()[:, (mbase + jt) * 256:(mbase + jt + 1) * 256]),
                waits=[(pe, pe.count), (ve, ve.count)] if jt == 0 else (),
                incs=[(ld, 16)])
        PtA = (self.bp1[:, 32768:33280], self.bp1[:, 33280:33792])
        PtB = (self.bp1[:, 33792:34304], self.bp1[:, 34304:34816])
        Pt = (PtA, PtB)
        pexp_h = (self.pexp, self.scr1[:, 0:512])
        t1 = self.uw_f[:, 0:512]
        for qp in range(8):
            kvA = 2 * (qp // 2)
            otok = [None, None]
            rstok = [None, None]
            sp_prev = 0
            pv_prev = 0
            qsl = slice(2 * qp * HS, (2 * qp + 2) * HS)
            for jt in range(njt):
                stok = [0, 0]
                for hi_ in range(2):
                    kv = kvA + hi_
                    base = (kv % 2) * 64
                    rows = slice(base, base + 64)
                    pc = kv // 2
                    kc = slice(pc * 2048 + jt * 128, pc * 2048 + (jt + 1) * 128)
                    tok = self.mm3(bSc[hi_][:, 0:512], self.k_hi[rows, kc],
                                   self.k_lo[rows, kc],
                                   self.q_hi[rows, qsl], self.q_lo[rows, qsl],
                                   start=True, stop=True,
                                   tile_position=(base, 0),
                                   waits=[(ve, ve.count), (sc, sc.count)],
                                   incs=[(pe, 1)])
                    stok[hi_] = P.emit("scalar", lambda e, hi_=hi_: e.activation(
                        out=pexp_h[hi_], in_=bSc[hi_][:, 0:512],
                        func=AF.Exp, scale=0.125),
                        waits=[(pe, tok), (ve, sp_prev)], incs=[(sc, 1)])
                mask_ap = bass.AP(self.bp2, self.mkb + jt * 256,
                                  [[21504, 128], [0, 2], [1, 256]])
                for hi_ in range(2):
                    P.emit("vector", lambda e, hi_=hi_, m=mask_ap: e.tensor_tensor(
                        out=pexp_h[hi_], in0=pexp_h[hi_], in1=m, op=ALU.mult),
                        waits=[(sc, stok[hi_]), (pe, pv_prev), (ld, mld)],
                        incs=[(ve, 1)])
                    sp_prev = self.split("vector", pexp_h[hi_],
                                         Pt[hi_][0], Pt[hi_][1], incs=[(ve, 1)])
                for hi_ in range(2):
                    P.emit("tensor", lambda e, hi_=hi_, jt=jt: e.matmul(
                        bRs[hi_][0:1, 0:512], self.onesb[:, 0:1],
                        Pt[hi_][0], start=(jt == 0), stop=False),
                        waits=[(ve, sp_prev)])
                    tok = P.emit("tensor", lambda e, hi_=hi_, jt=jt: e.matmul(
                        bRs[hi_][0:1, 0:512], self.onesb[:, 0:1],
                        Pt[hi_][1], start=False, stop=(jt == njt - 1)),
                        incs=[(pe, 1)] if jt == njt - 1 else ())
                    if jt == njt - 1:
                        rstok[hi_] = tok
                    kv = kvA + hi_
                    base = (kv % 2) * 64
                    vc = slice(jt * 512 + kv * 64, jt * 512 + (kv + 1) * 64)
                    pv_prev = self.mm3(bO[slice(base, base + 64), 0:512],
                                       self.v_hi[:, vc], self.v_lo[:, vc],
                                       Pt[hi_][0], Pt[hi_][1],
                                       start=(jt == 0), stop=(jt == njt - 1),
                                       incs=[(pe, 1)])
                    if jt == njt - 1:
                        otok[hi_] = pv_prev
            v1 = P.emit("vector", lambda e: e.reciprocal(
                out=self.misc[0:1, 0:512], in_=bRs[0][0:1, 0:512]),
                waits=[(pe, rstok[0])], incs=[(hzv, 1)])
            v2 = P.emit("vector", lambda e: e.reciprocal(
                out=self.misc[0:1, 512:1024], in_=bRs[1][0:1, 0:512]),
                waits=[(pe, rstok[1]), (hzv, v1)], incs=[(hzv, 1)])
            P.emit("tensor", lambda e: e.matmul(
                bBc[0:64, 0:512], self.cst[:, C_E0], self.misc[:, 0:512],
                start=True, stop=True), waits=[(hzv, v2)])
            btok = P.emit("tensor", lambda e: e.matmul(
                bBc[64:128, 0:512], self.cst[:, C_E0], self.misc[:, 512:1024],
                start=True, stop=True), incs=[(pe, 1)])
            P.emit("vector", lambda e, t1=t1: e.tensor_copy(
                out=t1, in_=bBc[:, 0:512]), waits=[(pe, btok)])
            P.emit("vector", lambda e, t1=t1: e.tensor_tensor(
                out=t1, in0=bO[:, 0:512], in1=t1, op=ALU.mult),
                waits=[(pe, otok[0]), (pe, otok[1])])
            osl = slice(2 * qp * HS, (2 * qp + 2) * HS)
            P.emit("vector", lambda e, osl=osl, t1=t1: e.scalar_tensor_tensor(
                out=self.o_comb[:, osl], in0=t1, scalar=0.5,
                in1=self.o_comb[:, osl], op0=ALU.mult, op1=ALU.add),
                incs=[(ve, 1)])

    def st_wo(self, oi):
        P = self.P
        ld, ve, pe, sc = P.sem("ld"), P.sem("ve"), P.sem("pe"), P.sem("sc")
        stt = P.sem("st")
        for i in range(4):
            self.split("vector", self.o_comb[:, i * 1024:(i + 1) * 1024],
                       self.oc_hi[:, i * 1024:(i + 1) * 1024],
                       self.oc_lo[:, i * 1024:(i + 1) * 1024],
                       waits=[(ve, ve.count)], incs=[(ve, 1)])
        for wv in range(2):
            wtok = {}
            last_mm = [pe.count, pe.count]
            last_split = [0, 0]
            for ic in range(NC):
                bi = ic % 2
                sbuf = self.wstrip[bi]
                sph = self.wsp_hi[:, bi * 1024:(bi + 1) * 1024]
                spl = self.wsp_lo_x[:, bi * 1024:(bi + 1) * 1024]
                tok_ld = P.emit("sync", lambda e, b=sbuf, ic=ic, wv=wv: e.dma_start(
                    out=b, in_=bass.AP(self.wo_in, ic * 128 * DM + wv * 1024,
                                       [[DM, 128], [1, 1024]])),
                    waits=[(ve, last_split[bi]), (pe, last_mm[bi])],
                    incs=[(ld, 16)])
                vtok = self.split("vector", sbuf, sph, spl,
                                  waits=[(ld, tok_ld), (pe, last_mm[bi])],
                                  incs=[(ve, 1)])
                last_split[bi] = vtok
                for oc in range(8):
                    dchunk = wv * 8 + oc
                    lh = sph[:, oc * 128:(oc + 1) * 128]
                    ll = spl[:, oc * 128:(oc + 1) * 128]
                    rh = self.oc_hi[:, ic * 256:(ic + 1) * 256]
                    rl = self.oc_lo[:, ic * 256:(ic + 1) * 256]
                    tok = self.mm3(self.pb[oc][:, 0:256], lh, ll, rh, rl,
                                   start=(ic == 0), stop=(ic == NC - 1),
                                   waits=[(ve, vtok)], incs=[(pe, 1)])
                    if ic == NC - 1:
                        wtok[dchunk] = tok
                last_mm[bi] = tok
            for oc in range(8):
                dchunk = wv * 8 + oc
                bank = self.pb[oc][:, 0:256]
                sl = oc % 2
                vtok = P.emit("vector", lambda e, b=bank, sl=sl: e.tensor_copy(
                    out=self.att_st[:, sl * 256:(sl + 1) * 256], in_=b),
                    waits=[(pe, wtok[dchunk]), (stt, max(0, stt.count - 16))],
                    incs=[(ve, 1)])
                off = oi * NC * HS + dchunk * HS
                P.emit("gpsimd", lambda e, sl=sl, off=off: e.dma_start(
                    out=self.attn_scr.ap()[:, off:off + HS],
                    in_=self.att_st[:, sl * 256:(sl + 1) * 256]),
                    waits=[(ve, vtok)], incs=[(stt, 16)])

    def st_slot(self, slot):
        P = self.P
        ve, pe = P.sem("ve"), P.sem("pe")
        for h2 in range(2):
            oi = slot * 2 + h2
            for t in range(2):
                self.ln_tile(self.xg_in, oi * HS + t * 128, t)
            P.emit("vector", lambda e: e.memset(self.o_comb[:], 0.0),
                   waits=[(pe, pe.count)], incs=[(ve, 1)])
            self.st_q(oi)
            self.slot_delta_o(oi)
            self.st_std(oi, slot, h2)
            self.st_wo(oi)

    def slot_delta_o(self, oi):
        P = self.P
        ld, ve, pe, sc = P.sem("ld"), P.sem("ve"), P.sem("pe"), P.sem("sc")
        stt = P.sem("st")
        bA, bO = self.pb[6], self.pb[7]

        def dyn_dma(dst, scr, item, width):
            def fn(e, dst=dst, scr=scr, item=item, width=width):
                v = e.value_load(self.gofs_sb[0:1, item:item + 1])
                inst = e.dma_start(out=dst, in_=bass.AP(scr, v,
                                                        [[scr.shape[1], 128],
                                                         [1, width]]))
                self.nc.free_register(v.val)
                return inst
            return fn
        ktok = P.emit("sync", dyn_dma(self.bp1[:, 32768:36864], self.klf_scr,
                                      oi * 16 + 8, 4096),
                      waits=[(stt, stt.count), (ve, ve.count), (pe, pe.count)],
                      incs=[(ld, 16)])
        for tt in range(2):
            for c in range(2):
                cp = slice(c * 64, (c + 1) * 64)
                ch = tt * 2 + c
                it = oi * 16 + tt * 2 + c
                stok_ = P.emit("sync", dyn_dma(self.bp1[:, 38912:39936],
                                               self.ssnap_scr, it, 1024),
                               waits=[(ve, ve.count)], incs=[(ld, 16)])
                utok = P.emit("sync", dyn_dma(self.bp1[:, 37888:38912],
                                              self.ups_scr, it + 4, 1024),
                              incs=[(ld, 16)])
                for m in range(4):
                    eqo = ch * 1024 + 4 * m * 64
                    eh = self.bp2[:, self.eqb + eqo:self.eqb + eqo + 256]
                    el = self.bp2[:, self.eqb + 4096 + eqo:
                                  self.eqb + 4096 + eqo + 256]
                    for z in range(2):
                        kv = 2 * m + z
                        base = z * 64
                        kcol = slice(m * 256 + tt * 128 + c * 64,
                                     m * 256 + tt * 128 + (c + 1) * 64)
                        klh, kll = self.klfz[z][0][:, kcol], self.klfz[z][1][:, kcol]
                        atok = self.mm3(bA[cp, 0:256], klh, kll, eh, el,
                                        start=True, stop=True,
                                        waits=[(ve, ve.count),
                                               (ld, max(ktok, utok))],
                                        incs=[(pe, 1)])
                        cmask = bass.AP(self.cst, cp.start * CONST_W + 384,
                                        [[CONST_W, 64], [0, 4], [1, 64]])
                        P.emit("vector", lambda e, cp=cp, cm=cmask: e.tensor_tensor(
                            out=self.scr1[cp, 0:256], in0=bA[cp, 0:256],
                            in1=cm, op=ALU.mult),
                            waits=[(pe, atok)])
                        attok = self.split("vector", self.scr1[cp, 0:256],
                                           self.bp1[cp, 39936:40192],
                                           self.bp1[cp, 40192:40448],
                                           incs=[(ve, 1)])
                        sh = self.bp1[:, 38912 + m * 128 + z * 64:
                                      38912 + m * 128 + z * 64 + 64]
                        sl = self.bp1[:, 39424 + m * 128 + z * 64:
                                      39424 + m * 128 + z * 64 + 64]
                        self.mm3(bO[slice(base, base + 64), 0:256], sh, sl, eh, el,
                                 start=True, stop=False)
                        uh = self.ups_hi[:, kv * 64:(kv + 1) * 64]
                        ul = self.ups_lo[:, kv * 64:(kv + 1) * 64]
                        otok = self.mm3(bO[slice(base, base + 64), 0:256], uh, ul,
                                        self.bp1[:, 39936:40192],
                                        self.bp1[:, 40192:40448],
                                        start=False, stop=True,
                                        waits=[(ve, attok)], incs=[(pe, 1)])
                    oap = bass.AP(self.fp, 4 * m * HS + tt * 128 + c * 64,
                                  [[13568, 128], [HS, 4], [1, 64]])
                    bOap = bass.AP(self.pb[7], 0, [[2048 // 4, 128], [64, 4], [1, 64]])
                    P.emit("vector", lambda e, oap=oap, bOap=bOap:
                           e.scalar_tensor_tensor(
                        out=oap, in0=bOap, scalar=0.5,
                        in1=oap, op0=ALU.mult, op1=ALU.add),
                        waits=[(pe, pe.count)], incs=[(ve, 1)])

    def st_precast(self, hs):
        P = self.P
        gld = P.sem("gld")
        jobs = []
        for ic in range(hs * 6, (hs + 1) * 6):
            jobs.append((bass.AP(self.gw_in, ic * 128 * 2048,
                                 [[2048, 128], [1, 2048]]),
                         self.gw8_scr, ic * 128 * 2048, 2048))
            jobs.append((bass.AP(self.uw_in, ic * 128 * 2048,
                                 [[2048, 128], [1, 2048]]),
                         self.uw8_scr, ic * 128 * 2048, 2048))
        for k in range(hs * 12, (hs + 1) * 12):
            jobs.append((bass.AP(self.dw_in, k * 128 * 1024,
                                 [[1024, 128], [1, 1024]]),
                         self.dw8_scr, k * 128 * 1024, 1024))
        for j, (src_ap, scr, off, w) in enumerate(jobs):
            bi = j % 2
            base = 4096 * bi
            ctok = P.emit("gpsimd", lambda e, s=src_ap, base=base, w=w:
                          e.dma_start(out=self.w8[:, base:base + w].bitcast(F8),
                                      in_=s),
                          waits=[(gld, self._pc_prev[bi])], incs=[(gld, 16)])
            self._pc_prev[bi] = P.emit(
                "gpsimd", lambda e, scr=scr, off=off, base=base, w=w:
                e.dma_start(out=bass.AP(scr, off, [[w, 128], [1, w]]),
                            in_=self.w8[:, base:base + w]),
                waits=[(gld, ctok)], incs=[(gld, 16)])

    def st_ffn(self, slot, parts="all"):
        P = self.P
        ld, ve, pe, sc = P.sem("ld"), P.sem("ve"), P.sem("pe"), P.sem("sc")
        gld, stt = P.sem("gld"), P.sem("st")
        f = self.fp
        outstage = [f[:, t * 2048:(t + 1) * 2048] for t in range(4)]
        gate32 = f[:, 4096:4608]
        fuse32 = self.misc[:, 0:512]
        att_re = self.misc[:, 512:1024]
        xnr_hi = self.bp2[:, 0:8192]
        xnr_lo = self.bp2[:, 8192:16384]
        oi0 = slot * 2
        for t in range(4):
            self.ln_tile(self.xg_in, slot * SEG + t * 128, t,
                         dst_hi=xnr_hi, dst_lo=xnr_lo, cstride=SEG)
        if parts == "ln":
            return
        mm_hist = [pe.count, pe.count]
        tok = 0
        for ic in range(NIC):
            bi = ic % 2
            base = 4096 * bi
            P.emit("sync", lambda e, ic=ic, base=base: e.dma_start(
                out=self.w8[:, base:base + 2048],
                in_=bass.AP(self.gw8_scr, ic * 128 * 2048,
                            [[2048, 128], [1, 2048]])),
                waits=[(pe, mm_hist[bi]), (gld, gld.count)] if ic < 2
                else [(pe, mm_hist[bi])],
                incs=[(ld, 16)])
            s2 = P.emit("sync", lambda e, ic=ic, base=base: e.dma_start(
                out=self.w8[:, base + 2048:base + 4096],
                in_=bass.AP(self.uw8_scr, ic * 128 * 2048,
                            [[2048, 128], [1, 2048]])),
                incs=[(ld, 16)])
            for dc in range(NC):
                lg = self.w8[:, base + dc * 128:base + (dc + 1) * 128].bitcast(F8)
                lu = self.w8[:, base + 2048 + dc * 128:
                             base + 2048 + (dc + 1) * 128].bitcast(F8)
                self.mm3(self.pb[0][:], lg, None,
                         xnr_hi[:, dc * 512:(dc + 1) * 512],
                         xnr_lo[:, dc * 512:(dc + 1) * 512],
                         start=(dc == 0), stop=(dc == NC - 1),
                         waits=[(ld, s2), (sc, sc.count), (ve, ve.count)]
                         if dc == 0 else (),
                         incs=[(pe, 1)] if dc == NC - 1 else (), passes=2)
                tok = self.mm3(self.pb[1][:], lu,
                               None, xnr_hi[:, dc * 512:(dc + 1) * 512],
                               xnr_lo[:, dc * 512:(dc + 1) * 512],
                               start=(dc == 0), stop=(dc == NC - 1),
                               incs=[(pe, 1)] if dc == NC - 1 else (), passes=2)
            mm_hist[bi] = tok
            gtok = P.emit("scalar", lambda e, ic=ic: e.activation(
                out=gate32, in_=self.pb[0][:], func=AF.Silu,
                scale=self.scl[:, ic:ic + 1]), waits=[(pe, tok)], incs=[(sc, 1)])
            P.emit("vector", lambda e, ic=ic: e.scalar_tensor_tensor(
                out=fuse32, in0=self.pb[1][:], scalar=self.scl[:, 48 + ic:49 + ic],
                in1=gate32, op0=ALU.mult, op1=ALU.mult),
                waits=[(sc, gtok), (ve, ve.count)])
            self.split("vector", fuse32,
                       self.fuse_hi[:, ic * 512:(ic + 1) * 512],
                       self.fuse_lo[:, ic * 512:(ic + 1) * 512], incs=[(ve, 1)])
        if parts == "gateup":
            return
        ostok = {}
        for t in range(4):
            row0 = slot * SEG + t * 128
            ostok[t] = P.emit("sync", lambda e, t=t, o=row0: e.dma_start(
                out=outstage[t], in_=self.xg_in.ap()[o:o + 128, :]),
                waits=[(pe, pe.count), (ve, ve.count), (sc, sc.count),
                       (stt, stt.count)],
                incs=[(ld, 16)])
        for dh in range(2):
            dtok = {}
            dprev = [P.sem("gld").count, P.sem("pe").count]
            dhist = [dprev[1], dprev[1]]
            for ic in range(NIC):
                bi = ic % 2
                base = 4096 * bi
                s1 = P.emit("sync", lambda e, ic=ic, dh=dh, base=base:
                            e.dma_start(
                    out=self.w8[:, base:base + 1024],
                    in_=bass.AP(self.dw8_scr, (dh * NIC + ic) * 128 * 1024,
                                [[1024, 128], [1, 1024]])),
                    waits=[(pe, dhist[bi])], incs=[(ld, 16)])
                for od in range(8):
                    ldw = self.w8[:, base + od * 128:
                                  base + (od + 1) * 128].bitcast(F8)
                    tok = self.mm3(self.pb[od][:],
                                   ldw, None,
                                   self.fuse_hi[:, ic * 512:(ic + 1) * 512],
                                   self.fuse_lo[:, ic * 512:(ic + 1) * 512],
                                   start=(ic == 0), stop=(ic == NIC - 1),
                                   waits=[(ld, s1), (ve, ve.count)]
                                   if ic == 0 else ([(ld, s1)] if od == 0 else ()),
                                   incs=[(pe, 1)] if (ic == NIC - 1 or od == 7)
                                   else (), passes=2)
                    if ic == NIC - 1:
                        dtok[od] = tok
                dhist[bi] = tok
            for od in range(8):
                dchunk = dh * 8 + od
                atok = P.emit("sync", lambda e, dchunk=dchunk: e.dma_start(
                    out=att_re, in_=bass.AP(self.attn_scr,
                                            oi0 * 4096 + dchunk * 256,
                                            [[16384, 128], [4096, 2], [1, 256]])),
                    waits=[(ve, ve.count)], incs=[(ld, 16)])
                ttok = P.emit("vector", lambda e, od=od, dchunk=dchunk:
                              e.scalar_tensor_tensor(
                    out=self.stage[:, 0:512], in0=self.pb[od][:],
                    scalar=self.scl[:, 96 + dchunk:97 + dchunk], in1=att_re,
                    op0=ALU.mult, op1=ALU.add),
                    waits=[(pe, dtok[od]), (ld, atok)], incs=[(ve, 1)])
                for t in range(4):
                    ptok = P.emit("tensor", lambda e, od=od, t=t: e.transpose(
                        self.pb[od][:, 0:128],
                        self.stage[:, t * 128:(t + 1) * 128],
                        self.cst[:, C_IDENT]),
                        waits=[(ve, ve.count), (pe, pe.count)], incs=[(pe, 1)])
                    P.emit("vector", lambda e, od=od, t=t, dchunk=dchunk:
                           e.tensor_add(
                        out=outstage[t][:, dchunk * 128:(dchunk + 1) * 128],
                        in0=self.pb[od][:, 0:128],
                        in1=outstage[t][:, dchunk * 128:(dchunk + 1) * 128]),
                        waits=[(pe, ptok), (ld, ostok[t])], incs=[(ve, 1)])
        for t in range(4):
            row0 = slot * SEG + t * 128
            P.emit("gpsimd", lambda e, t=t, row0=row0: e.dma_start(
                out=self.out.ap()[row0:row0 + 128, :], in_=outstage[t]),
                waits=[(ve, ve.count)], incs=[(stt, 16)])

    def finish(self):
        P = self.P
        stt = P.sem("st")
        for name, t, ap in self.taps:
            P.emit("gpsimd", lambda e, t=t, ap=ap: e.dma_start(out=t.ap(), in_=ap),
                   waits=P.barrier_counts(), incs=[(stt, 16)])
        P.emit("sync", lambda e: None, waits=[(stt, stt.count)])

    def build(self, upto=None):
        self.alloc()
        self.st_const()
        for hs in range(NHS):
            self.st_xn(hs)
            self.st_proj(hs)
            self.st_prep(hs)
            self.st_delta(hs)
            self.st_precast(hs)
            if upto is not None and hs >= upto:
                self.finish()
                self.P.build_into(self.nc, self.ctx)
                return self.nc
            if hs == 3:
                self.st_slot(0)
            if hs == 7:
                self.st_slot(1)
        self.st_ffn(0)
        self.st_ffn(1)
        self.finish()
        self.P.build_into(self.nc, self.ctx)
        return self.nc


def make_masks(own):
    m = np.zeros((128, 44 * 256), dtype=ml_dtypes.bfloat16)
    base = CoreBuilder.MASK_BASE
    njts = CoreBuilder.NJT
    j = np.arange(128)[:, None]
    r = np.arange(256)[None, :]
    for slot in range(2):
        for h2 in range(2):
            oi = slot * 2 + h2
            g = own[slot]
            ghs = 2 * g + h2
            for jt in range(njts[slot][h2]):
                tile = ((ghs * 256 + r) >= (jt * 128 + j))
                off = (base[oi] + jt) * 256
                m[:, off:off + 256] = tile.astype(ml_dtypes.bfloat16)
    return m


def make_gofs(own):
    g = np.zeros((1, 64), dtype=np.int32)
    for slot in range(2):
        for h2 in range(2):
            oi = slot * 2 + h2
            ghs = 2 * own[slot] + h2
            for tt in range(2):
                for c in range(2):
                    gc = 4 * ghs + 2 * tt + c
                    g[0, oi * 16 + tt * 2 + c] = gc * 1024
                    g[0, oi * 16 + 4 + tt * 2 + c] = gc * 1024
            g[0, oi * 16 + 8] = ghs * 4096
    return g


_CACHE = {}


def _get_nc():
    if "nc" not in _CACHE:
        _CACHE["nc"] = CoreBuilder(0).build()
    return _CACHE["nc"]


def _shard_inputs(inputs):
    perm = QCOL_PERM
    consts = make_consts()
    base = {
        "wq": np.ascontiguousarray(inputs["L0_attn_Wq"][:, perm]),
        "wk": np.ascontiguousarray(inputs["L0_attn_Wk"]),
        "wv": np.ascontiguousarray(inputs["L0_attn_Wv"]),
        "wo": np.ascontiguousarray(inputs["L0_attn_Wo"][perm, :]),
        "gw": np.ascontiguousarray(
            inputs["L0_ffn_gate_w"].reshape(16, 128, NIC, 128)
            .transpose(2, 1, 0, 3).reshape(NIC * 128, 2048)),
        "uw": np.ascontiguousarray(
            inputs["L0_ffn_up_w"].reshape(16, 128, NIC, 128)
            .transpose(2, 1, 0, 3).reshape(NIC * 128, 2048)),
        "dw": np.ascontiguousarray(
            inputs["L0_ffn_down_w"].reshape(NIC, 128, 2, 1024)
            .transpose(2, 0, 1, 3).reshape(2 * NIC * 128, 1024)),
        "gs": np.ascontiguousarray(inputs["L0_ffn_gate_scale"]),
        "us": np.ascontiguousarray(inputs["L0_ffn_up_scale"]),
        "ds": np.ascontiguousarray(inputs["L0_ffn_down_scale"]),
        "consts": consts,
    }
    masks = {h: make_masks(OWN[h]) for h in (0, 1)}
    gofs = {h: make_gofs(OWN[h]) for h in (0, 1)}
    x = inputs["L0_attn_hidden_states"]
    in_maps = []
    for core in range(8):
        b, h = core // 2, core % 2
        m = dict(base)
        m["x"] = np.ascontiguousarray(x[b])
        rows = np.r_[OWN[h][0] * SEG:(OWN[h][0] + 1) * SEG,
                     OWN[h][1] * SEG:(OWN[h][1] + 1) * SEG]
        m["xg"] = np.ascontiguousarray(x[b][rows])
        m["masks"] = masks[h]
        m["gofs"] = gofs[h]
        in_maps.append(m)
    return in_maps


def kernel(L0_attn_hidden_states, L0_attn_Wq, L0_attn_Wk, L0_attn_Wv, L0_attn_Wo,
           L0_ffn_gate_w, L0_ffn_up_w, L0_ffn_down_w, L0_ffn_gate_scale,
           L0_ffn_up_scale, L0_ffn_down_scale):
    inputs = dict(
        L0_attn_hidden_states=np.asarray(L0_attn_hidden_states),
        L0_attn_Wq=np.asarray(L0_attn_Wq), L0_attn_Wk=np.asarray(L0_attn_Wk),
        L0_attn_Wv=np.asarray(L0_attn_Wv), L0_attn_Wo=np.asarray(L0_attn_Wo),
        L0_ffn_gate_w=np.asarray(L0_ffn_gate_w),
        L0_ffn_up_w=np.asarray(L0_ffn_up_w),
        L0_ffn_down_w=np.asarray(L0_ffn_down_w),
        L0_ffn_gate_scale=np.asarray(L0_ffn_gate_scale),
        L0_ffn_up_scale=np.asarray(L0_ffn_up_scale),
        L0_ffn_down_scale=np.asarray(L0_ffn_down_scale))
    in_maps = _shard_inputs(inputs)
    nc = _get_nc()
    res = run_bass_kernel_spmd(nc, in_maps, core_ids=list(range(8)))
    out = np.empty((B, N, DM), dtype=np.float32)
    for core in range(8):
        b, h = core // 2, core % 2
        o = res.results[core]["out"]
        for slot in range(2):
            g = OWN[h][slot]
            out[b, g * SEG:(g + 1) * SEG] = o[slot * SEG:(slot + 1) * SEG]
    return out
